# version 07
# baseline (speedup 1.0000x reference)
import numpy as np
from contextlib import ExitStack
from dataclasses import dataclass

import concourse.bass as bass
import concourse.mybir as mybir
from concourse.bass_utils import run_bass_kernel_spmd

F32 = mybir.dt.float32
BF16 = mybir.dt.bfloat16
I32 = mybir.dt.int32
AF = mybir.ActivationFunctionType
ALU = mybir.AluOpType
AX = mybir.AxisListType
PI = float(np.pi)
TWO_PI = float(2.0 * np.pi)


@dataclass
class Cfg:
    SEQ: int = 4096
    QROWS: int = 2048
    D: int = 2048
    HQ: int = 16
    HKV: int = 4
    HD: int = 128
    FF: int = 8192
    SUBQ: int = 512
    TOKWIN: int = 512
    FSUB: int = 1024
    GU_GROUP: int = 4
    D_PAIR: int = 8
    theta: float = 10000.0
    eps1: float = 1e-5
    eps2: float = 1e-8
    SHIFT: float = 20.0
    dbg: bool = False

    @property
    def NDM(self): return self.D // 128
    @property
    def GRP(self): return self.HQ // self.HKV
    @property
    def NKC(self): return self.SEQ // 128
    @property
    def NSUB(self): return self.QROWS // self.SUBQ
    @property
    def CW(self): return self.GRP * self.SUBQ
    @property
    def NQT(self): return self.QROWS // 128
    @property
    def NHT(self): return self.FF // 128
    @property
    def NFSUB(self): return self.QROWS // self.FSUB
    @property
    def KVW(self): return self.HKV * self.HD
    @property
    def QTS(self): return self.SUBQ // 128
    @property
    def NGRP(self): return self.NHT // self.GU_GROUP
    @property
    def NPAIR(self): return self.NHT // self.D_PAIR
    @property
    def GW(self): return self.GU_GROUP * 128
    @property
    def QT_F(self): return self.FSUB // 128


def eo_ap(ap128, hd):
    ev = bass.AP(ap128.tensor, ap128.offset, [list(ap128.ap[0]), [2, hd // 2]])
    od = bass.AP(ap128.tensor, ap128.offset + 1, [list(ap128.ap[0]), [2, hd // 2]])
    return ev, od


class B:
    def __init__(self, nc, es):
        self.nc, self.es = nc, es
        self.sems, self.cnt = {}, {}

    def s(self, name):
        if name not in self.sems:
            self.sems[name] = self.es.enter_context(self.nc.semaphore(name))
            self.cnt[name] = 0
        return self.sems[name]

    def inc(self, ins, name, n=1):
        ins.then_inc(self.s(name), n)
        self.cnt[name] += n
        return self.cnt[name]

    def w(self, eng, name, val):
        if val > 0:
            eng.wait_ge(self.s(name), val)

    def f(self, eng, ins, name):
        n = self.inc(ins, name)
        eng.wait_ge(self.s(name), n)

    def wr(self, eng, name, ndma, item, depth, total):
        v = 16 * ndma * min(item + depth, total)
        if v > 0:
            eng.wait_ge(self.s(name), v)


def build_nc(cfg: Cfg):
    c = cfg
    nc = bass.Bass("TRN2", target_bir_lowering=False,
                   detect_race_conditions=False)
    es = ExitStack()
    b = B(nc, es)

    xb = nc.dram_tensor("xb", [c.SEQ, c.D], F32, kind="ExternalInput").ap()
    xq = nc.dram_tensor("xq", [c.QROWS, c.D], F32, kind="ExternalInput").ap()
    wq = nc.dram_tensor("wq", [c.D, c.D], F32, kind="ExternalInput").ap()
    wk = nc.dram_tensor("wk", [c.D, c.KVW], F32, kind="ExternalInput").ap()
    wv = nc.dram_tensor("wv", [c.D, c.KVW], F32, kind="ExternalInput").ap()
    wo = nc.dram_tensor("wo", [c.D, c.D], F32, kind="ExternalInput").ap()
    wg = nc.dram_tensor("wg", [c.FF, c.D], F32, kind="ExternalInput").ap()
    wu = nc.dram_tensor("wu", [c.FF, c.D], F32, kind="ExternalInput").ap()
    wd = nc.dram_tensor("wd", [c.D, c.FF], F32, kind="ExternalInput").ap()
    pos = nc.dram_tensor("pos", [1, 1], F32, kind="ExternalInput").ap()
    out = nc.dram_tensor("out", [c.QROWS, c.D], F32, kind="ExternalOutput").ap()

    xbf_s = nc.dram_tensor("xbf_s", [c.SEQ, c.D], BF16).ap()
    xqbf_s = nc.dram_tensor("xqbf_s", [c.QROWS, c.D], BF16).ap()
    qT_s = nc.dram_tensor("qT_s", [c.D, c.QROWS], BF16).ap()
    h1_s = nc.dram_tensor("h1_s", [c.QROWS, c.D], F32, kind="ExternalOutput" if True else "Internal").ap()
    xqn_s = nc.dram_tensor("xqn_s", [c.QROWS, c.D], BF16, kind="ExternalOutput").ap()
    wg_s = nc.dram_tensor("wg_s", [c.FF, c.D], BF16).ap()
    wu_s = nc.dram_tensor("wu_s", [c.FF, c.D], BF16).ap()
    wd_s = nc.dram_tensor("wd_s", [c.D, c.FF], BF16).ap()
    dout_s = nc.dram_tensor("dout_s", [c.QROWS, c.D], F32, kind="ExternalOutput").ap()
    invs_h = nc.dram_tensor("invs_h", [1, c.QROWS], F32).ap()

    szA = max(c.NDM * c.D, c.HKV * c.SEQ + c.NKC * c.KVW,
              2 * c.D_PAIR * c.FSUB + c.D_PAIR * c.D)
    arA = es.enter_context(nc.sbuf_tensor("arA", [128, szA], BF16))
    szB = max(2 * c.NDM * c.TOKWIN, 2 * c.HQ * c.SUBQ, c.NDM * c.FSUB)
    arB = es.enter_context(nc.sbuf_tensor("arB", [128, szB], BF16))
    szC = max(c.HQ * c.TOKWIN, 2 * c.NDM * c.KVW,
              2 * c.CW + c.HQ * c.SUBQ + 2 * c.D,
              2 * c.NDM * c.GW)
    arC = es.enter_context(nc.sbuf_tensor("arC", [128, szC], BF16))
    szD = max(c.SEQ + c.QROWS, 3 * c.D)
    arD = es.enter_context(nc.sbuf_tensor("arD", [128, szD], F32))
    PREW = c.D // 2
    arF = es.enter_context(nc.sbuf_tensor("arF", [128, 2 * PREW], BF16))
    sacc = es.enter_context(nc.sbuf_tensor("sacc", [128, max(c.CW, c.D)], F32))

    onesc = es.enter_context(nc.sbuf_tensor("onesc", [128, 1], F32))
    cvals = [float(-c.SHIFT), float(c.eps1), float(c.eps2)]
    consts = es.enter_context(nc.sbuf_tensor("consts", [128, len(cvals)], F32))
    for _i, _v in enumerate(cvals):
        nc.const_aps.register(F32, _v, consts[:, _i:_i + 1])
    ones1 = es.enter_context(nc.sbuf_tensor("ones1", [1, 128], F32))
    stats = es.enter_context(nc.sbuf_tensor("stats", [128, 8], F32))
    invs_sb = es.enter_context(nc.sbuf_tensor("invs_sb", [128, c.NQT], F32))
    statlog = es.enter_context(nc.sbuf_tensor("statlog", [128, 4 * c.NQT], F32))
    fcol = es.enter_context(nc.sbuf_tensor("fcol", [64, 4], F32))
    fcoli = es.enter_context(nc.sbuf_tensor("fcoli", [64, 1], I32))
    poscol = es.enter_context(nc.sbuf_tensor("poscol", [64, 1], F32))
    TW = min(512, c.SEQ, c.QROWS)
    twin_i = es.enter_context(nc.sbuf_tensor("twin_i", [64, TW], I32))
    twin_f = es.enter_context(nc.sbuf_tensor("twin_f", [128, max(2048, 2 * c.TOKWIN, c.CW)], F32))

    pa = es.enter_context(nc.psum_tensor("pa", [128, 2048], F32))
    pb = es.enter_context(nc.psum_tensor("pb", [128, 2048], F32))
    PSLOT = [pa[:, i * 512:(i + 1) * 512] for i in range(4)] + \
            [pb[:, i * 512:(i + 1) * 512] for i in range(4)]

    SCALE = float(1.0 / np.sqrt(c.HD))
    NW_Q = c.QROWS // c.TOKWIN
    NW_K = c.SEQ // c.TOKWIN
    TT_W = c.TOKWIN // 128
    NXQ = c.QROWS // 128
    NXB = c.SEQ // 128
    N5 = 512
    NCH = c.NKC
    PR = 2
    assert c.TOKWIN <= 512 and c.CW <= 2048 and c.SUBQ <= 512
    assert c.FSUB <= 2048 and c.D_PAIR % c.GU_GROUP == 0

    wq_perm = arA
    kT = arA[:, 0:c.HKV * c.SEQ]
    vT = arA[:, c.HKV * c.SEQ: c.HKV * c.SEQ + c.NKC * c.KVW]
    pTb = arA[:, 0:2 * c.D_PAIR * c.FSUB]
    wdbuf = arA[:, 2 * c.D_PAIR * c.FSUB: 2 * c.D_PAIR * c.FSUB + c.D_PAIR * c.D]

    xTw = arB
    qTsub = arB
    xqT = arB[:, 0:c.NDM * c.FSUB]

    qTst = arC[:, 0:c.HQ * c.TOKWIN]
    wkv = arC[:, 0:2 * c.NDM * c.KVW]
    pring = arC[:, 0:PR * c.CW]
    attnT = arC[:, PR * c.CW: PR * c.CW + c.HQ * c.SUBQ]
    woring = arC[:, PR * c.CW + c.HQ * c.SUBQ: PR * c.CW + c.HQ * c.SUBQ + 2 * c.D]
    gubufG = arC[:, 0:c.NDM * c.GW]
    gubufU = arC[:, c.NDM * c.GW:2 * c.NDM * c.GW]

    cosK = arD[0:64, 0:c.SEQ]
    sinK = arD[64:128, 0:c.SEQ]
    cosQ = arD[0:64, c.SEQ:c.SEQ + c.QROWS]
    sinQ = arD[64:128, c.SEQ:c.SEQ + c.QROWS]
    tbufA = arD[:, 0:c.D]
    tbufB = arD[:, c.D:2 * c.D]
    xorig = arD[:, 2 * c.D:3 * c.D]

    sacc_f = sacc[:, 0:c.D]
    qtempi = sacc[:, 0:c.D].bitcast(I32)[:, 0:c.D]
    abuf = [tbufA, tbufB]
    foutr = [xorig, sacc_f]
    h1t = tbufA
    xqstage = pring[:, 0:c.D]

    stage = arC[:, c.HQ * c.TOKWIN: c.HQ * c.TOKWIN + 2 * c.D]
    ibc = twin_f[:, 0:c.FSUB].bitcast(F32)[:, 0:c.FSUB] if False else twin_f[:, 0:c.FSUB]
    wpre = arF
    invrow = twin_f[0:1, 0:c.CW]
    s1v = twin_f[0:64, 0:c.TOKWIN]
    s2v = twin_f[0:64, c.TOKWIN:2 * c.TOKWIN]
    iotaF = twin_f[64:128, 0:TW]
    angT = twin_f[64:128, TW:2 * TW]
    scX = twin_f[64:128, 2 * TW:3 * TW]
    scY = twin_f[64:128, 3 * TW:4 * TW]

    with nc.Block() as block:
        @block.gpsimd
        def _(g):
            ins = g.iota(twin_i[:, :], [[1, TW]], base=0, channel_multiplier=0)
            b.inc(ins, 'tab')
            ins = g.iota(fcoli[:, :], [[0, 1]], base=0, channel_multiplier=1)
            b.inc(ins, 'tab')
            ins = g.dma_start(out=poscol[:, :], in_=pos[:, :].broadcast_to([64, 1]))
            b.inc(ins, 'posl', 16)

        @block.vector
        def _(v):
            v.memset(onesc[:, :], 1.0)
            for _i, _v in enumerate(cvals):
                v.memset(consts[:, _i:_i + 1], _v)
            ins = v.memset(ones1[:, :], 1.0)
            b.inc(ins, 'tab')
        TAB0 = b.cnt['tab']

        @block.gpsimd
        def _(g):
            def cast_stream(src, dst, ntiles, lsem, wsem):
                for t in range(ntiles):
                    slot = stage[:, (t % 2) * c.D:(t % 2) * c.D + c.D]
                    if t >= 2:
                        b.w(g, wsem, 16 * (t - 1))
                    ins = g.dma_start(out=slot, in_=src[t * 128:(t + 1) * 128, :])
                    b.inc(ins, lsem, 16)
                    b.w(g, lsem, 16 * (t + 1))
                    ins = g.dma_start(out=dst[t * 128:(t + 1) * 128, :], in_=slot)
                    b.inc(ins, wsem, 16)
            cast_stream(xq, xqbf_s, NXQ, 'xql', 'xqw')
            cast_stream(xb, xbf_s, NXB, 'xbl', 'xbw')
        XQW, XBW = b.cnt['xqw'], b.cnt['xbw']

        @block.gpsimd
        def _(g):
            b.w(g, 'xbw', XBW)
            for ch in range(c.NDM):
                if ch >= 2:
                    b.w(g, 'wqp', ch - 1)
                ins = g.dma_start(out=stage[:, (ch % 2) * c.D:(ch % 2) * c.D + c.D],
                                  in_=wq[ch * 128:(ch + 1) * 128, :])
                b.inc(ins, 'wql', 16)

        @block.vector
        def _(v):
            for ch in range(c.NDM):
                b.wr(v, 'wql', 1, ch, 2, c.NDM)
                slot = stage[:, (ch % 2) * c.D:(ch % 2) * c.D + c.D]
                ins = None
                for h in range(c.HQ):
                    ev, od = eo_ap(slot[:, h * c.HD:(h + 1) * c.HD], c.HD)
                    dst = wq_perm[:, ch * c.D + h * c.HD: ch * c.D + (h + 1) * c.HD]
                    ins = v.tensor_single_scalar(out=dst[:, 0:64], in_=ev, scalar=SCALE, op=ALU.mult)
                    ins = v.tensor_single_scalar(out=dst[:, 64:128], in_=od, scalar=SCALE, op=ALU.mult)
                b.inc(ins, 'wqp')
        WQP = b.cnt['wqp']

        @block.vector
        def _(v):
            b.w(v, 'tab', TAB0)
            v.tensor_copy(out=fcol[:, 0:1], in_=fcoli[:, :])
            ins = v.tensor_copy(out=iotaF, in_=twin_i[:, :])
            b.inc(ins, 'tab')
        @block.scalar
        def _(s):
            b.w(s, 'tab', TAB0 + 1)
            ins = s.activation(out=fcol[:, 1:2], in_=fcol[:, 0:1], func=AF.Exp,
                               scale=float(-np.log(c.theta) / 64.0))
            b.inc(ins, 'tab')
        TAB1 = b.cnt['tab']

        def emit_tables(name, nwin, cosbuf, sinbuf, use_pos, prior=None):
            sV, sS = name + 'V', name + 'S'
            @block.vector
            def _(v):
                b.w(v, 'tab', TAB1)
                if prior is not None:
                    b.w(v, prior, b.cnt[prior])
                if use_pos:
                    b.w(v, 'posl', 16)
                for w in range(nwin):
                    ang = angT
                    if w >= 1:
                        b.w(v, sS, 2 * w)
                    if use_pos:
                        v.tensor_scalar(out=ang, in0=iotaF, scalar1=float(w * TW),
                                        scalar2=poscol[:, 0:1], op0=ALU.add, op1=ALU.add)
                    else:
                        v.tensor_single_scalar(out=ang, in_=iotaF, scalar=float(w * TW), op=ALU.add)
                    v.tensor_scalar_mul(out=ang, in0=ang, scalar1=fcol[:, 1:2])
                    v.tensor_scalar(out=scX, in0=ang, scalar1=float(1.0 / TWO_PI),
                                    scalar2=384.0, op0=ALU.mult, op1=ALU.add)
                    ib = scY.bitcast(I32)[:, 0:TW]
                    v.tensor_copy(out=ib, in_=scX)
                    v.tensor_copy(out=scY, in_=ib)
                    ins = v.tensor_tensor(out=scX, in0=scX, in1=scY, op=ALU.subtract)
                    b.f(v, ins, 'vf')
                    v.tensor_single_scalar(out=scX, in_=scX, scalar=0.5, op=ALU.is_ge)
                    v.tensor_tensor(out=scX, in0=scX, in1=scY, op=ALU.add)
                    ins = v.tensor_scalar(out=scX, in0=scX, scalar1=-384.0, scalar2=float(-TWO_PI),
                                    op0=ALU.add, op1=ALU.mult)
                    b.f(v, ins, 'vf')
                    v.tensor_tensor(out=ang, in0=ang, in1=scX, op=ALU.add)
                    ins = v.tensor_single_scalar(out=ang, in_=ang, scalar=PI, op=ALU.min)
                    b.f(v, ins, 'vf')
                    ins = v.tensor_single_scalar(out=ang, in_=ang, scalar=-PI, op=ALU.max)
                    b.inc(ins, sV)
                    b.w(v, sS, 2 * w + 1)
                    ins = v.tensor_single_scalar(out=ang, in_=ang, scalar=float(PI / 2), op=ALU.add)
                    b.f(v, ins, 'vf')
                    ins = v.tensor_single_scalar(out=scX, in_=ang, scalar=PI, op=ALU.is_gt)
                    b.f(v, ins, 'vf')
                    ins = v.scalar_tensor_tensor(out=ang, in0=scX, scalar=float(-TWO_PI),
                                                 in1=ang, op0=ALU.mult, op1=ALU.add)
                    b.inc(ins, sV)
            @block.scalar
            def _(s):
                for w in range(nwin):
                    b.w(s, sV, 2 * w + 1)
                    ins = s.activation(out=sinbuf[:, w * TW:(w + 1) * TW], in_=angT, func=AF.Sin)
                    b.inc(ins, sS)
                    b.w(s, sV, 2 * w + 2)
                    ins = s.activation(out=cosbuf[:, w * TW:(w + 1) * TW], in_=angT, func=AF.Sin)
                    b.inc(ins, sS)
            return b.cnt[sS]

        emit_tables('tK', c.SEQ // TW, cosK, sinK, False)
        emit_tables('tQ', c.QROWS // TW, cosQ, sinQ, True, prior='tKS')
        TKS, TQS = b.cnt['tKS'], b.cnt['tQS']

        @block.sync
        def _(sy):
            b.w(sy, 'xqw', XQW)
            def load_win(sy, w):
                slot = xTw[:, (w % 2) * c.NDM * c.TOKWIN:]
                for ch in range(c.NDM):
                    ins = sy.dma_start(
                        out=slot[:, ch * c.TOKWIN:(ch + 1) * c.TOKWIN],
                        in_=xqbf_s[w * c.TOKWIN:(w + 1) * c.TOKWIN, ch * 128:(ch + 1) * 128],
                        transpose=True)
                    b.inc(ins, 'xTq', 16)
            load_win(sy, 0)
            if NW_Q > 1:
                load_win(sy, 1)
            for w in range(NW_Q):
                b.w(sy, 'ropeQ', (w + 1) * c.HQ * 2)
                dst = bass.AP(qT_s.tensor, w * c.TOKWIN,
                              [[c.QROWS, 128], [128 * c.QROWS, c.HQ], [1, c.TOKWIN]])
                src2 = qTst[:, :].rearrange("p (h t) -> p h t", h=c.HQ)
                ins = sy.dma_start(out=dst, in_=src2)
                b.inc(ins, 'qtsD', 16)
                if w + 2 < NW_Q:
                    b.w(sy, 'qmm', (w + 1) * c.HQ)
                    load_win(sy, w + 2)

        @block.tensor
        def _(t):
            b.w(t, 'wqp', WQP)
            idx = 0
            for w in range(NW_Q):
                b.wr(t, 'xTq', c.NDM, w, 2, NW_Q)
                for h in range(c.HQ):
                    slotp = PSLOT[idx % 8]
                    if idx >= 8:
                        b.w(t, 'ropeQ', (idx - 8 + 1) * 2)
                    ins = None
                    for ch in range(c.NDM):
                        ins = t.matmul(slotp[:, 0:c.TOKWIN],
                                       wq_perm[:, ch * c.D + h * c.HD: ch * c.D + (h + 1) * c.HD],
                                       xTw[:, (w % 2) * c.NDM * c.TOKWIN + ch * c.TOKWIN:
                                           (w % 2) * c.NDM * c.TOKWIN + (ch + 1) * c.TOKWIN],
                                       start=(ch == 0), stop=(ch == c.NDM - 1))
                    b.inc(ins, 'qmm')
                    idx += 1

        @block.vector
        def _(v):
            b.w(v, 'tQS', TQS)
            idx = 0
            for w in range(NW_Q):
                for h in range(c.HQ):
                    slotp = PSLOT[idx % 8]
                    b.w(v, 'qmm', idx + 1)
                    if h == 0:
                        b.w(v, 'qtsD', 16 * w)
                    pe, po = slotp[0:64, 0:c.TOKWIN], slotp[64:128, 0:c.TOKWIN]
                    cw_ = cosQ[:, w * c.TOKWIN:(w + 1) * c.TOKWIN]
                    sw_ = sinQ[:, w * c.TOKWIN:(w + 1) * c.TOKWIN]
                    de = qTst[0:64, h * c.TOKWIN:(h + 1) * c.TOKWIN]
                    do = qTst[64:128, h * c.TOKWIN:(h + 1) * c.TOKWIN]
                    v.tensor_tensor(out=s1v, in0=pe, in1=cw_, op=ALU.mult)
                    v.tensor_tensor(out=s2v, in0=po, in1=sw_, op=ALU.mult)
                    v.tensor_tensor(out=de, in0=s1v, in1=s2v, op=ALU.subtract)
                    v.tensor_tensor(out=s1v, in0=po, in1=cw_, op=ALU.mult)
                    v.tensor_tensor(out=s2v, in0=pe, in1=sw_, op=ALU.mult)
                    ins = v.tensor_tensor(out=do, in0=s1v, in1=s2v, op=ALU.add)
                    b.inc(ins, 'ropeQ', 2)
                    idx += 1

        QTSD = b.cnt['qtsD']

        @block.gpsimd
        def _(g):
            b.w(g, 'wqp', WQP)
            for ch in range(c.NDM):
                if ch >= 2:
                    b.w(g, 'wkp', ch - 1)
                ins = g.dma_start(out=stage[:, (ch % 2) * c.D:(ch % 2) * c.D + c.KVW],
                                  in_=wk[ch * 128:(ch + 1) * 128, :])
                b.inc(ins, 'wkl', 16)
            b.w(g, 'wkp', c.NDM)
            for ch in range(c.NDM):
                ins = g.dma_start(out=wkv[:, (c.NDM + ch) * c.KVW:(c.NDM + ch + 1) * c.KVW],
                                  in_=wv[ch * 128:(ch + 1) * 128, :])
                b.inc(ins, 'wvl', 16)
        WVL = b.cnt['wvl']

        @block.vector
        def _(v):
            b.w(v, 'qtsD', QTSD)
            for ch in range(c.NDM):
                b.wr(v, 'wkl', 1, ch, 2, c.NDM)
                slot = stage[:, (ch % 2) * c.D:(ch % 2) * c.D + c.KVW]
                ins = None
                for h in range(c.HKV):
                    ev, od = eo_ap(slot[:, h * c.HD:(h + 1) * c.HD], c.HD)
                    dst = wkv[:, ch * c.KVW + h * c.HD: ch * c.KVW + (h + 1) * c.HD]
                    ins = v.tensor_copy(out=dst[:, 0:64], in_=ev)
                    ins = v.tensor_copy(out=dst[:, 64:128], in_=od)
                b.inc(ins, 'wkp')
        WKP = b.cnt['wkp']

        @block.sync
        def _(sy):
            b.w(sy, 'xbw', XBW)
            b.w(sy, 'qmm', c.QROWS // c.TOKWIN * c.HQ)
            for w in range(NW_K):
                if w >= 2:
                    b.w(sy, 'vmm', (w - 1) * TT_W)
                slot = xTw[:, (w % 2) * c.NDM * c.TOKWIN:]
                for ch in range(c.NDM):
                    ins = sy.dma_start(
                        out=slot[:, ch * c.TOKWIN:(ch + 1) * c.TOKWIN],
                        in_=xbf_s[w * c.TOKWIN:(w + 1) * c.TOKWIN, ch * 128:(ch + 1) * 128],
                        transpose=True)
                    b.inc(ins, 'xTb', 16)

        @block.tensor
        def _(t):
            b.w(t, 'wkp', WKP)
            b.w(t, 'wvl', WVL)
            b.w(t, 'ropeQ', b.cnt['ropeQ'])
            for w in range(NW_K):
                b.wr(t, 'xTb', c.NDM, w, 2, NW_K)
                if w >= 2:
                    b.w(t, 'ropeK', (w - 1) * c.HKV * 2)
                if w >= 1:
                    b.w(t, 'vev', w * TT_W)
                for kv in range(c.HKV):
                    slotp = PSLOT[(w % 2) * 4 + kv]
                    ins = None
                    for ch in range(c.NDM):
                        ins = t.matmul(slotp[:, 0:c.TOKWIN],
                                       wkv[:, ch * c.KVW + kv * c.HD: ch * c.KVW + (kv + 1) * c.HD],
                                       xTw[:, (w % 2) * c.NDM * c.TOKWIN + ch * c.TOKWIN:
                                           (w % 2) * c.NDM * c.TOKWIN + (ch + 1) * c.TOKWIN],
                                       start=(ch == 0), stop=(ch == c.NDM - 1))
                    b.inc(ins, 'kmm')
                b.w(t, 'ropeK', w * c.HKV * 2)
                if w >= 2:
                    b.w(t, 'vev', (w - 1) * TT_W)
                for tt in range(TT_W):
                    slotp = PSLOT[(1 - (w % 2)) * 4 + tt % 4]
                    ins = None
                    for ch in range(c.NDM):
                        ins = t.matmul(slotp[:, 0:c.KVW],
                                       xTw[:, (w % 2) * c.NDM * c.TOKWIN + ch * c.TOKWIN + tt * 128:
                                           (w % 2) * c.NDM * c.TOKWIN + ch * c.TOKWIN + tt * 128 + 128],
                                       wkv[:, (c.NDM + ch) * c.KVW:(c.NDM + ch + 1) * c.KVW],
                                       start=(ch == 0), stop=(ch == c.NDM - 1))
                    b.inc(ins, 'vmm')

        @block.vector
        def _(v):
            b.w(v, 'tKS', TKS)
            for w in range(NW_K):
                for kv in range(c.HKV):
                    slotp = PSLOT[(w % 2) * 4 + kv]
                    b.w(v, 'kmm', w * c.HKV + kv + 1)
                    pe, po = slotp[0:64, 0:c.TOKWIN], slotp[64:128, 0:c.TOKWIN]
                    cw_ = cosK[:, w * c.TOKWIN:(w + 1) * c.TOKWIN]
                    sw_ = sinK[:, w * c.TOKWIN:(w + 1) * c.TOKWIN]
                    base = kv * c.SEQ + w * c.TOKWIN
                    de, do = kT[0:64, base:base + c.TOKWIN], kT[64:128, base:base + c.TOKWIN]
                    v.tensor_tensor(out=s1v, in0=pe, in1=cw_, op=ALU.mult)
                    v.tensor_tensor(out=s2v, in0=po, in1=sw_, op=ALU.mult)
                    v.tensor_tensor(out=de, in0=s1v, in1=s2v, op=ALU.subtract)
                    v.tensor_tensor(out=s1v, in0=po, in1=cw_, op=ALU.mult)
                    v.tensor_tensor(out=s2v, in0=pe, in1=sw_, op=ALU.mult)
                    ins = v.tensor_tensor(out=do, in0=s1v, in1=s2v, op=ALU.add)
                    b.inc(ins, 'ropeK', 2)
                for tt in range(TT_W):
                    slotp = PSLOT[(1 - (w % 2)) * 4 + tt % 4]
                    b.w(v, 'vmm', w * TT_W + tt + 1)
                    ins = v.tensor_copy(out=vT[:, (w * TT_W + tt) * c.KVW:(w * TT_W + tt + 1) * c.KVW],
                                        in_=slotp[:, 0:c.KVW])
                    b.inc(ins, 'vev')
        KMM_ALL, VEV_ALL, ROPEK_ALL = b.cnt['kmm'], b.cnt['vev'], b.cnt['ropeK']

        pre_jobs = []
        for (wsrc, wdst, rows, cols) in ((wg, wg_s, c.FF, c.D),
                                         (wu, wu_s, c.FF, c.D),
                                         (wd, wd_s, c.D, c.FF)):
            wcols = min(cols, PREW)
            for rtile in range(rows // 128):
                for piece in range(cols // wcols):
                    pre_jobs.append((wsrc, wdst, rtile, piece * wcols, wcols))
        NPRE = len(pre_jobs)

        @block.gpsimd
        def _(g):
            b.w(g, 'wkp', WKP)
            for j, (wsrc, wdst, rt, w0, wc) in enumerate(pre_jobs):
                slot = wpre[:, (j % 2) * PREW:(j % 2) * PREW + wc]
                if j >= 2:
                    b.w(g, 'prw', 16 * (j - 1))
                ins = g.dma_start(out=slot, in_=wsrc[rt * 128:(rt + 1) * 128, w0:w0 + wc])
                b.inc(ins, 'prl', 16)
        @block.scalar
        def _(s):
            for j, (wsrc, wdst, rt, w0, wc) in enumerate(pre_jobs):
                slot = wpre[:, (j % 2) * PREW:(j % 2) * PREW + wc]
                b.wr(s, 'prl', 1, j, 2, NPRE)
                ins = s.sign(out=slot, in_=slot)
                b.inc(ins, 'prs')
                b.w(s, 'prs', b.cnt['prs'])
                ins = s.dma_start(out=wdst[rt * 128:(rt + 1) * 128, w0:w0 + wc], in_=slot)
                b.inc(ins, 'prw', 16)
        PRW = b.cnt['prw']

        @block.sync
        def _(sy):
            b.w(sy, 'kmm', KMM_ALL)
            b.w(sy, 'vmm', NW_K * TT_W)
            b.w(sy, 'qtsD', QTSD)
            for sub in range(c.NSUB):
                if sub >= 2:
                    b.w(sy, 'smm', (sub - 1) * c.HKV * NCH)
                slot = qTsub[:, (sub % 2) * c.HQ * c.SUBQ:]
                for h in range(c.HQ):
                    ins = sy.dma_start(
                        out=slot[:, h * c.SUBQ:(h + 1) * c.SUBQ],
                        in_=qT_s[h * 128:(h + 1) * 128, sub * c.SUBQ:(sub + 1) * c.SUBQ])
                    b.inc(ins, 'qsl', 16)

        @block.gpsimd
        def _(g):
            for sub in range(c.NSUB):
                for qp in range(c.QTS // 2):
                    for ch in range(c.HQ):
                        j = (sub * (c.QTS // 2) + qp) * c.HQ + ch
                        if j >= 2:
                            b.w(g, 'woc', j - 1)
                        slot = woring[:, (j % 2) * c.D:(j % 2) * c.D + c.D]
                        ins = g.dma_start(out=slot, in_=wo[ch * 128:(ch + 1) * 128, :])
                        b.inc(ins, 'wol0' if j % 2 == 0 else 'wol1', 16)
                    for qti in (qp * 2, qp * 2 + 1):
                        qt = sub * c.QTS + qti
                        if qt >= 1:
                            b.w(g, 'epi', qt)
                        ins = g.dma_start(out=xorig[:, :], in_=xq[qt * 128:(qt + 1) * 128, :])
                        b.inc(ins, 'xol', 16)

        @block.tensor
        def _(t):
            b.w(t, 'tab', TAB0)
            b.w(t, 'kmm', KMM_ALL)
            b.w(t, 'ropeK', ROPEK_ALL)
            b.w(t, 'vev', VEV_ALL)
            for sub in range(c.NSUB):
                b.wr(t, 'qsl', c.HQ, sub, 2, c.NSUB)
                if sub >= 1:
                    b.w(t, 'ln1a', sub * c.QTS)
                for kv in range(c.HKV):
                    u = sub * c.HKV + kv
                    qs = qTsub[:, (sub % 2) * c.HQ * c.SUBQ + kv * c.GRP * c.SUBQ:
                               (sub % 2) * c.HQ * c.SUBQ + (kv + 1) * c.GRP * c.SUBQ]
                    b.w(t, 'divd', u * c.GRP)
                    for i in range(NCH):
                        gi = u * NCH + i
                        if gi >= 1:
                            b.w(t, 'exp', gi)
                        ins = None
                        for n in range(c.CW // N5 if c.CW >= N5 else 1):
                            wdt = min(N5, c.CW)
                            ins = t.matmul(pa[:, n * N5:n * N5 + wdt],
                                           kT[:, kv * c.SEQ + i * 128: kv * c.SEQ + (i + 1) * 128],
                                           qs[:, n * N5:n * N5 + wdt],
                                           start=True, stop=True)
                        b.inc(ins, 'smm')
                        if i >= 1:
                            pi = (gi - 1) % PR
                            ins = None
                            for n in range(c.CW // N5 if c.CW >= N5 else 1):
                                wdt = min(N5, c.CW)
                                ins = t.matmul(pb[:, n * N5:n * N5 + wdt],
                                               vT[:, (i - 1) * c.KVW + kv * c.HD:
                                                  (i - 1) * c.KVW + (kv + 1) * c.HD],
                                               pring[:, pi * c.CW + n * N5: pi * c.CW + n * N5 + wdt],
                                               start=(i == 1), stop=False)
                            b.inc(ins, 'avm')
                    gi = u * NCH + NCH - 1
                    b.w(t, 'exp', gi + 1)
                    pi = gi % PR
                    ins = None
                    for n in range(c.CW // N5 if c.CW >= N5 else 1):
                        wdt = min(N5, c.CW)
                        ins = t.matmul(pb[:, n * N5:n * N5 + wdt],
                                       vT[:, (NCH - 1) * c.KVW + kv * c.HD:
                                          (NCH - 1) * c.KVW + (kv + 1) * c.HD],
                                       pring[:, pi * c.CW + n * N5: pi * c.CW + n * N5 + wdt],
                                       start=(NCH == 1), stop=True)
                    b.inc(ins, 'avm')
                    b.w(t, 'sadd', (u + 1) * NCH)
                    ins = None
                    for n in range(max(1, c.CW // N5)):
                        wdt = min(N5, c.CW)
                        ins = t.matmul(pa[0:1, n * N5:n * N5 + wdt],
                                       onesc[:, :], sacc[:, n * N5:n * N5 + wdt],
                                       start=True, stop=True)
                    b.inc(ins, 'srm')
                    b.w(t, 'rcp', u + 1)
                    for j in range(c.GRP):
                        if u * c.GRP + j >= 1:
                            b.w(t, 'divd', u * c.GRP + j)
                        ins = t.matmul(pa[:, 512:512 + c.SUBQ],
                                       ones1[:, :], invrow[:, j * c.SUBQ:(j + 1) * c.SUBQ],
                                       start=True, stop=True)
                        b.inc(ins, 'bcm')
                b.w(t, 'divd', (sub + 1) * c.HKV * c.GRP)
                for qp in range(c.QTS // 2):
                    if sub * (c.QTS // 2) + qp >= 1:
                        b.w(t, 'ln1a', (sub * (c.QTS // 2) + qp) * 2)
                    ins = None
                    for ch in range(c.HQ):
                        j = (sub * (c.QTS // 2) + qp) * c.HQ + ch
                        b.w(t, 'wol0' if j % 2 == 0 else 'wol1', 16 * (j // 2 + 1))
                        wslot = woring[:, (j % 2) * c.D:(j % 2) * c.D + c.D]
                        for half in range(2):
                            qti = qp * 2 + half
                            pp = pa if half == 0 else pb
                            for n in range(c.D // N5):
                                ins = t.matmul(pp[:, n * N5:(n + 1) * N5],
                                               attnT[:, ch * c.SUBQ + qti * 128: ch * c.SUBQ + qti * 128 + 128],
                                               wslot[:, n * N5:(n + 1) * N5],
                                               start=(ch == 0), stop=(ch == c.HQ - 1))
                        b.inc(ins, 'woc')

        @block.scalar
        def _(s):
            b.w(s, 'tab', TAB0)
            for sub in range(c.NSUB):
                for kv in range(c.HKV):
                    u = sub * c.HKV + kv
                    for i in range(NCH):
                        gi = u * NCH + i
                        b.w(s, 'smm', gi + 1)
                        if gi >= PR:
                            b.w(s, 'avm', gi - PR + 1)
                            b.w(s, 'sadd', gi - PR + 1)
                        pi = gi % PR
                        ins = s.activation(out=pring[:, pi * c.CW:(pi + 1) * c.CW],
                                           in_=pa[:, 0:c.CW], func=AF.Exp,
                                           bias=float(-c.SHIFT), scale=1.0)
                        b.inc(ins, 'exp')
                    b.w(s, 'srm', u + 1)
                    ins = s.copy(out=invrow[:, :], in_=pa[0:1, 0:c.CW])
                    b.inc(ins, 'srow')
                    if kv == 0:
                        b.w(s, 'ln1e', sub * c.QTS)
                    for j in range(c.GRP):
                        b.w(s, 'bcm', u * c.GRP + j + 1)
                        if u * c.GRP + j >= 1:
                            b.w(s, 'divd', u * c.GRP + j)
                        ins = s.copy(out=tbufA[:, 0:c.SUBQ], in_=pa[:, 512:512 + c.SUBQ])
                        b.inc(ins, 'bcc')
                for qti in range(c.QTS):
                    qt = sub * c.QTS + qti
                    b.w(s, 'ln1a', qt + 1)
                    ins = s.activation(out=tbufB[:, :], in_=tbufA[:, :], func=AF.Square,
                                       accum_out=stats[:, 1:2])
                    b.inc(ins, 'ln1b')
                    b.w(s, 'ln1c', qt + 1)
                    ins = s.activation(out=stats[:, 5:6], in_=stats[:, 4:5], func=AF.Ln,
                                 bias=float(c.eps1), scale=1.0)
                    b.f(s, ins, 'sf')
                    ins = s.activation(out=stats[:, 5:6], in_=stats[:, 5:6], func=AF.Exp,
                                       scale=-0.5)
                    b.inc(ins, 'ln1d')
                    b.w(s, 'ln1e', qt + 1)
                    ins = s.activation(out=tbufB[:, :], in_=tbufA[:, :], func=AF.Identity,
                                       bias=stats[:, 6:7], scale=stats[:, 7:8])
                    b.inc(ins, 'epi')
                    b.w(s, 'epi', b.cnt['epi'])
                    ins = s.dma_start(out=h1_s[qt * 128:(qt + 1) * 128, :], in_=tbufB[:, :])
                    b.inc(ins, 'h1w', 16)

        @block.vector
        def _(v):
            for sub in range(c.NSUB):
                for kv in range(c.HKV):
                    u = sub * c.HKV + kv
                    for i in range(NCH):
                        gi = u * NCH + i
                        b.w(v, 'exp', gi + 1)
                        pi = gi % PR
                        if i == 0:
                            ins = v.tensor_copy(out=sacc[:, 0:c.CW],
                                                in_=pring[:, pi * c.CW:(pi + 1) * c.CW])
                        else:
                            ins = v.tensor_tensor(out=sacc[:, 0:c.CW], in0=sacc[:, 0:c.CW],
                                                  in1=pring[:, pi * c.CW:(pi + 1) * c.CW],
                                                  op=ALU.add)
                        b.inc(ins, 'sadd')
                    b.w(v, 'srow', u + 1)
                    ins = v.reciprocal(out=invrow[:, :], in_=invrow[:, :])
                    b.inc(ins, 'rcp')
                    for j in range(c.GRP):
                        b.w(v, 'bcc', u * c.GRP + j + 1)
                        b.w(v, 'avm', (u + 1) * NCH)
                        h = kv * c.GRP + j
                        ins = v.tensor_tensor(
                            out=attnT[:, h * c.SUBQ:(h + 1) * c.SUBQ],
                            in0=pb[:, j * c.SUBQ:(j + 1) * c.SUBQ],
                            in1=tbufA[:, 0:c.SUBQ],
                            op=ALU.mult)
                        b.inc(ins, 'divd')
                for qti in range(c.QTS):
                    qt = sub * c.QTS + qti
                    pp = pa if qti % 2 == 0 else pb
                    b.w(v, 'woc', (sub * (c.QTS // 2) + qti // 2 + 1) * c.HQ)
                    b.w(v, 'xol', 16 * (qt + 1))
                    if qt >= 1:
                        b.w(v, 'epi', qt)
                    v.tensor_tensor(out=tbufA[:, :], in0=pp[:, 0:c.D], in1=xorig[:, :], op=ALU.add)
                    ins = v.reduce_sum(out=stats[:, 0:1], in_=tbufA[:, :], axis=AX.X)
                    b.inc(ins, 'ln1a')
                    b.w(v, 'ln1b', qt + 1)
                    ins = v.tensor_scalar_mul(out=stats[:, 2:3], in0=stats[:, 0:1], scalar1=float(1.0 / c.D))
                    v.tensor_scalar_mul(out=stats[:, 3:4], in0=stats[:, 1:2], scalar1=float(1.0 / c.D))
                    b.f(v, ins, 'vf')
                    ins = v.tensor_tensor(out=stats[:, 4:5], in0=stats[:, 2:3], in1=stats[:, 2:3], op=ALU.mult)
                    b.f(v, ins, 'vf')
                    ins = v.tensor_tensor(out=stats[:, 4:5], in0=stats[:, 3:4], in1=stats[:, 4:5], op=ALU.subtract)
                    b.inc(ins, 'ln1c')
                    if c.dbg:
                        v.tensor_copy(out=statlog[:, 4 * qt:4 * qt + 1], in_=stats[:, 0:1])
                        v.tensor_copy(out=statlog[:, 4 * qt + 1:4 * qt + 2], in_=stats[:, 1:2])
                        v.tensor_copy(out=statlog[:, 4 * qt + 2:4 * qt + 3], in_=stats[:, 4:5])
                    b.w(v, 'ln1d', qt + 1)
                    if c.dbg:
                        v.tensor_copy(out=statlog[:, 4 * qt + 3:4 * qt + 4], in_=stats[:, 5:6])
                    ins = v.tensor_tensor(out=stats[:, 6:7], in0=stats[:, 2:3], in1=stats[:, 5:6], op=ALU.mult)
                    b.f(v, ins, 'vf')
                    ins = v.tensor_scalar_mul(out=stats[:, 6:7], in0=stats[:, 6:7], scalar1=-1.0)
                    b.f(v, ins, 'vf')
                    ins = v.tensor_copy(out=stats[:, 7:8], in_=stats[:, 5:6])
                    b.inc(ins, 'ln1e')

        H1W = b.cnt['h1w']

        if c.dbg:
            dbg_st = nc.dram_tensor("dbg_st", [128, 4 * c.NQT], F32, kind="ExternalOutput").ap()

            @block.sync
            def _(sy):
                b.w(sy, 'ln1e', c.NQT)
                ins = sy.dma_start(out=dbg_st[:, :], in_=statlog[:, :])
                b.inc(ins, 'dbg3', 16)
                b.w(sy, 'dbg3', b.cnt['dbg3'])

            dbg_cos = nc.dram_tensor("dbg_cos", [128, c.SEQ + c.QROWS], F32, kind="ExternalOutput").ap()

            @block.sync
            def _(sy):
                b.w(sy, 'tKS', TKS)
                b.w(sy, 'tQS', TQS)
                ins = sy.dma_start(out=dbg_cos[:, 0:c.SEQ], in_=arD[:, 0:c.SEQ])
                b.inc(ins, 'dbg2', 16)
                ins = sy.dma_start(out=dbg_cos[:, c.SEQ:], in_=arD[:, c.SEQ:c.SEQ + c.QROWS])
                b.inc(ins, 'dbg2', 16)
                b.w(sy, 'dbg2', 32)

            dbg_attnT = nc.dram_tensor("dbg_attnT", [128, c.HQ * c.SUBQ], BF16, kind="ExternalOutput").ap()
            dbg_kT = nc.dram_tensor("dbg_kT", [128, c.HKV * c.SEQ], BF16, kind="ExternalOutput").ap()
            dbg_vT = nc.dram_tensor("dbg_vT", [128, c.NKC * c.KVW], BF16, kind="ExternalOutput").ap()
            dbg_sacc = nc.dram_tensor("dbg_sacc", [128, c.CW], F32, kind="ExternalOutput").ap()
            dbg_inv = nc.dram_tensor("dbg_inv", [1, c.CW], F32, kind="ExternalOutput").ap()
            dbg_p0 = nc.dram_tensor("dbg_p0", [128, c.CW], BF16, kind="ExternalOutput").ap()

            @block.sync
            def _(sy):
                b.w(sy, 'divd', c.GRP)
                b.w(sy, 'sadd', NCH)
                for ap_, dst in ((attnT, dbg_attnT), (kT, dbg_kT), (vT, dbg_vT),
                                 (sacc[:, 0:c.CW], dbg_sacc), (invrow, dbg_inv),
                                 (pring[:, 0:c.CW], dbg_p0)):
                    ins = sy.dma_start(out=dst[:, :], in_=ap_)
                    b.inc(ins, 'dbgw', 16)
                b.w(sy, 'dbgw', b.cnt['dbgw'])

        @block.sync
        def _(sy):
            b.w(sy, 'h1w', H1W)
            for qt in range(c.NQT):
                if qt >= 1:
                    b.w(sy, 'qzd', 16 * qt)
                ins = sy.dma_start(out=tbufA[:, :], in_=h1_s[qt * 128:(qt + 1) * 128, :])
                b.inc(ins, 'h1r', 16)

        @block.vector
        def _(v):
            b.w(v, 'epi', c.NQT)
            b.w(v, 'sadd', b.cnt['sadd'])
            b.w(v, 'exp', b.cnt['exp'])
            for qt in range(c.NQT):
                b.w(v, 'h1r', 16 * (qt + 1))
                ins = v.reduce_sum(out=stats[:, 0:1], in_=tbufA[:, :], axis=AX.X)
                b.inc(ins, 'ln2a')
                b.w(v, 'ln2b', qt + 1)
                ins = v.tensor_scalar_mul(out=stats[:, 2:3], in0=stats[:, 0:1], scalar1=float(1.0 / c.D))
                v.tensor_scalar_mul(out=stats[:, 3:4], in0=stats[:, 1:2], scalar1=float(1.0 / c.D))
                b.f(v, ins, 'vf')
                ins = v.tensor_tensor(out=stats[:, 4:5], in0=stats[:, 2:3], in1=stats[:, 2:3], op=ALU.mult)
                b.f(v, ins, 'vf')
                ins = v.tensor_tensor(out=stats[:, 4:5], in0=stats[:, 3:4], in1=stats[:, 4:5], op=ALU.subtract)
                b.inc(ins, 'ln2c')
                b.w(v, 'ln2d', qt + 1)
                ins = v.tensor_tensor(out=stats[:, 6:7], in0=stats[:, 2:3], in1=stats[:, 5:6], op=ALU.mult)
                b.f(v, ins, 'vf')
                ins = v.tensor_scalar_mul(out=stats[:, 6:7], in0=stats[:, 6:7], scalar1=-1.0)
                b.f(v, ins, 'vf')
                ins = v.tensor_copy(out=stats[:, 7:8], in_=stats[:, 5:6])
                b.inc(ins, 'ln2e')
                b.w(v, 'ln2f', qt + 1)
                ins = v.reduce_max(out=stats[:, 0:1], in_=tbufB[:, :], axis=AX.X,
                             apply_absolute_value=True)
                b.f(v, ins, 'vf')
                ins = v.reciprocal(out=stats[:, 1:2], in_=stats[:, 0:1])
                b.f(v, ins, 'vf')
                ins = v.tensor_scalar_mul(out=stats[:, 1:2], in0=stats[:, 1:2], scalar1=127.0)
                b.f(v, ins, 'vf')
                v.reciprocal(out=invs_sb[:, qt:qt + 1], in_=stats[:, 1:2])
                if qt >= 1:
                    b.w(v, 'qzd', 16 * qt)
                v.tensor_scalar(out=xorig[:, :], in0=tbufB[:, :], scalar1=stats[:, 1:2],
                                scalar2=384.0, op0=ALU.mult, op1=ALU.add)
                v.tensor_copy(out=qtempi[:, :], in_=xorig[:, :])
                v.tensor_copy(out=tbufB[:, :], in_=qtempi[:, :])
                ins = v.tensor_tensor(out=xorig[:, :], in0=xorig[:, :], in1=tbufB[:, :], op=ALU.subtract)
                b.f(v, ins, 'vf')
                v.tensor_single_scalar(out=xorig[:, :], in_=xorig[:, :], scalar=0.5, op=ALU.is_ge)
                ins = v.scalar_tensor_tensor(out=xqstage, in0=tbufB[:, :], scalar=-384.0,
                                             in1=xorig[:, :], op0=ALU.add, op1=ALU.add)
                b.inc(ins, 'qz')

        @block.scalar
        def _(s):
            for qt in range(c.NQT):
                b.w(s, 'ln2a', qt + 1)
                ins = s.activation(out=tbufB[:, :], in_=tbufA[:, :], func=AF.Square,
                                   accum_out=stats[:, 1:2])
                b.inc(ins, 'ln2b')
                b.w(s, 'ln2c', qt + 1)
                ins = s.activation(out=stats[:, 5:6], in_=stats[:, 4:5], func=AF.Ln,
                             bias=float(c.eps2), scale=1.0)
                b.f(s, ins, 'sf')
                ins = s.activation(out=stats[:, 5:6], in_=stats[:, 5:6], func=AF.Exp,
                                   scale=-0.5)
                b.inc(ins, 'ln2d')
                b.w(s, 'ln2e', qt + 1)
                ins = s.activation(out=tbufB[:, :], in_=tbufA[:, :], func=AF.Identity,
                                   bias=stats[:, 6:7], scale=stats[:, 7:8])
                b.inc(ins, 'ln2f')
                b.w(s, 'qz', qt + 1)
                ins = s.dma_start(out=xqn_s[qt * 128:(qt + 1) * 128, :], in_=xqstage)
                b.inc(ins, 'qzd', 16)
        QZD = b.cnt['qzd']

        GPP = c.D_PAIR // c.GU_GROUP

        @block.sync
        def _(sy):
            b.w(sy, 'qzd', QZD)
            b.w(sy, 'prw', PRW)
            dstT = bass.AP(invs_h.tensor, 0, [[1, 128], [128, c.NQT]])
            ins = sy.dma_start(out=dstT, in_=invs_sb[:, 0:c.NQT], allow_slow_non_contiguous=True)
            b.inc(ins, 'ivw', 16)
            b.w(sy, 'ivw', 16)
            for fs in range(c.NFSUB):
                if fs >= 1:
                    b.w(sy, 'pmul', fs * c.NGRP * c.GU_GROUP)
                ins = sy.dma_start(out=ibc, in_=invs_h[0:1, fs * c.FSUB:(fs + 1) * c.FSUB]
                                   .broadcast_to([128, c.FSUB]))
                b.inc(ins, 'ibl', 16)
                if fs >= 1:
                    b.w(sy, 'gmm', fs * c.NGRP * c.GU_GROUP)
                for ch in range(c.NDM):
                    ins = sy.dma_start(out=xqT[:, ch * c.FSUB:(ch + 1) * c.FSUB],
                                       in_=xqn_s[fs * c.FSUB:(fs + 1) * c.FSUB,
                                                 ch * 128:(ch + 1) * 128], transpose=True)
                    b.inc(ins, 'xqTl', 16)
                for pr_ in range(c.NPAIR):
                    P = fs * c.NPAIR + pr_
                    for gidx in range(pr_ * (c.D_PAIR // c.GU_GROUP),
                                      (pr_ + 1) * (c.D_PAIR // c.GU_GROUP)):
                        GG = fs * c.NGRP + gidx
                        b.w(sy, 'gmm', GG * c.GU_GROUP)
                        for ch in range(c.NDM):
                            ins = sy.dma_start(
                                out=gubufG[:, ch * c.GW:(ch + 1) * c.GW],
                                in_=wg_s[gidx * c.GW:(gidx + 1) * c.GW, ch * 128:(ch + 1) * 128],
                                transpose=True)
                            b.inc(ins, 'wgl', 16)
                        b.w(sy, 'umm', GG * c.GU_GROUP)
                        for ch in range(c.NDM):
                            ins = sy.dma_start(
                                out=gubufU[:, ch * c.GW:(ch + 1) * c.GW],
                                in_=wu_s[gidx * c.GW:(gidx + 1) * c.GW, ch * 128:(ch + 1) * 128],
                                transpose=True)
                            b.inc(ins, 'wul', 16)
                    b.w(sy, 'dmm', P * c.QT_F)
                    for hc in range(c.D_PAIR):
                        col = (pr_ * c.D_PAIR + hc) * 128
                        ins = sy.dma_start(out=wdbuf[:, hc * c.D:(hc + 1) * c.D],
                                           in_=wd_s[0:c.D, col:col + 128], transpose=True)
                        b.inc(ins, 'wdl', 16)

        @block.tensor
        def _(t):
            b.w(t, 'woc', b.cnt['woc'])
            for fs in range(c.NFSUB):
                b.w(t, 'xqTl', 16 * c.NDM * (fs + 1))
                for gidx in range(c.NGRP):
                    GG = fs * c.NGRP + gidx
                    P = GG // GPP
                    b.w(t, 'wgl', 16 * c.NDM * (GG + 1))
                    b.w(t, 'wul', 16 * c.NDM * (GG + 1))
                    if gidx % GPP == 0:
                        b.w(t, 'flv', P * c.QT_F)
                    for ht in range(c.GU_GROUP):
                        k = GG * c.GU_GROUP + ht
                        if k >= 1:
                            b.w(t, 'aev', k)
                        ins = None
                        for ch in range(c.NDM):
                            for n in range(max(1, c.FSUB // N5)):
                                wdt = min(N5, c.FSUB)
                                ins = t.matmul(pa[:, n * N5:n * N5 + wdt],
                                               gubufG[:, ch * c.GW + ht * 128: ch * c.GW + ht * 128 + 128],
                                               xqT[:, ch * c.FSUB + n * N5: ch * c.FSUB + n * N5 + wdt],
                                               start=(ch == 0), stop=(ch == c.NDM - 1))
                        b.inc(ins, 'gmm')
                        if k >= 1:
                            b.w(t, 'pmul', k)
                        ins = None
                        UOFF = 1024
                        for n in range(max(1, c.FSUB // N5)):
                            wdt = min(N5, c.FSUB)
                            ups = pa[:, UOFF + n * N5: UOFF + n * N5 + wdt]
                            for ch in range(c.NDM):
                                ins = t.matmul(ups,
                                               gubufU[:, ch * c.GW + ht * 128: ch * c.GW + ht * 128 + 128],
                                               xqT[:, ch * c.FSUB + n * N5: ch * c.FSUB + n * N5 + wdt],
                                               start=(ch == 0), stop=(ch == c.NDM - 1))
                        b.inc(ins, 'umm')
                    if (gidx + 1) % GPP == 0:
                        b.w(t, 'pmul', (GG + 1) * c.GU_GROUP)
                        b.w(t, 'wdl', 16 * c.D_PAIR * (P + 1))
                        for qti in range(c.QT_F):
                            fi = P * c.QT_F + qti
                            pp = pb if qti % 2 == 0 else pa
                            if qti >= 2:
                                b.w(t, 'flv', fi - 1)
                            elif qti == 1 and P >= 1:
                                pass
                            ins = None
                            ps_ = (P % 2) * c.D_PAIR * c.FSUB
                            for hc in range(c.D_PAIR):
                                for n in range(c.D // N5):
                                    ins = t.matmul(pp[:, n * N5:(n + 1) * N5],
                                                   pTb[:, ps_ + hc * c.FSUB + qti * 128:
                                                       ps_ + hc * c.FSUB + qti * 128 + 128],
                                                   wdbuf[:, hc * c.D + n * N5: hc * c.D + (n + 1) * N5],
                                                   start=(hc == 0), stop=(hc == c.D_PAIR - 1))
                            b.inc(ins, 'dmm')

        @block.vector
        def _(v):
            for fs in range(c.NFSUB):
                for gidx in range(c.NGRP):
                    GG = fs * c.NGRP + gidx
                    P = GG // GPP
                    ps_ = (P % 2) * c.D_PAIR * c.FSUB
                    within = (gidx % GPP) * c.GU_GROUP
                    for ht in range(c.GU_GROUP):
                        k = GG * c.GU_GROUP + ht
                        b.w(v, 'gmm', k + 1)
                        if k >= 2:
                            b.w(v, 'sil', k - 2 + 1)
                        if gidx == 0 and ht == 0:
                            b.w(v, 'ibl', 16 * (fs + 1))
                        ins = v.tensor_tensor(out=abuf[ht % 2][:, 0:c.FSUB],
                                              in0=pa[:, 0:c.FSUB], in1=ibc, op=ALU.mult)
                        b.inc(ins, 'aev')
                        b.w(v, 'umm', k + 1)
                        b.w(v, 'sil', k + 1)
                        ups = pa[:, 1024:1024 + c.FSUB]
                        ins = v.tensor_tensor(out=pTb[:, ps_ + (within + ht) * c.FSUB:
                                                     ps_ + (within + ht + 1) * c.FSUB],
                                              in0=abuf[ht % 2][:, 0:c.FSUB], in1=ups, op=ALU.mult)
                        b.inc(ins, 'pmul')
                    if (gidx + 1) % GPP == 0:
                        for qti in range(c.QT_F):
                            fi = P * c.QT_F + qti
                            pp = pb if qti % 2 == 0 else pa
                            b.w(v, 'dmm', fi + 1)
                            b.w(v, 'flw', 16 * fi)
                            ins = v.tensor_copy(out=foutr[qti % 2][:, :], in_=pp[:, 0:c.D])
                            b.inc(ins, 'flv')

        @block.scalar
        def _(s):
            for fs in range(c.NFSUB):
                for gidx in range(c.NGRP):
                    for ht in range(c.GU_GROUP):
                        k = (fs * c.NGRP + gidx) * c.GU_GROUP + ht
                        b.w(s, 'aev', k + 1)
                        ins = s.activation(out=abuf[ht % 2][:, 0:c.FSUB],
                                           in_=abuf[ht % 2][:, 0:c.FSUB], func=AF.Silu)
                        b.inc(ins, 'sil')

        @block.gpsimd
        def _(g):
            for fs in range(c.NFSUB):
                for pr_ in range(c.NPAIR):
                    P = fs * c.NPAIR + pr_
                    for qti in range(c.QT_F):
                        fi = P * c.QT_F + qti
                        b.w(g, 'flv', fi + 1)
                        qt = fs * c.QT_F + qti
                        if pr_ >= 1:
                            b.w(g, 'flw', 16 * ((P - 1) * c.QT_F + qti + 1))
                        ins = g.dma_start(out=dout_s[qt * 128:(qt + 1) * 128, :],
                                          in_=foutr[qti % 2][:, :],
                                          accum_op=(ALU.add if pr_ >= 1 else ALU.bypass))
                        b.inc(ins, 'flw', 16)
        FLW = b.cnt['flw']

        @block.sync
        def _(sy):
            b.w(sy, 'flw', FLW)
            b.w(sy, 'sil', b.cnt['sil'])
            for qt in range(c.NQT):
                if qt >= 1:
                    b.w(sy, 'ep3', qt)
                ins = sy.dma_start(out=foutr[0][:, :], in_=dout_s[qt * 128:(qt + 1) * 128, :])
                b.inc(ins, 'dor', 16)
                ins = sy.dma_start(out=h1t[:, :], in_=h1_s[qt * 128:(qt + 1) * 128, :])
                b.inc(ins, 'dor', 16)

        @block.vector
        def _(v):
            for qt in range(c.NQT):
                b.w(v, 'dor', 32 * (qt + 1))
                v.scalar_tensor_tensor(out=foutr[1][:, :], in0=foutr[0][:, :],
                                       scalar=invs_sb[:, qt:qt + 1], in1=h1t[:, :],
                                       op0=ALU.mult, op1=ALU.add)
                ins = v.reduce_sum(out=stats[:, 0:1], in_=foutr[1][:, :], axis=AX.X)
                b.inc(ins, 'ln3a')
                b.w(v, 'ln3b', qt + 1)
                ins = v.tensor_scalar_mul(out=stats[:, 2:3], in0=stats[:, 0:1], scalar1=float(1.0 / c.D))
                v.tensor_scalar_mul(out=stats[:, 3:4], in0=stats[:, 1:2], scalar1=float(1.0 / c.D))
                b.f(v, ins, 'vf')
                ins = v.tensor_tensor(out=stats[:, 4:5], in0=stats[:, 2:3], in1=stats[:, 2:3], op=ALU.mult)
                b.f(v, ins, 'vf')
                ins = v.tensor_tensor(out=stats[:, 4:5], in0=stats[:, 3:4], in1=stats[:, 4:5], op=ALU.subtract)
                b.inc(ins, 'ln3c')
                b.w(v, 'ln3d', qt + 1)
                ins = v.tensor_tensor(out=stats[:, 6:7], in0=stats[:, 2:3], in1=stats[:, 5:6], op=ALU.mult)
                b.f(v, ins, 'vf')
                ins = v.tensor_scalar_mul(out=stats[:, 6:7], in0=stats[:, 6:7], scalar1=-1.0)
                b.f(v, ins, 'vf')
                ins = v.tensor_copy(out=stats[:, 7:8], in_=stats[:, 5:6])
                b.inc(ins, 'ln3e')

        @block.scalar
        def _(s):
            for qt in range(c.NQT):
                b.w(s, 'ln3a', qt + 1)
                ins = s.activation(out=foutr[0][:, :], in_=foutr[1][:, :], func=AF.Square,
                                   accum_out=stats[:, 1:2])
                b.inc(ins, 'ln3b')
                b.w(s, 'ln3c', qt + 1)
                ins = s.activation(out=stats[:, 5:6], in_=stats[:, 4:5], func=AF.Ln,
                             bias=float(c.eps1), scale=1.0)
                b.f(s, ins, 'sf')
                ins = s.activation(out=stats[:, 5:6], in_=stats[:, 5:6], func=AF.Exp, scale=-0.5)
                b.inc(ins, 'ln3d')
                b.w(s, 'ln3e', qt + 1)
                ins = s.activation(out=foutr[1][:, :], in_=foutr[1][:, :], func=AF.Identity,
                                   bias=stats[:, 6:7], scale=stats[:, 7:8])
                b.inc(ins, 'ln3f')

        @block.gpsimd
        def _(g):
            for qt in range(c.NQT):
                b.w(g, 'ln3f', qt + 1)
                ins = g.dma_start(out=out[qt * 128:(qt + 1) * 128, :], in_=foutr[1][:, :])
                b.inc(ins, 'outw', 16)
                g.wait_ge(b.s('outw'), 16 * (qt + 1))
                ins = g.memset(stats[0:1, 1:2], 0.0)
                b.inc(ins, 'ep3')

    es.close()
    return nc


def kernel(L0_attn_hidden_states, L0_attn_Wq, L0_attn_Wk, L0_attn_Wv,
           L0_attn_Wo, L0_ffn_W_g, L0_ffn_W_u, L0_ffn_W_d):
    cfg = Cfg()
    x = np.ascontiguousarray(np.asarray(L0_attn_hidden_states, dtype=np.float32))
    B_, S_, D_ = x.shape
    nc = build_nc(cfg)
    ws = dict(wq=np.ascontiguousarray(L0_attn_Wq, dtype=np.float32),
              wk=np.ascontiguousarray(L0_attn_Wk, dtype=np.float32),
              wv=np.ascontiguousarray(L0_attn_Wv, dtype=np.float32),
              wo=np.ascontiguousarray(L0_attn_Wo, dtype=np.float32),
              wg=np.ascontiguousarray(L0_ffn_W_g, dtype=np.float32),
              wu=np.ascontiguousarray(L0_ffn_W_u, dtype=np.float32),
              wd=np.ascontiguousarray(L0_ffn_W_d, dtype=np.float32))
    in_maps = []
    for core in range(8):
        bidx, r = core // 2, core % 2
        im = dict(ws)
        im["xb"] = np.ascontiguousarray(x[bidx])
        im["xq"] = np.ascontiguousarray(x[bidx, r * cfg.QROWS:(r + 1) * cfg.QROWS])
        im["pos"] = np.array([[float(r * cfg.QROWS)]], dtype=np.float32)
        in_maps.append(im)
    res = run_bass_kernel_spmd(nc, in_maps, core_ids=list(range(8)))
    outp = np.empty((B_, S_, D_), dtype=np.float32)
    for core in range(8):
        bidx, r = core // 2, core % 2
        outp[bidx, r * cfg.QROWS:(r + 1) * cfg.QROWS] = res.results[core]["out"]
    return outp


# version 08
# speedup vs baseline: 1.0456x; 1.0456x over previous
import numpy as np
from contextlib import ExitStack
from dataclasses import dataclass

import concourse.bass as bass
import concourse.mybir as mybir
from concourse.bass_utils import run_bass_kernel_spmd

F32 = mybir.dt.float32
BF16 = mybir.dt.bfloat16
I32 = mybir.dt.int32
AF = mybir.ActivationFunctionType
ALU = mybir.AluOpType
AX = mybir.AxisListType
PI = float(np.pi)
TWO_PI = float(2.0 * np.pi)


@dataclass
class Cfg:
    SEQ: int = 4096
    QROWS: int = 2048
    D: int = 2048
    HQ: int = 16
    HKV: int = 4
    HD: int = 128
    FF: int = 8192
    SUBQ: int = 512
    TOKWIN: int = 512
    FSUB: int = 1024
    GU_GROUP: int = 4
    D_PAIR: int = 8
    theta: float = 10000.0
    eps1: float = 1e-5
    eps2: float = 1e-8
    SHIFT: float = 20.0
    dbg: bool = False

    @property
    def NDM(self): return self.D // 128
    @property
    def GRP(self): return self.HQ // self.HKV
    @property
    def NKC(self): return self.SEQ // 128
    @property
    def NSUB(self): return self.QROWS // self.SUBQ
    @property
    def CW(self): return self.GRP * self.SUBQ
    @property
    def NQT(self): return self.QROWS // 128
    @property
    def NHT(self): return self.FF // 128
    @property
    def NFSUB(self): return self.QROWS // self.FSUB
    @property
    def KVW(self): return self.HKV * self.HD
    @property
    def QTS(self): return self.SUBQ // 128
    @property
    def NGRP(self): return self.NHT // self.GU_GROUP
    @property
    def NPAIR(self): return self.NHT // self.D_PAIR
    @property
    def GW(self): return self.GU_GROUP * 128
    @property
    def QT_F(self): return self.FSUB // 128


def eo_ap(ap128, hd):
    ev = bass.AP(ap128.tensor, ap128.offset, [list(ap128.ap[0]), [2, hd // 2]])
    od = bass.AP(ap128.tensor, ap128.offset + 1, [list(ap128.ap[0]), [2, hd // 2]])
    return ev, od


class B:
    def __init__(self, nc, es):
        self.nc, self.es = nc, es
        self.sems, self.cnt = {}, {}

    def s(self, name):
        if name not in self.sems:
            self.sems[name] = self.es.enter_context(self.nc.semaphore(name))
            self.cnt[name] = 0
        return self.sems[name]

    def inc(self, ins, name, n=1):
        ins.then_inc(self.s(name), n)
        self.cnt[name] += n
        return self.cnt[name]

    def w(self, eng, name, val):
        if val > 0:
            eng.wait_ge(self.s(name), val)

    def f(self, eng, ins, name):
        n = self.inc(ins, name)
        eng.wait_ge(self.s(name), n)

    def wr(self, eng, name, ndma, item, depth, total):
        v = 16 * ndma * min(item + depth, total)
        if v > 0:
            eng.wait_ge(self.s(name), v)


def build_nc(cfg: Cfg):
    c = cfg
    nc = bass.Bass("TRN2", target_bir_lowering=False,
                   detect_race_conditions=False)
    es = ExitStack()
    b = B(nc, es)

    xbT = nc.dram_tensor("xbT", [c.D, c.SEQ], F32, kind="ExternalInput").ap()
    xq = nc.dram_tensor("xq", [c.QROWS, c.D], F32, kind="ExternalInput").ap()
    xqT_i = nc.dram_tensor("xqT_i", [c.D, c.QROWS], F32, kind="ExternalInput").ap()
    wq = nc.dram_tensor("wq", [c.D, c.D], F32, kind="ExternalInput").ap()
    wk = nc.dram_tensor("wk", [c.D, c.KVW], F32, kind="ExternalInput").ap()
    wv = nc.dram_tensor("wv", [c.D, c.KVW], F32, kind="ExternalInput").ap()
    wo = nc.dram_tensor("wo", [c.D, c.D], F32, kind="ExternalInput").ap()
    wgT = nc.dram_tensor("wgT", [c.D, c.FF], F32, kind="ExternalInput").ap()
    wuT = nc.dram_tensor("wuT", [c.D, c.FF], F32, kind="ExternalInput").ap()
    wdT = nc.dram_tensor("wdT", [c.FF, c.D], F32, kind="ExternalInput").ap()
    pos = nc.dram_tensor("pos", [1, 1], F32, kind="ExternalInput").ap()
    out = nc.dram_tensor("out", [c.QROWS, c.D], F32, kind="ExternalOutput").ap()

    xbfT_s = nc.dram_tensor("xbfT_s", [c.D, c.SEQ], BF16).ap()
    xqbfT_s = nc.dram_tensor("xqbfT_s", [c.D, c.QROWS], BF16).ap()
    qT_s = nc.dram_tensor("qT_s", [c.D, c.QROWS], BF16).ap()
    h1_s = nc.dram_tensor("h1_s", [c.QROWS, c.D], F32, kind="ExternalOutput" if True else "Internal").ap()
    xqn_s = nc.dram_tensor("xqn_s", [c.QROWS, c.D], BF16, kind="ExternalOutput").ap()
    wgT_s = nc.dram_tensor("wgT_s", [c.D, c.FF], BF16).ap()
    wuT_s = nc.dram_tensor("wuT_s", [c.D, c.FF], BF16).ap()
    wdT_s = nc.dram_tensor("wdT_s", [c.FF, c.D], BF16).ap()
    dout_s = nc.dram_tensor("dout_s", [c.QROWS, c.D], F32, kind="ExternalOutput").ap()
    invs_h = nc.dram_tensor("invs_h", [1, c.QROWS], F32).ap()

    szA = max(c.NDM * c.D, c.HKV * c.SEQ + c.NKC * c.KVW,
              2 * c.D_PAIR * c.FSUB + c.D_PAIR * c.D)
    arA = es.enter_context(nc.sbuf_tensor("arA", [128, szA], BF16))
    szB = max(2 * c.NDM * c.TOKWIN, 2 * c.HQ * c.SUBQ, c.NDM * c.FSUB)
    arB = es.enter_context(nc.sbuf_tensor("arB", [128, szB], BF16))
    szC = max(c.HQ * c.TOKWIN, 2 * c.NDM * c.KVW,
              2 * c.CW + c.HQ * c.SUBQ + 2 * c.D,
              2 * c.NDM * c.GW)
    arC = es.enter_context(nc.sbuf_tensor("arC", [128, szC], BF16))
    szD = max(c.SEQ + c.QROWS, 3 * c.D)
    arD = es.enter_context(nc.sbuf_tensor("arD", [128, szD], F32))
    PREW = c.D // 2
    arF = es.enter_context(nc.sbuf_tensor("arF", [128, 2 * PREW], BF16))
    sacc = es.enter_context(nc.sbuf_tensor("sacc", [128, max(c.CW, c.D)], F32))

    onesc = es.enter_context(nc.sbuf_tensor("onesc", [128, 1], F32))
    cvals = [float(-c.SHIFT), float(c.eps1), float(c.eps2)]
    consts = es.enter_context(nc.sbuf_tensor("consts", [128, len(cvals)], F32))
    for _i, _v in enumerate(cvals):
        nc.const_aps.register(F32, _v, consts[:, _i:_i + 1])
    ones1 = es.enter_context(nc.sbuf_tensor("ones1", [1, 128], F32))
    stats = es.enter_context(nc.sbuf_tensor("stats", [128, 8], F32))
    invs_sb = es.enter_context(nc.sbuf_tensor("invs_sb", [128, c.NQT], F32))
    statlog = es.enter_context(nc.sbuf_tensor("statlog", [128, 4 * c.NQT], F32))
    fcol = es.enter_context(nc.sbuf_tensor("fcol", [64, 4], F32))
    fcoli = es.enter_context(nc.sbuf_tensor("fcoli", [64, 1], I32))
    poscol = es.enter_context(nc.sbuf_tensor("poscol", [64, 1], F32))
    TW = min(512, c.SEQ, c.QROWS)
    twin_i = es.enter_context(nc.sbuf_tensor("twin_i", [64, TW], I32))
    twin_f = es.enter_context(nc.sbuf_tensor("twin_f", [128, max(2048, 2 * c.TOKWIN, c.CW)], F32))

    pa = es.enter_context(nc.psum_tensor("pa", [128, 2048], F32))
    pb = es.enter_context(nc.psum_tensor("pb", [128, 2048], F32))
    PSLOT = [pa[:, i * 512:(i + 1) * 512] for i in range(4)] + \
            [pb[:, i * 512:(i + 1) * 512] for i in range(4)]

    SCALE = float(1.0 / np.sqrt(c.HD))
    NW_Q = c.QROWS // c.TOKWIN
    NW_K = c.SEQ // c.TOKWIN
    TT_W = c.TOKWIN // 128
    NXQ = c.QROWS // 128
    NXB = c.SEQ // 128
    N5 = 512
    NCH = c.NKC
    PR = 2
    assert c.TOKWIN <= 512 and c.CW <= 2048 and c.SUBQ <= 512
    assert c.FSUB <= 2048 and c.D_PAIR % c.GU_GROUP == 0

    wq_perm = arA
    kT = arA[:, 0:c.HKV * c.SEQ]
    vT = arA[:, c.HKV * c.SEQ: c.HKV * c.SEQ + c.NKC * c.KVW]
    pTb = arA[:, 0:2 * c.D_PAIR * c.FSUB]
    wdbuf = arA[:, 2 * c.D_PAIR * c.FSUB: 2 * c.D_PAIR * c.FSUB + c.D_PAIR * c.D]

    xTw = arB
    qTsub = arB
    xqT = arB[:, 0:c.NDM * c.FSUB]

    qTst = arC[:, 0:c.HQ * c.TOKWIN]
    wkv = arC[:, 0:2 * c.NDM * c.KVW]
    pring = arC[:, 0:PR * c.CW]
    attnT = arC[:, PR * c.CW: PR * c.CW + c.HQ * c.SUBQ]
    woring = arC[:, PR * c.CW + c.HQ * c.SUBQ: PR * c.CW + c.HQ * c.SUBQ + 2 * c.D]
    gubufG = arC[:, 0:c.NDM * c.GW]
    gubufU = arC[:, c.NDM * c.GW:2 * c.NDM * c.GW]

    cosK = arD[0:64, 0:c.SEQ]
    sinK = arD[64:128, 0:c.SEQ]
    cosQ = arD[0:64, c.SEQ:c.SEQ + c.QROWS]
    sinQ = arD[64:128, c.SEQ:c.SEQ + c.QROWS]
    tbufA = arD[:, 0:c.D]
    tbufB = arD[:, c.D:2 * c.D]
    xorig = arD[:, 2 * c.D:3 * c.D]

    sacc_f = sacc[:, 0:c.D]
    qtempi = sacc[:, 0:c.D].bitcast(I32)[:, 0:c.D]
    abuf = [tbufA, tbufB]
    foutr = [xorig, sacc_f]
    h1t = tbufA
    xqstage = pring[:, 0:c.D]

    stage = arC[:, c.HQ * c.TOKWIN: c.HQ * c.TOKWIN + 2 * c.D]
    ibc = twin_f[:, 0:c.FSUB].bitcast(F32)[:, 0:c.FSUB] if False else twin_f[:, 0:c.FSUB]
    wpre = arF
    invrow = twin_f[0:1, 0:c.CW]
    s1v = twin_f[0:64, 0:c.TOKWIN]
    s2v = twin_f[0:64, c.TOKWIN:2 * c.TOKWIN]
    iotaF = twin_f[64:128, 0:TW]
    angT = twin_f[64:128, TW:2 * TW]
    scX = twin_f[64:128, 2 * TW:3 * TW]
    scY = twin_f[64:128, 3 * TW:4 * TW]

    with nc.Block() as block:
        @block.gpsimd
        def _(g):
            ins = g.iota(twin_i[:, :], [[1, TW]], base=0, channel_multiplier=0)
            b.inc(ins, 'tab')
            ins = g.iota(fcoli[:, :], [[0, 1]], base=0, channel_multiplier=1)
            b.inc(ins, 'tab')
            ins = g.dma_start(out=poscol[:, :], in_=pos[:, :].broadcast_to([64, 1]))
            b.inc(ins, 'posl', 16)

        @block.vector
        def _(v):
            v.memset(onesc[:, :], 1.0)
            for _i, _v in enumerate(cvals):
                v.memset(consts[:, _i:_i + 1], _v)
            ins = v.memset(ones1[:, :], 1.0)
            b.inc(ins, 'tab')
        TAB0 = b.cnt['tab']

        @block.gpsimd
        def _(g):
            def cast_stream(src, dst, ntiles, width, lsem, wsem):
                for t in range(ntiles):
                    slot = stage[:, (t % 2) * c.D:(t % 2) * c.D + width] \
                        if width <= c.D else stage[:, 0:width]
                    if width <= c.D:
                        if t >= 2:
                            b.w(g, wsem, 16 * (t - 1))
                    else:
                        if t >= 1:
                            b.w(g, wsem, 16 * t)
                    ins = g.dma_start(out=slot, in_=src[t * 128:(t + 1) * 128, :])
                    b.inc(ins, lsem, 16)
                    b.w(g, lsem, 16 * (t + 1))
                    ins = g.dma_start(out=dst[t * 128:(t + 1) * 128, :], in_=slot)
                    b.inc(ins, wsem, 16)
            cast_stream(xqT_i, xqbfT_s, c.NDM, c.QROWS, 'xql', 'xqw')
            cast_stream(xbT, xbfT_s, c.NDM, c.SEQ, 'xbl', 'xbw')
        XQW, XBW = b.cnt['xqw'], b.cnt['xbw']

        @block.gpsimd
        def _(g):
            b.w(g, 'xbw', XBW)
            for ch in range(c.NDM):
                if ch >= 2:
                    b.w(g, 'wqp', ch - 1)
                ins = g.dma_start(out=stage[:, (ch % 2) * c.D:(ch % 2) * c.D + c.D],
                                  in_=wq[ch * 128:(ch + 1) * 128, :])
                b.inc(ins, 'wql', 16)

        @block.vector
        def _(v):
            for ch in range(c.NDM):
                b.wr(v, 'wql', 1, ch, 2, c.NDM)
                slot = stage[:, (ch % 2) * c.D:(ch % 2) * c.D + c.D]
                ins = None
                for h in range(c.HQ):
                    ev, od = eo_ap(slot[:, h * c.HD:(h + 1) * c.HD], c.HD)
                    dst = wq_perm[:, ch * c.D + h * c.HD: ch * c.D + (h + 1) * c.HD]
                    ins = v.tensor_single_scalar(out=dst[:, 0:64], in_=ev, scalar=SCALE, op=ALU.mult)
                    ins = v.tensor_single_scalar(out=dst[:, 64:128], in_=od, scalar=SCALE, op=ALU.mult)
                b.inc(ins, 'wqp')
        WQP = b.cnt['wqp']

        @block.vector
        def _(v):
            b.w(v, 'tab', TAB0)
            v.tensor_copy(out=fcol[:, 0:1], in_=fcoli[:, :])
            ins = v.tensor_copy(out=iotaF, in_=twin_i[:, :])
            b.inc(ins, 'tab')
        @block.scalar
        def _(s):
            b.w(s, 'tab', TAB0 + 1)
            ins = s.activation(out=fcol[:, 1:2], in_=fcol[:, 0:1], func=AF.Exp,
                               scale=float(-np.log(c.theta) / 64.0))
            b.inc(ins, 'tab')
        TAB1 = b.cnt['tab']

        def emit_tables(name, nwin, cosbuf, sinbuf, use_pos, prior=None):
            sV, sS = name + 'V', name + 'S'
            @block.vector
            def _(v):
                b.w(v, 'tab', TAB1)
                if prior is not None:
                    b.w(v, prior, b.cnt[prior])
                if use_pos:
                    b.w(v, 'posl', 16)
                for w in range(nwin):
                    ang = angT
                    if w >= 1:
                        b.w(v, sS, 2 * w)
                    if use_pos:
                        v.tensor_scalar(out=ang, in0=iotaF, scalar1=float(w * TW),
                                        scalar2=poscol[:, 0:1], op0=ALU.add, op1=ALU.add)
                    else:
                        v.tensor_single_scalar(out=ang, in_=iotaF, scalar=float(w * TW), op=ALU.add)
                    v.tensor_scalar_mul(out=ang, in0=ang, scalar1=fcol[:, 1:2])
                    v.tensor_scalar(out=scX, in0=ang, scalar1=float(1.0 / TWO_PI),
                                    scalar2=384.0, op0=ALU.mult, op1=ALU.add)
                    ib = scY.bitcast(I32)[:, 0:TW]
                    v.tensor_copy(out=ib, in_=scX)
                    v.tensor_copy(out=scY, in_=ib)
                    ins = v.tensor_tensor(out=scX, in0=scX, in1=scY, op=ALU.subtract)
                    b.f(v, ins, 'vf')
                    v.tensor_single_scalar(out=scX, in_=scX, scalar=0.5, op=ALU.is_ge)
                    v.tensor_tensor(out=scX, in0=scX, in1=scY, op=ALU.add)
                    ins = v.tensor_scalar(out=scX, in0=scX, scalar1=-384.0, scalar2=float(-TWO_PI),
                                    op0=ALU.add, op1=ALU.mult)
                    b.f(v, ins, 'vf')
                    v.tensor_tensor(out=ang, in0=ang, in1=scX, op=ALU.add)
                    ins = v.tensor_single_scalar(out=ang, in_=ang, scalar=PI, op=ALU.min)
                    b.f(v, ins, 'vf')
                    ins = v.tensor_single_scalar(out=ang, in_=ang, scalar=-PI, op=ALU.max)
                    b.inc(ins, sV)
                    b.w(v, sS, 2 * w + 1)
                    ins = v.tensor_single_scalar(out=ang, in_=ang, scalar=float(PI / 2), op=ALU.add)
                    b.f(v, ins, 'vf')
                    ins = v.tensor_single_scalar(out=scX, in_=ang, scalar=PI, op=ALU.is_gt)
                    b.f(v, ins, 'vf')
                    ins = v.scalar_tensor_tensor(out=ang, in0=scX, scalar=float(-TWO_PI),
                                                 in1=ang, op0=ALU.mult, op1=ALU.add)
                    b.inc(ins, sV)
            @block.scalar
            def _(s):
                for w in range(nwin):
                    b.w(s, sV, 2 * w + 1)
                    ins = s.activation(out=sinbuf[:, w * TW:(w + 1) * TW], in_=angT, func=AF.Sin)
                    b.inc(ins, sS)
                    b.w(s, sV, 2 * w + 2)
                    ins = s.activation(out=cosbuf[:, w * TW:(w + 1) * TW], in_=angT, func=AF.Sin)
                    b.inc(ins, sS)
            return b.cnt[sS]

        emit_tables('tK', c.SEQ // TW, cosK, sinK, False)
        emit_tables('tQ', c.QROWS // TW, cosQ, sinQ, True, prior='tKS')
        TKS, TQS = b.cnt['tKS'], b.cnt['tQS']

        @block.sync
        def _(sy):
            b.w(sy, 'xqw', XQW)
            def load_win(sy, w):
                slot = xTw[:, (w % 2) * c.NDM * c.TOKWIN:]
                dst = bass.AP(slot.tensor, slot.offset,
                              [list(slot.ap[0]), [c.TOKWIN, c.NDM], [1, c.TOKWIN]])
                srcap = bass.AP(xqbfT_s.tensor, w * c.TOKWIN,
                                [[c.QROWS, 128], [128 * c.QROWS, c.NDM], [1, c.TOKWIN]])
                ins = sy.dma_start(out=dst, in_=srcap)
                b.inc(ins, 'xTq', 16)
            load_win(sy, 0)
            if NW_Q > 1:
                load_win(sy, 1)
            for w in range(NW_Q):
                b.w(sy, 'ropeQ', (w + 1) * c.HQ * 2)
                dst = bass.AP(qT_s.tensor, w * c.TOKWIN,
                              [[c.QROWS, 128], [128 * c.QROWS, c.HQ], [1, c.TOKWIN]])
                src2 = qTst[:, :].rearrange("p (h t) -> p h t", h=c.HQ)
                ins = sy.dma_start(out=dst, in_=src2)
                b.inc(ins, 'qtsD', 16)
                if w + 2 < NW_Q:
                    b.w(sy, 'qmm', (w + 1) * c.HQ)
                    load_win(sy, w + 2)

        @block.tensor
        def _(t):
            b.w(t, 'wqp', WQP)
            idx = 0
            for w in range(NW_Q):
                b.wr(t, 'xTq', 1, w, 2, NW_Q)
                for h in range(c.HQ):
                    slotp = PSLOT[idx % 8]
                    if idx >= 8:
                        b.w(t, 'ropeQ', (idx - 8 + 1) * 2)
                    ins = None
                    for ch in range(c.NDM):
                        ins = t.matmul(slotp[:, 0:c.TOKWIN],
                                       wq_perm[:, ch * c.D + h * c.HD: ch * c.D + (h + 1) * c.HD],
                                       xTw[:, (w % 2) * c.NDM * c.TOKWIN + ch * c.TOKWIN:
                                           (w % 2) * c.NDM * c.TOKWIN + (ch + 1) * c.TOKWIN],
                                       start=(ch == 0), stop=(ch == c.NDM - 1))
                    b.inc(ins, 'qmm')
                    idx += 1

        @block.vector
        def _(v):
            b.w(v, 'tQS', TQS)
            idx = 0
            for w in range(NW_Q):
                for h in range(c.HQ):
                    slotp = PSLOT[idx % 8]
                    b.w(v, 'qmm', idx + 1)
                    if h == 0:
                        b.w(v, 'qtsD', 16 * w)
                    pe, po = slotp[0:64, 0:c.TOKWIN], slotp[64:128, 0:c.TOKWIN]
                    cw_ = cosQ[:, w * c.TOKWIN:(w + 1) * c.TOKWIN]
                    sw_ = sinQ[:, w * c.TOKWIN:(w + 1) * c.TOKWIN]
                    de = qTst[0:64, h * c.TOKWIN:(h + 1) * c.TOKWIN]
                    do = qTst[64:128, h * c.TOKWIN:(h + 1) * c.TOKWIN]
                    v.tensor_tensor(out=s1v, in0=pe, in1=cw_, op=ALU.mult)
                    v.tensor_tensor(out=s2v, in0=po, in1=sw_, op=ALU.mult)
                    v.tensor_tensor(out=de, in0=s1v, in1=s2v, op=ALU.subtract)
                    v.tensor_tensor(out=s1v, in0=po, in1=cw_, op=ALU.mult)
                    v.tensor_tensor(out=s2v, in0=pe, in1=sw_, op=ALU.mult)
                    ins = v.tensor_tensor(out=do, in0=s1v, in1=s2v, op=ALU.add)
                    b.inc(ins, 'ropeQ', 2)
                    idx += 1

        QTSD = b.cnt['qtsD']

        @block.gpsimd
        def _(g):
            b.w(g, 'wqp', WQP)
            for ch in range(c.NDM):
                if ch >= 2:
                    b.w(g, 'wkp', ch - 1)
                ins = g.dma_start(out=stage[:, (ch % 2) * c.D:(ch % 2) * c.D + c.KVW],
                                  in_=wk[ch * 128:(ch + 1) * 128, :])
                b.inc(ins, 'wkl', 16)
            b.w(g, 'wkp', c.NDM)
            for ch in range(c.NDM):
                ins = g.dma_start(out=wkv[:, (c.NDM + ch) * c.KVW:(c.NDM + ch + 1) * c.KVW],
                                  in_=wv[ch * 128:(ch + 1) * 128, :])
                b.inc(ins, 'wvl', 16)
        WVL = b.cnt['wvl']

        @block.vector
        def _(v):
            b.w(v, 'qtsD', QTSD)
            for ch in range(c.NDM):
                b.wr(v, 'wkl', 1, ch, 2, c.NDM)
                slot = stage[:, (ch % 2) * c.D:(ch % 2) * c.D + c.KVW]
                ins = None
                for h in range(c.HKV):
                    ev, od = eo_ap(slot[:, h * c.HD:(h + 1) * c.HD], c.HD)
                    dst = wkv[:, ch * c.KVW + h * c.HD: ch * c.KVW + (h + 1) * c.HD]
                    ins = v.tensor_copy(out=dst[:, 0:64], in_=ev)
                    ins = v.tensor_copy(out=dst[:, 64:128], in_=od)
                b.inc(ins, 'wkp')
        WKP = b.cnt['wkp']

        @block.sync
        def _(sy):
            b.w(sy, 'xbw', XBW)
            b.w(sy, 'qmm', c.QROWS // c.TOKWIN * c.HQ)
            for w in range(NW_K):
                if w >= 2:
                    b.w(sy, 'vmm', (w - 1) * TT_W)
                slot = xTw[:, (w % 2) * c.NDM * c.TOKWIN:]
                dst = bass.AP(slot.tensor, slot.offset,
                              [list(slot.ap[0]), [c.TOKWIN, c.NDM], [1, c.TOKWIN]])
                srcap = bass.AP(xbfT_s.tensor, w * c.TOKWIN,
                                [[c.SEQ, 128], [128 * c.SEQ, c.NDM], [1, c.TOKWIN]])
                ins = sy.dma_start(out=dst, in_=srcap)
                b.inc(ins, 'xTb', 16)

        @block.tensor
        def _(t):
            b.w(t, 'wkp', WKP)
            b.w(t, 'wvl', WVL)
            b.w(t, 'ropeQ', b.cnt['ropeQ'])
            for w in range(NW_K):
                b.wr(t, 'xTb', 1, w, 2, NW_K)
                if w >= 2:
                    b.w(t, 'ropeK', (w - 1) * c.HKV * 2)
                if w >= 1:
                    b.w(t, 'vev', w * TT_W)
                for kv in range(c.HKV):
                    slotp = PSLOT[(w % 2) * 4 + kv]
                    ins = None
                    for ch in range(c.NDM):
                        ins = t.matmul(slotp[:, 0:c.TOKWIN],
                                       wkv[:, ch * c.KVW + kv * c.HD: ch * c.KVW + (kv + 1) * c.HD],
                                       xTw[:, (w % 2) * c.NDM * c.TOKWIN + ch * c.TOKWIN:
                                           (w % 2) * c.NDM * c.TOKWIN + (ch + 1) * c.TOKWIN],
                                       start=(ch == 0), stop=(ch == c.NDM - 1))
                    b.inc(ins, 'kmm')
                b.w(t, 'ropeK', w * c.HKV * 2)
                if w >= 2:
                    b.w(t, 'vev', (w - 1) * TT_W)
                for tt in range(TT_W):
                    slotp = PSLOT[(1 - (w % 2)) * 4 + tt % 4]
                    ins = None
                    for ch in range(c.NDM):
                        ins = t.matmul(slotp[:, 0:c.KVW],
                                       xTw[:, (w % 2) * c.NDM * c.TOKWIN + ch * c.TOKWIN + tt * 128:
                                           (w % 2) * c.NDM * c.TOKWIN + ch * c.TOKWIN + tt * 128 + 128],
                                       wkv[:, (c.NDM + ch) * c.KVW:(c.NDM + ch + 1) * c.KVW],
                                       start=(ch == 0), stop=(ch == c.NDM - 1))
                    b.inc(ins, 'vmm')

        @block.vector
        def _(v):
            b.w(v, 'tKS', TKS)
            for w in range(NW_K):
                for kv in range(c.HKV):
                    slotp = PSLOT[(w % 2) * 4 + kv]
                    b.w(v, 'kmm', w * c.HKV + kv + 1)
                    pe, po = slotp[0:64, 0:c.TOKWIN], slotp[64:128, 0:c.TOKWIN]
                    cw_ = cosK[:, w * c.TOKWIN:(w + 1) * c.TOKWIN]
                    sw_ = sinK[:, w * c.TOKWIN:(w + 1) * c.TOKWIN]
                    base = kv * c.SEQ + w * c.TOKWIN
                    de, do = kT[0:64, base:base + c.TOKWIN], kT[64:128, base:base + c.TOKWIN]
                    v.tensor_tensor(out=s1v, in0=pe, in1=cw_, op=ALU.mult)
                    v.tensor_tensor(out=s2v, in0=po, in1=sw_, op=ALU.mult)
                    v.tensor_tensor(out=de, in0=s1v, in1=s2v, op=ALU.subtract)
                    v.tensor_tensor(out=s1v, in0=po, in1=cw_, op=ALU.mult)
                    v.tensor_tensor(out=s2v, in0=pe, in1=sw_, op=ALU.mult)
                    ins = v.tensor_tensor(out=do, in0=s1v, in1=s2v, op=ALU.add)
                    b.inc(ins, 'ropeK', 2)
                for tt in range(TT_W):
                    slotp = PSLOT[(1 - (w % 2)) * 4 + tt % 4]
                    b.w(v, 'vmm', w * TT_W + tt + 1)
                    ins = v.tensor_copy(out=vT[:, (w * TT_W + tt) * c.KVW:(w * TT_W + tt + 1) * c.KVW],
                                        in_=slotp[:, 0:c.KVW])
                    b.inc(ins, 'vev')
        KMM_ALL, VEV_ALL, ROPEK_ALL = b.cnt['kmm'], b.cnt['vev'], b.cnt['ropeK']

        pre_jobs = []
        for (wsrc, wdst, rows, cols) in ((wgT, wgT_s, c.D, c.FF),
                                         (wuT, wuT_s, c.D, c.FF),
                                         (wdT, wdT_s, c.FF, c.D)):
            wcols = min(cols, PREW)
            for rtile in range(rows // 128):
                for piece in range(cols // wcols):
                    pre_jobs.append((wsrc, wdst, rtile, piece * wcols, wcols))
        NPRE = len(pre_jobs)

        @block.gpsimd
        def _(g):
            b.w(g, 'wkp', WKP)
            for j, (wsrc, wdst, rt, w0, wc) in enumerate(pre_jobs):
                slot = wpre[:, (j % 2) * PREW:(j % 2) * PREW + wc]
                if j >= 2:
                    b.w(g, 'prw', 16 * (j - 1))
                ins = g.dma_start(out=slot, in_=wsrc[rt * 128:(rt + 1) * 128, w0:w0 + wc])
                b.inc(ins, 'prl', 16)
        @block.scalar
        def _(s):
            for j, (wsrc, wdst, rt, w0, wc) in enumerate(pre_jobs):
                slot = wpre[:, (j % 2) * PREW:(j % 2) * PREW + wc]
                b.wr(s, 'prl', 1, j, 2, NPRE)
                ins = s.sign(out=slot, in_=slot)
                b.inc(ins, 'prs')
                b.w(s, 'prs', b.cnt['prs'])
                ins = s.dma_start(out=wdst[rt * 128:(rt + 1) * 128, w0:w0 + wc], in_=slot)
                b.inc(ins, 'prw', 16)
        PRW = b.cnt['prw']

        @block.sync
        def _(sy):
            b.w(sy, 'kmm', KMM_ALL)
            b.w(sy, 'vmm', NW_K * TT_W)
            b.w(sy, 'qtsD', QTSD)
            for sub in range(c.NSUB):
                if sub >= 2:
                    b.w(sy, 'smm', (sub - 1) * c.HKV * NCH)
                slot = qTsub[:, (sub % 2) * c.HQ * c.SUBQ:]
                for h in range(c.HQ):
                    ins = sy.dma_start(
                        out=slot[:, h * c.SUBQ:(h + 1) * c.SUBQ],
                        in_=qT_s[h * 128:(h + 1) * 128, sub * c.SUBQ:(sub + 1) * c.SUBQ])
                    b.inc(ins, 'qsl', 16)

        @block.gpsimd
        def _(g):
            for sub in range(c.NSUB):
                for qp in range(c.QTS // 2):
                    for ch in range(c.HQ):
                        j = (sub * (c.QTS // 2) + qp) * c.HQ + ch
                        if j >= 2:
                            b.w(g, 'woc', j - 1)
                        slot = woring[:, (j % 2) * c.D:(j % 2) * c.D + c.D]
                        ins = g.dma_start(out=slot, in_=wo[ch * 128:(ch + 1) * 128, :])
                        b.inc(ins, 'wol0' if j % 2 == 0 else 'wol1', 16)
                    for qti in (qp * 2, qp * 2 + 1):
                        qt = sub * c.QTS + qti
                        if qt >= 1:
                            b.w(g, 'epi', qt)
                        ins = g.dma_start(out=xorig[:, :], in_=xq[qt * 128:(qt + 1) * 128, :])
                        b.inc(ins, 'xol', 16)

        @block.tensor
        def _(t):
            b.w(t, 'tab', TAB0)
            b.w(t, 'kmm', KMM_ALL)
            b.w(t, 'ropeK', ROPEK_ALL)
            b.w(t, 'vev', VEV_ALL)
            for sub in range(c.NSUB):
                b.wr(t, 'qsl', c.HQ, sub, 2, c.NSUB)
                if sub >= 1:
                    b.w(t, 'ln1a', sub * c.QTS)
                for kv in range(c.HKV):
                    u = sub * c.HKV + kv
                    qs = qTsub[:, (sub % 2) * c.HQ * c.SUBQ + kv * c.GRP * c.SUBQ:
                               (sub % 2) * c.HQ * c.SUBQ + (kv + 1) * c.GRP * c.SUBQ]
                    b.w(t, 'divd', u * c.GRP)
                    for i in range(NCH):
                        gi = u * NCH + i
                        if gi >= 1:
                            b.w(t, 'exp', gi)
                        ins = None
                        for n in range(c.CW // N5 if c.CW >= N5 else 1):
                            wdt = min(N5, c.CW)
                            ins = t.matmul(pa[:, n * N5:n * N5 + wdt],
                                           kT[:, kv * c.SEQ + i * 128: kv * c.SEQ + (i + 1) * 128],
                                           qs[:, n * N5:n * N5 + wdt],
                                           start=True, stop=True)
                        b.inc(ins, 'smm')
                        if i >= 1:
                            pi = (gi - 1) % PR
                            ins = None
                            for n in range(c.CW // N5 if c.CW >= N5 else 1):
                                wdt = min(N5, c.CW)
                                ins = t.matmul(pb[:, n * N5:n * N5 + wdt],
                                               vT[:, (i - 1) * c.KVW + kv * c.HD:
                                                  (i - 1) * c.KVW + (kv + 1) * c.HD],
                                               pring[:, pi * c.CW + n * N5: pi * c.CW + n * N5 + wdt],
                                               start=(i == 1), stop=False)
                            b.inc(ins, 'avm')
                    gi = u * NCH + NCH - 1
                    b.w(t, 'exp', gi + 1)
                    pi = gi % PR
                    ins = None
                    for n in range(c.CW // N5 if c.CW >= N5 else 1):
                        wdt = min(N5, c.CW)
                        ins = t.matmul(pb[:, n * N5:n * N5 + wdt],
                                       vT[:, (NCH - 1) * c.KVW + kv * c.HD:
                                          (NCH - 1) * c.KVW + (kv + 1) * c.HD],
                                       pring[:, pi * c.CW + n * N5: pi * c.CW + n * N5 + wdt],
                                       start=(NCH == 1), stop=True)
                    b.inc(ins, 'avm')
                    b.w(t, 'sadd', (u + 1) * NCH)
                    ins = None
                    for n in range(max(1, c.CW // N5)):
                        wdt = min(N5, c.CW)
                        ins = t.matmul(pa[0:1, n * N5:n * N5 + wdt],
                                       onesc[:, :], sacc[:, n * N5:n * N5 + wdt],
                                       start=True, stop=True)
                    b.inc(ins, 'srm')
                    b.w(t, 'rcp', u + 1)
                    for j in range(c.GRP):
                        if u * c.GRP + j >= 1:
                            b.w(t, 'divd', u * c.GRP + j)
                        ins = t.matmul(pa[:, 512:512 + c.SUBQ],
                                       ones1[:, :], invrow[:, j * c.SUBQ:(j + 1) * c.SUBQ],
                                       start=True, stop=True)
                        b.inc(ins, 'bcm')
                b.w(t, 'divd', (sub + 1) * c.HKV * c.GRP)
                for qp in range(c.QTS // 2):
                    if sub * (c.QTS // 2) + qp >= 1:
                        b.w(t, 'ln1a', (sub * (c.QTS // 2) + qp) * 2)
                    ins = None
                    for ch in range(c.HQ):
                        j = (sub * (c.QTS // 2) + qp) * c.HQ + ch
                        b.w(t, 'wol0' if j % 2 == 0 else 'wol1', 16 * (j // 2 + 1))
                        wslot = woring[:, (j % 2) * c.D:(j % 2) * c.D + c.D]
                        for half in range(2):
                            qti = qp * 2 + half
                            pp = pa if half == 0 else pb
                            for n in range(c.D // N5):
                                ins = t.matmul(pp[:, n * N5:(n + 1) * N5],
                                               attnT[:, ch * c.SUBQ + qti * 128: ch * c.SUBQ + qti * 128 + 128],
                                               wslot[:, n * N5:(n + 1) * N5],
                                               start=(ch == 0), stop=(ch == c.HQ - 1))
                        b.inc(ins, 'woc')

        @block.scalar
        def _(s):
            b.w(s, 'tab', TAB0)
            for sub in range(c.NSUB):
                for kv in range(c.HKV):
                    u = sub * c.HKV + kv
                    for i in range(NCH):
                        gi = u * NCH + i
                        b.w(s, 'smm', gi + 1)
                        if gi >= PR:
                            b.w(s, 'avm', gi - PR + 1)
                            b.w(s, 'sadd', gi - PR + 1)
                        pi = gi % PR
                        ins = s.activation(out=pring[:, pi * c.CW:(pi + 1) * c.CW],
                                           in_=pa[:, 0:c.CW], func=AF.Exp,
                                           bias=float(-c.SHIFT), scale=1.0)
                        b.inc(ins, 'exp')
                    b.w(s, 'srm', u + 1)
                    ins = s.copy(out=invrow[:, :], in_=pa[0:1, 0:c.CW])
                    b.inc(ins, 'srow')
                    if kv == 0:
                        b.w(s, 'ln1e', sub * c.QTS)
                    for j in range(c.GRP):
                        b.w(s, 'bcm', u * c.GRP + j + 1)
                        if u * c.GRP + j >= 1:
                            b.w(s, 'divd', u * c.GRP + j)
                        ins = s.copy(out=tbufA[:, 0:c.SUBQ], in_=pa[:, 512:512 + c.SUBQ])
                        b.inc(ins, 'bcc')
                for qti in range(c.QTS):
                    qt = sub * c.QTS + qti
                    b.w(s, 'ln1a', qt + 1)
                    ins = s.activation(out=tbufB[:, :], in_=tbufA[:, :], func=AF.Square,
                                       accum_out=stats[:, 1:2])
                    b.inc(ins, 'ln1b')
                    b.w(s, 'ln1c', qt + 1)
                    ins = s.activation(out=stats[:, 5:6], in_=stats[:, 4:5], func=AF.Ln,
                                 bias=float(c.eps1), scale=1.0)
                    b.f(s, ins, 'sf')
                    ins = s.activation(out=stats[:, 5:6], in_=stats[:, 5:6], func=AF.Exp,
                                       scale=-0.5)
                    b.inc(ins, 'ln1d')
                    b.w(s, 'ln1e', qt + 1)
                    ins = s.activation(out=tbufB[:, :], in_=tbufA[:, :], func=AF.Identity,
                                       bias=stats[:, 6:7], scale=stats[:, 7:8])
                    b.inc(ins, 'epi')
                    b.w(s, 'epi', b.cnt['epi'])
                    ins = s.dma_start(out=h1_s[qt * 128:(qt + 1) * 128, :], in_=tbufB[:, :])
                    b.inc(ins, 'h1w', 16)

        @block.vector
        def _(v):
            for sub in range(c.NSUB):
                for kv in range(c.HKV):
                    u = sub * c.HKV + kv
                    for i in range(NCH):
                        gi = u * NCH + i
                        b.w(v, 'exp', gi + 1)
                        pi = gi % PR
                        if i == 0:
                            ins = v.tensor_copy(out=sacc[:, 0:c.CW],
                                                in_=pring[:, pi * c.CW:(pi + 1) * c.CW])
                        else:
                            ins = v.tensor_tensor(out=sacc[:, 0:c.CW], in0=sacc[:, 0:c.CW],
                                                  in1=pring[:, pi * c.CW:(pi + 1) * c.CW],
                                                  op=ALU.add)
                        b.inc(ins, 'sadd')
                    b.w(v, 'srow', u + 1)
                    ins = v.reciprocal(out=invrow[:, :], in_=invrow[:, :])
                    b.inc(ins, 'rcp')
                    for j in range(c.GRP):
                        b.w(v, 'bcc', u * c.GRP + j + 1)
                        b.w(v, 'avm', (u + 1) * NCH)
                        h = kv * c.GRP + j
                        ins = v.tensor_tensor(
                            out=attnT[:, h * c.SUBQ:(h + 1) * c.SUBQ],
                            in0=pb[:, j * c.SUBQ:(j + 1) * c.SUBQ],
                            in1=tbufA[:, 0:c.SUBQ],
                            op=ALU.mult)
                        b.inc(ins, 'divd')
                for qti in range(c.QTS):
                    qt = sub * c.QTS + qti
                    pp = pa if qti % 2 == 0 else pb
                    b.w(v, 'woc', (sub * (c.QTS // 2) + qti // 2 + 1) * c.HQ)
                    b.w(v, 'xol', 16 * (qt + 1))
                    if qt >= 1:
                        b.w(v, 'epi', qt)
                    v.tensor_tensor(out=tbufA[:, :], in0=pp[:, 0:c.D], in1=xorig[:, :], op=ALU.add)
                    ins = v.reduce_sum(out=stats[:, 0:1], in_=tbufA[:, :], axis=AX.X)
                    b.inc(ins, 'ln1a')
                    b.w(v, 'ln1b', qt + 1)
                    ins = v.tensor_scalar_mul(out=stats[:, 2:3], in0=stats[:, 0:1], scalar1=float(1.0 / c.D))
                    v.tensor_scalar_mul(out=stats[:, 3:4], in0=stats[:, 1:2], scalar1=float(1.0 / c.D))
                    b.f(v, ins, 'vf')
                    ins = v.tensor_tensor(out=stats[:, 4:5], in0=stats[:, 2:3], in1=stats[:, 2:3], op=ALU.mult)
                    b.f(v, ins, 'vf')
                    ins = v.tensor_tensor(out=stats[:, 4:5], in0=stats[:, 3:4], in1=stats[:, 4:5], op=ALU.subtract)
                    b.inc(ins, 'ln1c')
                    if c.dbg:
                        v.tensor_copy(out=statlog[:, 4 * qt:4 * qt + 1], in_=stats[:, 0:1])
                        v.tensor_copy(out=statlog[:, 4 * qt + 1:4 * qt + 2], in_=stats[:, 1:2])
                        v.tensor_copy(out=statlog[:, 4 * qt + 2:4 * qt + 3], in_=stats[:, 4:5])
                    b.w(v, 'ln1d', qt + 1)
                    if c.dbg:
                        v.tensor_copy(out=statlog[:, 4 * qt + 3:4 * qt + 4], in_=stats[:, 5:6])
                    ins = v.tensor_tensor(out=stats[:, 6:7], in0=stats[:, 2:3], in1=stats[:, 5:6], op=ALU.mult)
                    b.f(v, ins, 'vf')
                    ins = v.tensor_scalar_mul(out=stats[:, 6:7], in0=stats[:, 6:7], scalar1=-1.0)
                    b.f(v, ins, 'vf')
                    ins = v.tensor_copy(out=stats[:, 7:8], in_=stats[:, 5:6])
                    b.inc(ins, 'ln1e')

        H1W = b.cnt['h1w']

        if c.dbg:
            dbg_st = nc.dram_tensor("dbg_st", [128, 4 * c.NQT], F32, kind="ExternalOutput").ap()

            @block.sync
            def _(sy):
                b.w(sy, 'ln1e', c.NQT)
                ins = sy.dma_start(out=dbg_st[:, :], in_=statlog[:, :])
                b.inc(ins, 'dbg3', 16)
                b.w(sy, 'dbg3', b.cnt['dbg3'])

            dbg_cos = nc.dram_tensor("dbg_cos", [128, c.SEQ + c.QROWS], F32, kind="ExternalOutput").ap()

            @block.sync
            def _(sy):
                b.w(sy, 'tKS', TKS)
                b.w(sy, 'tQS', TQS)
                ins = sy.dma_start(out=dbg_cos[:, 0:c.SEQ], in_=arD[:, 0:c.SEQ])
                b.inc(ins, 'dbg2', 16)
                ins = sy.dma_start(out=dbg_cos[:, c.SEQ:], in_=arD[:, c.SEQ:c.SEQ + c.QROWS])
                b.inc(ins, 'dbg2', 16)
                b.w(sy, 'dbg2', 32)

            dbg_attnT = nc.dram_tensor("dbg_attnT", [128, c.HQ * c.SUBQ], BF16, kind="ExternalOutput").ap()
            dbg_kT = nc.dram_tensor("dbg_kT", [128, c.HKV * c.SEQ], BF16, kind="ExternalOutput").ap()
            dbg_vT = nc.dram_tensor("dbg_vT", [128, c.NKC * c.KVW], BF16, kind="ExternalOutput").ap()
            dbg_sacc = nc.dram_tensor("dbg_sacc", [128, c.CW], F32, kind="ExternalOutput").ap()
            dbg_inv = nc.dram_tensor("dbg_inv", [1, c.CW], F32, kind="ExternalOutput").ap()
            dbg_p0 = nc.dram_tensor("dbg_p0", [128, c.CW], BF16, kind="ExternalOutput").ap()

            @block.sync
            def _(sy):
                b.w(sy, 'divd', c.GRP)
                b.w(sy, 'sadd', NCH)
                for ap_, dst in ((attnT, dbg_attnT), (kT, dbg_kT), (vT, dbg_vT),
                                 (sacc[:, 0:c.CW], dbg_sacc), (invrow, dbg_inv),
                                 (pring[:, 0:c.CW], dbg_p0)):
                    ins = sy.dma_start(out=dst[:, :], in_=ap_)
                    b.inc(ins, 'dbgw', 16)
                b.w(sy, 'dbgw', b.cnt['dbgw'])

        @block.sync
        def _(sy):
            b.w(sy, 'h1w', H1W)
            for qt in range(c.NQT):
                if qt >= 1:
                    b.w(sy, 'qzd', 16 * qt)
                ins = sy.dma_start(out=tbufA[:, :], in_=h1_s[qt * 128:(qt + 1) * 128, :])
                b.inc(ins, 'h1r', 16)

        @block.vector
        def _(v):
            b.w(v, 'epi', c.NQT)
            b.w(v, 'sadd', b.cnt['sadd'])
            b.w(v, 'exp', b.cnt['exp'])
            for qt in range(c.NQT):
                b.w(v, 'h1r', 16 * (qt + 1))
                ins = v.reduce_sum(out=stats[:, 0:1], in_=tbufA[:, :], axis=AX.X)
                b.inc(ins, 'ln2a')
                b.w(v, 'ln2b', qt + 1)
                ins = v.tensor_scalar_mul(out=stats[:, 2:3], in0=stats[:, 0:1], scalar1=float(1.0 / c.D))
                v.tensor_scalar_mul(out=stats[:, 3:4], in0=stats[:, 1:2], scalar1=float(1.0 / c.D))
                b.f(v, ins, 'vf')
                ins = v.tensor_tensor(out=stats[:, 4:5], in0=stats[:, 2:3], in1=stats[:, 2:3], op=ALU.mult)
                b.f(v, ins, 'vf')
                ins = v.tensor_tensor(out=stats[:, 4:5], in0=stats[:, 3:4], in1=stats[:, 4:5], op=ALU.subtract)
                b.inc(ins, 'ln2c')
                b.w(v, 'ln2d', qt + 1)
                ins = v.tensor_tensor(out=stats[:, 6:7], in0=stats[:, 2:3], in1=stats[:, 5:6], op=ALU.mult)
                b.f(v, ins, 'vf')
                ins = v.tensor_scalar_mul(out=stats[:, 6:7], in0=stats[:, 6:7], scalar1=-1.0)
                b.f(v, ins, 'vf')
                ins = v.tensor_copy(out=stats[:, 7:8], in_=stats[:, 5:6])
                b.inc(ins, 'ln2e')
                b.w(v, 'ln2f', qt + 1)
                ins = v.reduce_max(out=stats[:, 0:1], in_=tbufB[:, :], axis=AX.X,
                             apply_absolute_value=True)
                b.f(v, ins, 'vf')
                ins = v.reciprocal(out=stats[:, 1:2], in_=stats[:, 0:1])
                b.f(v, ins, 'vf')
                ins = v.tensor_scalar_mul(out=stats[:, 1:2], in0=stats[:, 1:2], scalar1=127.0)
                b.f(v, ins, 'vf')
                v.reciprocal(out=invs_sb[:, qt:qt + 1], in_=stats[:, 1:2])
                if qt >= 1:
                    b.w(v, 'qzd', 16 * qt)
                v.tensor_scalar(out=xorig[:, :], in0=tbufB[:, :], scalar1=stats[:, 1:2],
                                scalar2=384.0, op0=ALU.mult, op1=ALU.add)
                v.tensor_copy(out=qtempi[:, :], in_=xorig[:, :])
                v.tensor_copy(out=tbufB[:, :], in_=qtempi[:, :])
                ins = v.tensor_tensor(out=xorig[:, :], in0=xorig[:, :], in1=tbufB[:, :], op=ALU.subtract)
                b.f(v, ins, 'vf')
                v.tensor_single_scalar(out=xorig[:, :], in_=xorig[:, :], scalar=0.5, op=ALU.is_ge)
                ins = v.scalar_tensor_tensor(out=xqstage, in0=tbufB[:, :], scalar=-384.0,
                                             in1=xorig[:, :], op0=ALU.add, op1=ALU.add)
                b.inc(ins, 'qz')

        @block.scalar
        def _(s):
            for qt in range(c.NQT):
                b.w(s, 'ln2a', qt + 1)
                ins = s.activation(out=tbufB[:, :], in_=tbufA[:, :], func=AF.Square,
                                   accum_out=stats[:, 1:2])
                b.inc(ins, 'ln2b')
                b.w(s, 'ln2c', qt + 1)
                ins = s.activation(out=stats[:, 5:6], in_=stats[:, 4:5], func=AF.Ln,
                             bias=float(c.eps2), scale=1.0)
                b.f(s, ins, 'sf')
                ins = s.activation(out=stats[:, 5:6], in_=stats[:, 5:6], func=AF.Exp,
                                   scale=-0.5)
                b.inc(ins, 'ln2d')
                b.w(s, 'ln2e', qt + 1)
                ins = s.activation(out=tbufB[:, :], in_=tbufA[:, :], func=AF.Identity,
                                   bias=stats[:, 6:7], scale=stats[:, 7:8])
                b.inc(ins, 'ln2f')
                b.w(s, 'qz', qt + 1)
                ins = s.dma_start(out=xqn_s[qt * 128:(qt + 1) * 128, :], in_=xqstage)
                b.inc(ins, 'qzd', 16)
        QZD = b.cnt['qzd']

        GPP = c.D_PAIR // c.GU_GROUP

        @block.sync
        def _(sy):
            b.w(sy, 'qzd', QZD)
            b.w(sy, 'prw', PRW)
            dstT = bass.AP(invs_h.tensor, 0, [[1, 128], [128, c.NQT]])
            ins = sy.dma_start(out=dstT, in_=invs_sb[:, 0:c.NQT], allow_slow_non_contiguous=True)
            b.inc(ins, 'ivw', 16)
            b.w(sy, 'ivw', 16)
            for fs in range(c.NFSUB):
                if fs >= 1:
                    b.w(sy, 'pmul', fs * c.NGRP * c.GU_GROUP)
                ins = sy.dma_start(out=ibc, in_=invs_h[0:1, fs * c.FSUB:(fs + 1) * c.FSUB]
                                   .broadcast_to([128, c.FSUB]))
                b.inc(ins, 'ibl', 16)
                if fs >= 1:
                    b.w(sy, 'gmm', fs * c.NGRP * c.GU_GROUP)
                for ch in range(c.NDM):
                    ins = sy.dma_start(out=xqT[:, ch * c.FSUB:(ch + 1) * c.FSUB],
                                       in_=xqn_s[fs * c.FSUB:(fs + 1) * c.FSUB,
                                                 ch * 128:(ch + 1) * 128], transpose=True)
                    b.inc(ins, 'xqTl', 16)
                for pr_ in range(c.NPAIR):
                    P = fs * c.NPAIR + pr_
                    for gidx in range(pr_ * (c.D_PAIR // c.GU_GROUP),
                                      (pr_ + 1) * (c.D_PAIR // c.GU_GROUP)):
                        GG = fs * c.NGRP + gidx
                        b.w(sy, 'gmm', GG * c.GU_GROUP)
                        dstG = bass.AP(gubufG.tensor, gubufG.offset,
                                       [list(gubufG.ap[0]), [c.GW, c.NDM], [1, c.GW]])
                        srcG = bass.AP(wgT_s.tensor, gidx * c.GW,
                                       [[c.FF, 128], [128 * c.FF, c.NDM], [1, c.GW]])
                        ins = sy.dma_start(out=dstG, in_=srcG)
                        b.inc(ins, 'wgl', 16)
                        b.w(sy, 'umm', GG * c.GU_GROUP)
                        dstU = bass.AP(gubufU.tensor, gubufU.offset,
                                       [list(gubufU.ap[0]), [c.GW, c.NDM], [1, c.GW]])
                        srcU = bass.AP(wuT_s.tensor, gidx * c.GW,
                                       [[c.FF, 128], [128 * c.FF, c.NDM], [1, c.GW]])
                        ins = sy.dma_start(out=dstU, in_=srcU)
                        b.inc(ins, 'wul', 16)
                    b.w(sy, 'dmm', P * c.QT_F)
                    for hc in range(c.D_PAIR):
                        row = (pr_ * c.D_PAIR + hc) * 128
                        ins = sy.dma_start(out=wdbuf[:, hc * c.D:(hc + 1) * c.D],
                                           in_=wdT_s[row:row + 128, :])
                        b.inc(ins, 'wdl', 16)

        @block.tensor
        def _(t):
            b.w(t, 'woc', b.cnt['woc'])
            for fs in range(c.NFSUB):
                b.w(t, 'xqTl', 16 * c.NDM * (fs + 1))
                for gidx in range(c.NGRP):
                    GG = fs * c.NGRP + gidx
                    P = GG // GPP
                    b.w(t, 'wgl', 16 * (GG + 1))
                    b.w(t, 'wul', 16 * (GG + 1))
                    if gidx % GPP == 0:
                        b.w(t, 'flv', P * c.QT_F)
                    for ht in range(c.GU_GROUP):
                        k = GG * c.GU_GROUP + ht
                        if k >= 1:
                            b.w(t, 'aev', k)
                        ins = None
                        for ch in range(c.NDM):
                            for n in range(max(1, c.FSUB // N5)):
                                wdt = min(N5, c.FSUB)
                                ins = t.matmul(pa[:, n * N5:n * N5 + wdt],
                                               gubufG[:, ch * c.GW + ht * 128: ch * c.GW + ht * 128 + 128],
                                               xqT[:, ch * c.FSUB + n * N5: ch * c.FSUB + n * N5 + wdt],
                                               start=(ch == 0), stop=(ch == c.NDM - 1))
                        b.inc(ins, 'gmm')
                        if k >= 1:
                            b.w(t, 'pmul', k)
                        ins = None
                        UOFF = 1024
                        for n in range(max(1, c.FSUB // N5)):
                            wdt = min(N5, c.FSUB)
                            ups = pa[:, UOFF + n * N5: UOFF + n * N5 + wdt]
                            for ch in range(c.NDM):
                                ins = t.matmul(ups,
                                               gubufU[:, ch * c.GW + ht * 128: ch * c.GW + ht * 128 + 128],
                                               xqT[:, ch * c.FSUB + n * N5: ch * c.FSUB + n * N5 + wdt],
                                               start=(ch == 0), stop=(ch == c.NDM - 1))
                        b.inc(ins, 'umm')
                    if (gidx + 1) % GPP == 0:
                        b.w(t, 'pmul', (GG + 1) * c.GU_GROUP)
                        b.w(t, 'wdl', 16 * c.D_PAIR * (P + 1))
                        for qti in range(c.QT_F):
                            fi = P * c.QT_F + qti
                            pp = pb if qti % 2 == 0 else pa
                            if qti >= 2:
                                b.w(t, 'flv', fi - 1)
                            elif qti == 1 and P >= 1:
                                pass
                            ins = None
                            ps_ = (P % 2) * c.D_PAIR * c.FSUB
                            for hc in range(c.D_PAIR):
                                for n in range(c.D // N5):
                                    ins = t.matmul(pp[:, n * N5:(n + 1) * N5],
                                                   pTb[:, ps_ + hc * c.FSUB + qti * 128:
                                                       ps_ + hc * c.FSUB + qti * 128 + 128],
                                                   wdbuf[:, hc * c.D + n * N5: hc * c.D + (n + 1) * N5],
                                                   start=(hc == 0), stop=(hc == c.D_PAIR - 1))
                            b.inc(ins, 'dmm')

        @block.vector
        def _(v):
            for fs in range(c.NFSUB):
                for gidx in range(c.NGRP):
                    GG = fs * c.NGRP + gidx
                    P = GG // GPP
                    ps_ = (P % 2) * c.D_PAIR * c.FSUB
                    within = (gidx % GPP) * c.GU_GROUP
                    for ht in range(c.GU_GROUP):
                        k = GG * c.GU_GROUP + ht
                        b.w(v, 'gmm', k + 1)
                        if k >= 2:
                            b.w(v, 'sil', k - 2 + 1)
                        if gidx == 0 and ht == 0:
                            b.w(v, 'ibl', 16 * (fs + 1))
                        ins = v.tensor_tensor(out=abuf[ht % 2][:, 0:c.FSUB],
                                              in0=pa[:, 0:c.FSUB], in1=ibc, op=ALU.mult)
                        b.inc(ins, 'aev')
                        b.w(v, 'umm', k + 1)
                        b.w(v, 'sil', k + 1)
                        ups = pa[:, 1024:1024 + c.FSUB]
                        ins = v.tensor_tensor(out=pTb[:, ps_ + (within + ht) * c.FSUB:
                                                     ps_ + (within + ht + 1) * c.FSUB],
                                              in0=abuf[ht % 2][:, 0:c.FSUB], in1=ups, op=ALU.mult)
                        b.inc(ins, 'pmul')
                    if (gidx + 1) % GPP == 0:
                        for qti in range(c.QT_F):
                            fi = P * c.QT_F + qti
                            pp = pb if qti % 2 == 0 else pa
                            b.w(v, 'dmm', fi + 1)
                            b.w(v, 'flw', 16 * fi)
                            ins = v.tensor_copy(out=foutr[qti % 2][:, :], in_=pp[:, 0:c.D])
                            b.inc(ins, 'flv')

        @block.scalar
        def _(s):
            for fs in range(c.NFSUB):
                for gidx in range(c.NGRP):
                    for ht in range(c.GU_GROUP):
                        k = (fs * c.NGRP + gidx) * c.GU_GROUP + ht
                        b.w(s, 'aev', k + 1)
                        ins = s.activation(out=abuf[ht % 2][:, 0:c.FSUB],
                                           in_=abuf[ht % 2][:, 0:c.FSUB], func=AF.Silu)
                        b.inc(ins, 'sil')

        @block.gpsimd
        def _(g):
            for fs in range(c.NFSUB):
                for pr_ in range(c.NPAIR):
                    P = fs * c.NPAIR + pr_
                    for qti in range(c.QT_F):
                        fi = P * c.QT_F + qti
                        b.w(g, 'flv', fi + 1)
                        qt = fs * c.QT_F + qti
                        if pr_ >= 1:
                            b.w(g, 'flw', 16 * ((P - 1) * c.QT_F + qti + 1))
                        ins = g.dma_start(out=dout_s[qt * 128:(qt + 1) * 128, :],
                                          in_=foutr[qti % 2][:, :],
                                          accum_op=(ALU.add if pr_ >= 1 else ALU.bypass))
                        b.inc(ins, 'flw', 16)
        FLW = b.cnt['flw']

        @block.sync
        def _(sy):
            b.w(sy, 'flw', FLW)
            b.w(sy, 'sil', b.cnt['sil'])
            for qt in range(c.NQT):
                if qt >= 1:
                    b.w(sy, 'ep3', qt)
                ins = sy.dma_start(out=foutr[0][:, :], in_=dout_s[qt * 128:(qt + 1) * 128, :])
                b.inc(ins, 'dor', 16)
                ins = sy.dma_start(out=h1t[:, :], in_=h1_s[qt * 128:(qt + 1) * 128, :])
                b.inc(ins, 'dor', 16)

        @block.vector
        def _(v):
            for qt in range(c.NQT):
                b.w(v, 'dor', 32 * (qt + 1))
                v.scalar_tensor_tensor(out=foutr[1][:, :], in0=foutr[0][:, :],
                                       scalar=invs_sb[:, qt:qt + 1], in1=h1t[:, :],
                                       op0=ALU.mult, op1=ALU.add)
                ins = v.reduce_sum(out=stats[:, 0:1], in_=foutr[1][:, :], axis=AX.X)
                b.inc(ins, 'ln3a')
                b.w(v, 'ln3b', qt + 1)
                ins = v.tensor_scalar_mul(out=stats[:, 2:3], in0=stats[:, 0:1], scalar1=float(1.0 / c.D))
                v.tensor_scalar_mul(out=stats[:, 3:4], in0=stats[:, 1:2], scalar1=float(1.0 / c.D))
                b.f(v, ins, 'vf')
                ins = v.tensor_tensor(out=stats[:, 4:5], in0=stats[:, 2:3], in1=stats[:, 2:3], op=ALU.mult)
                b.f(v, ins, 'vf')
                ins = v.tensor_tensor(out=stats[:, 4:5], in0=stats[:, 3:4], in1=stats[:, 4:5], op=ALU.subtract)
                b.inc(ins, 'ln3c')
                b.w(v, 'ln3d', qt + 1)
                ins = v.tensor_tensor(out=stats[:, 6:7], in0=stats[:, 2:3], in1=stats[:, 5:6], op=ALU.mult)
                b.f(v, ins, 'vf')
                ins = v.tensor_scalar_mul(out=stats[:, 6:7], in0=stats[:, 6:7], scalar1=-1.0)
                b.f(v, ins, 'vf')
                ins = v.tensor_copy(out=stats[:, 7:8], in_=stats[:, 5:6])
                b.inc(ins, 'ln3e')

        @block.scalar
        def _(s):
            for qt in range(c.NQT):
                b.w(s, 'ln3a', qt + 1)
                ins = s.activation(out=foutr[0][:, :], in_=foutr[1][:, :], func=AF.Square,
                                   accum_out=stats[:, 1:2])
                b.inc(ins, 'ln3b')
                b.w(s, 'ln3c', qt + 1)
                ins = s.activation(out=stats[:, 5:6], in_=stats[:, 4:5], func=AF.Ln,
                             bias=float(c.eps1), scale=1.0)
                b.f(s, ins, 'sf')
                ins = s.activation(out=stats[:, 5:6], in_=stats[:, 5:6], func=AF.Exp, scale=-0.5)
                b.inc(ins, 'ln3d')
                b.w(s, 'ln3e', qt + 1)
                ins = s.activation(out=foutr[1][:, :], in_=foutr[1][:, :], func=AF.Identity,
                                   bias=stats[:, 6:7], scale=stats[:, 7:8])
                b.inc(ins, 'ln3f')

        @block.gpsimd
        def _(g):
            for qt in range(c.NQT):
                b.w(g, 'ln3f', qt + 1)
                ins = g.dma_start(out=out[qt * 128:(qt + 1) * 128, :], in_=foutr[1][:, :])
                b.inc(ins, 'outw', 16)
                g.wait_ge(b.s('outw'), 16 * (qt + 1))
                ins = g.memset(stats[0:1, 1:2], 0.0)
                b.inc(ins, 'ep3')

    es.close()
    return nc


def kernel(L0_attn_hidden_states, L0_attn_Wq, L0_attn_Wk, L0_attn_Wv,
           L0_attn_Wo, L0_ffn_W_g, L0_ffn_W_u, L0_ffn_W_d):
    cfg = Cfg()
    x = np.ascontiguousarray(np.asarray(L0_attn_hidden_states, dtype=np.float32))
    B_, S_, D_ = x.shape
    nc = build_nc(cfg)
    ws = dict(wq=np.ascontiguousarray(L0_attn_Wq, dtype=np.float32),
              wk=np.ascontiguousarray(L0_attn_Wk, dtype=np.float32),
              wv=np.ascontiguousarray(L0_attn_Wv, dtype=np.float32),
              wo=np.ascontiguousarray(L0_attn_Wo, dtype=np.float32),
              wgT=np.ascontiguousarray(np.asarray(L0_ffn_W_g, dtype=np.float32).T),
              wuT=np.ascontiguousarray(np.asarray(L0_ffn_W_u, dtype=np.float32).T),
              wdT=np.ascontiguousarray(np.asarray(L0_ffn_W_d, dtype=np.float32).T))
    in_maps = []
    for core in range(8):
        bidx, r = core // 2, core % 2
        im = dict(ws)
        im["xbT"] = np.ascontiguousarray(x[bidx].T)
        im["xq"] = np.ascontiguousarray(x[bidx, r * cfg.QROWS:(r + 1) * cfg.QROWS])
        im["xqT_i"] = np.ascontiguousarray(x[bidx, r * cfg.QROWS:(r + 1) * cfg.QROWS].T)
        im["pos"] = np.array([[float(r * cfg.QROWS)]], dtype=np.float32)
        in_maps.append(im)
    res = run_bass_kernel_spmd(nc, in_maps, core_ids=list(range(8)))
    outp = np.empty((B_, S_, D_), dtype=np.float32)
    for core in range(8):
        bidx, r = core // 2, core % 2
        outp[bidx, r * cfg.QROWS:(r + 1) * cfg.QROWS] = res.results[core]["out"]
    return outp


# version 09
# speedup vs baseline: 1.1920x; 1.1400x over previous
import numpy as np
from contextlib import ExitStack
from dataclasses import dataclass

import concourse.bass as bass
import concourse.mybir as mybir
from concourse.bass_utils import run_bass_kernel_spmd

F32 = mybir.dt.float32
BF16 = mybir.dt.bfloat16
I32 = mybir.dt.int32
AF = mybir.ActivationFunctionType
ALU = mybir.AluOpType
AX = mybir.AxisListType
PI = float(np.pi)
TWO_PI = float(2.0 * np.pi)


@dataclass
class Cfg:
    SEQ: int = 4096
    QROWS: int = 2048
    D: int = 2048
    HQ: int = 16
    HKV: int = 4
    HD: int = 128
    FF: int = 8192
    SUBQ: int = 512
    TOKWIN: int = 512
    FSUB: int = 1024
    GU_GROUP: int = 4
    D_PAIR: int = 8
    theta: float = 10000.0
    eps1: float = 1e-5
    eps2: float = 1e-8
    SHIFT: float = 20.0
    dbg: bool = False

    @property
    def NDM(self): return self.D // 128
    @property
    def GRP(self): return self.HQ // self.HKV
    @property
    def NKC(self): return self.SEQ // 128
    @property
    def NSUB(self): return self.QROWS // self.SUBQ
    @property
    def CW(self): return self.GRP * self.SUBQ
    @property
    def NQT(self): return self.QROWS // 128
    @property
    def NHT(self): return self.FF // 128
    @property
    def NFSUB(self): return self.QROWS // self.FSUB
    @property
    def KVW(self): return self.HKV * self.HD
    @property
    def QTS(self): return self.SUBQ // 128
    @property
    def NGRP(self): return self.NHT // self.GU_GROUP
    @property
    def NPAIR(self): return self.NHT // self.D_PAIR
    @property
    def GW(self): return self.GU_GROUP * 128
    @property
    def QT_F(self): return self.FSUB // 128


def eo_ap(ap128, hd):
    ev = bass.AP(ap128.tensor, ap128.offset, [list(ap128.ap[0]), [2, hd // 2]])
    od = bass.AP(ap128.tensor, ap128.offset + 1, [list(ap128.ap[0]), [2, hd // 2]])
    return ev, od


class B:
    def __init__(self, nc, es):
        self.nc, self.es = nc, es
        self.sems, self.cnt = {}, {}

    def s(self, name):
        if name not in self.sems:
            self.sems[name] = self.es.enter_context(self.nc.semaphore(name))
            self.cnt[name] = 0
        return self.sems[name]

    def inc(self, ins, name, n=1):
        ins.then_inc(self.s(name), n)
        self.cnt[name] += n
        return self.cnt[name]

    def w(self, eng, name, val):
        if val > 0:
            eng.wait_ge(self.s(name), val)

    def f(self, eng, ins, name):
        n = self.inc(ins, name)
        eng.wait_ge(self.s(name), n)

    def wr(self, eng, name, ndma, item, depth, total):
        v = 16 * ndma * min(item + depth, total)
        if v > 0:
            eng.wait_ge(self.s(name), v)


def build_nc(cfg: Cfg):
    c = cfg
    nc = bass.Bass("TRN2", target_bir_lowering=False,
                   detect_race_conditions=False)
    es = ExitStack()
    b = B(nc, es)

    xbT = nc.dram_tensor("xbT", [c.D, c.SEQ], F32, kind="ExternalInput").ap()
    xq = nc.dram_tensor("xq", [c.QROWS, c.D], F32, kind="ExternalInput").ap()
    xqT_i = nc.dram_tensor("xqT_i", [c.D, c.QROWS], F32, kind="ExternalInput").ap()
    wq = nc.dram_tensor("wq", [c.D, c.D], F32, kind="ExternalInput").ap()
    wk = nc.dram_tensor("wk", [c.D, c.KVW], F32, kind="ExternalInput").ap()
    wv = nc.dram_tensor("wv", [c.D, c.KVW], F32, kind="ExternalInput").ap()
    wo = nc.dram_tensor("wo", [c.D, c.D], F32, kind="ExternalInput").ap()
    wgT = nc.dram_tensor("wgT", [c.D, c.FF], F32, kind="ExternalInput").ap()
    wuT = nc.dram_tensor("wuT", [c.D, c.FF], F32, kind="ExternalInput").ap()
    wdT = nc.dram_tensor("wdT", [c.FF, c.D], F32, kind="ExternalInput").ap()
    pos = nc.dram_tensor("pos", [1, 1], F32, kind="ExternalInput").ap()
    out = nc.dram_tensor("out", [c.QROWS, c.D], F32, kind="ExternalOutput").ap()

    xbfT_s = nc.dram_tensor("xbfT_s", [c.D, c.SEQ], BF16).ap()
    xqbfT_s = nc.dram_tensor("xqbfT_s", [c.D, c.QROWS], BF16).ap()
    qT_s = nc.dram_tensor("qT_s", [c.D, c.QROWS], BF16).ap()
    h1_s = nc.dram_tensor("h1_s", [c.QROWS, c.D], F32, kind="ExternalOutput" if True else "Internal").ap()
    xqn_s = nc.dram_tensor("xqn_s", [c.QROWS, c.D], BF16, kind="ExternalOutput").ap()
    wgT_s = nc.dram_tensor("wgT_s", [c.D, c.FF], BF16).ap()
    wuT_s = nc.dram_tensor("wuT_s", [c.D, c.FF], BF16).ap()
    wdT_s = nc.dram_tensor("wdT_s", [c.FF, c.D], BF16).ap()
    dout_s = nc.dram_tensor("dout_s", [c.QROWS, c.D], F32, kind="ExternalOutput").ap()
    invs_h = nc.dram_tensor("invs_h", [1, c.QROWS], F32).ap()

    szA = max(c.NDM * c.D, c.HKV * c.SEQ + c.NKC * c.KVW,
              2 * c.D_PAIR * c.FSUB + c.D_PAIR * c.D)
    arA = es.enter_context(nc.sbuf_tensor("arA", [128, szA], BF16))
    szB = max(2 * c.NDM * c.TOKWIN, 2 * c.HQ * c.SUBQ, c.NDM * c.FSUB)
    arB = es.enter_context(nc.sbuf_tensor("arB", [128, szB], BF16))
    szC = max(c.HQ * c.TOKWIN, 2 * c.NDM * c.KVW,
              2 * c.CW + c.HQ * c.SUBQ + 2 * c.D,
              2 * c.NDM * c.GW)
    arC = es.enter_context(nc.sbuf_tensor("arC", [128, szC], BF16))
    szD = max(c.SEQ + c.QROWS, 3 * c.D)
    arD = es.enter_context(nc.sbuf_tensor("arD", [128, szD], F32))
    PREW = c.D // 4
    arF = es.enter_context(nc.sbuf_tensor("arF", [128, 4 * PREW], BF16))
    sacc = es.enter_context(nc.sbuf_tensor("sacc", [128, max(c.CW, c.D)], F32))

    onesc = es.enter_context(nc.sbuf_tensor("onesc", [128, 1], F32))
    cvals = [float(-c.SHIFT), float(c.eps1), float(c.eps2)]
    consts = es.enter_context(nc.sbuf_tensor("consts", [128, len(cvals)], F32))
    for _i, _v in enumerate(cvals):
        nc.const_aps.register(F32, _v, consts[:, _i:_i + 1])
    ones1 = es.enter_context(nc.sbuf_tensor("ones1", [1, 128], F32))
    stats = es.enter_context(nc.sbuf_tensor("stats", [128, 8], F32))
    invs_sb = es.enter_context(nc.sbuf_tensor("invs_sb", [128, c.NQT], F32))
    statlog = es.enter_context(nc.sbuf_tensor("statlog", [128, 4 * c.NQT], F32))
    fcol = es.enter_context(nc.sbuf_tensor("fcol", [64, 4], F32))
    fcoli = es.enter_context(nc.sbuf_tensor("fcoli", [64, 1], I32))
    poscol = es.enter_context(nc.sbuf_tensor("poscol", [64, 1], F32))
    TW = min(512, c.SEQ, c.QROWS)
    twin_i = es.enter_context(nc.sbuf_tensor("twin_i", [64, TW], I32))
    twin_f = es.enter_context(nc.sbuf_tensor("twin_f", [128, max(2048, 2 * c.TOKWIN, c.CW)], F32))

    pa = es.enter_context(nc.psum_tensor("pa", [128, 2048], F32))
    pb = es.enter_context(nc.psum_tensor("pb", [128, 2048], F32))
    PSLOT = [pa[:, i * 512:(i + 1) * 512] for i in range(4)] + \
            [pb[:, i * 512:(i + 1) * 512] for i in range(4)]

    SCALE = float(1.0 / np.sqrt(c.HD))
    NW_Q = c.QROWS // c.TOKWIN
    NW_K = c.SEQ // c.TOKWIN
    TT_W = c.TOKWIN // 128
    NXQ = c.QROWS // 128
    NXB = c.SEQ // 128
    N5 = 512
    NCH = c.NKC
    PR = 2
    assert c.TOKWIN <= 512 and c.CW <= 2048 and c.SUBQ <= 512
    assert c.FSUB <= 2048 and c.D_PAIR % c.GU_GROUP == 0

    wq_perm = arA
    kT = arA[:, 0:c.HKV * c.SEQ]
    vT = arA[:, c.HKV * c.SEQ: c.HKV * c.SEQ + c.NKC * c.KVW]
    pTb = arA[:, 0:2 * c.D_PAIR * c.FSUB]
    wdbuf = arA[:, 2 * c.D_PAIR * c.FSUB: 2 * c.D_PAIR * c.FSUB + c.D_PAIR * c.D]

    xTw = arB
    qTsub = arB
    xqT = arB[:, 0:c.NDM * c.FSUB]

    qTst = arC[:, 0:c.HQ * c.TOKWIN]
    wkv = arC[:, 0:2 * c.NDM * c.KVW]
    pring = arC[:, 0:PR * c.CW]
    attnT = arC[:, PR * c.CW: PR * c.CW + c.HQ * c.SUBQ]
    woring = arC[:, PR * c.CW + c.HQ * c.SUBQ: PR * c.CW + c.HQ * c.SUBQ + 2 * c.D]
    gubufG = arC[:, 0:c.NDM * c.GW]
    gubufU = arC[:, c.NDM * c.GW:2 * c.NDM * c.GW]

    cosK = arD[0:64, 0:c.SEQ]
    sinK = arD[64:128, 0:c.SEQ]
    cosQ = arD[0:64, c.SEQ:c.SEQ + c.QROWS]
    sinQ = arD[64:128, c.SEQ:c.SEQ + c.QROWS]
    tbufA = arD[:, 0:c.D]
    tbufB = arD[:, c.D:2 * c.D]
    xorig = arD[:, 2 * c.D:3 * c.D]

    sacc_f = sacc[:, 0:c.D]
    qtempi = sacc[:, 0:c.D].bitcast(I32)[:, 0:c.D]
    abuf = [tbufA, tbufB]
    foutr = [xorig, sacc_f]
    h1t = tbufA
    xqstage = pring[:, 0:c.D]

    stage = arC[:, c.HQ * c.TOKWIN: c.HQ * c.TOKWIN + 2 * c.D]
    ibc = twin_f[:, 0:c.FSUB].bitcast(F32)[:, 0:c.FSUB] if False else twin_f[:, 0:c.FSUB]
    wpre = arF
    invrow = twin_f[0:1, 0:c.CW]
    s1v = twin_f[0:64, 0:c.TOKWIN]
    s2v = twin_f[0:64, c.TOKWIN:2 * c.TOKWIN]
    iotaF = twin_f[64:128, 0:TW]
    angT = twin_f[64:128, TW:2 * TW]
    scX = twin_f[64:128, 2 * TW:3 * TW]
    scY = twin_f[64:128, 3 * TW:4 * TW]

    with nc.Block() as block:
        @block.gpsimd
        def _(g):
            ins = g.iota(twin_i[:, :], [[1, TW]], base=0, channel_multiplier=0)
            b.inc(ins, 'tab')
            ins = g.iota(fcoli[:, :], [[0, 1]], base=0, channel_multiplier=1)
            b.inc(ins, 'tab')
            ins = g.dma_start(out=poscol[:, :], in_=pos[:, :].broadcast_to([64, 1]))
            b.inc(ins, 'posl', 16)

        @block.vector
        def _(v):
            v.memset(onesc[:, :], 1.0)
            for _i, _v in enumerate(cvals):
                v.memset(consts[:, _i:_i + 1], _v)
            ins = v.memset(ones1[:, :], 1.0)
            b.inc(ins, 'tab')
        TAB0 = b.cnt['tab']

        @block.gpsimd
        def _(g):
            def cast_stream(src, dst, ntiles, width, lsem, wsem):
                for t in range(ntiles):
                    slot = stage[:, (t % 2) * c.D:(t % 2) * c.D + width] \
                        if width <= c.D else stage[:, 0:width]
                    if width <= c.D:
                        if t >= 2:
                            b.w(g, wsem, 16 * (t - 1))
                    else:
                        if t >= 1:
                            b.w(g, wsem, 16 * t)
                    ins = g.dma_start(out=slot, in_=src[t * 128:(t + 1) * 128, :])
                    b.inc(ins, lsem, 16)
                    b.w(g, lsem, 16 * (t + 1))
                    ins = g.dma_start(out=dst[t * 128:(t + 1) * 128, :], in_=slot)
                    b.inc(ins, wsem, 16)
            cast_stream(xqT_i, xqbfT_s, c.NDM, c.QROWS, 'xql', 'xqw')
            cast_stream(xbT, xbfT_s, c.NDM, c.SEQ, 'xbl', 'xbw')
        XQW, XBW = b.cnt['xqw'], b.cnt['xbw']

        @block.gpsimd
        def _(g):
            b.w(g, 'xbw', XBW)
            for ch in range(c.NDM):
                if ch >= 2:
                    b.w(g, 'wqp', ch - 1)
                ins = g.dma_start(out=stage[:, (ch % 2) * c.D:(ch % 2) * c.D + c.D],
                                  in_=wq[ch * 128:(ch + 1) * 128, :])
                b.inc(ins, 'wql', 16)

        @block.vector
        def _(v):
            for ch in range(c.NDM):
                b.wr(v, 'wql', 1, ch, 2, c.NDM)
                slot = stage[:, (ch % 2) * c.D:(ch % 2) * c.D + c.D]
                ins = None
                for h in range(c.HQ):
                    ev, od = eo_ap(slot[:, h * c.HD:(h + 1) * c.HD], c.HD)
                    dst = wq_perm[:, ch * c.D + h * c.HD: ch * c.D + (h + 1) * c.HD]
                    ins = v.tensor_single_scalar(out=dst[:, 0:64], in_=ev, scalar=SCALE, op=ALU.mult)
                    ins = v.tensor_single_scalar(out=dst[:, 64:128], in_=od, scalar=SCALE, op=ALU.mult)
                b.inc(ins, 'wqp')
        WQP = b.cnt['wqp']

        @block.vector
        def _(v):
            b.w(v, 'tab', TAB0)
            v.tensor_copy(out=fcol[:, 0:1], in_=fcoli[:, :])
            ins = v.tensor_copy(out=iotaF, in_=twin_i[:, :])
            b.inc(ins, 'tab')
        @block.scalar
        def _(s):
            b.w(s, 'tab', TAB0 + 1)
            ins = s.activation(out=fcol[:, 1:2], in_=fcol[:, 0:1], func=AF.Exp,
                               scale=float(-np.log(c.theta) / 64.0))
            b.inc(ins, 'tab')
        TAB1 = b.cnt['tab']

        def emit_tables(name, nwin, cosbuf, sinbuf, use_pos, prior=None):
            sV, sS = name + 'V', name + 'S'
            @block.vector
            def _(v):
                b.w(v, 'tab', TAB1)
                if prior is not None:
                    b.w(v, prior, b.cnt[prior])
                if use_pos:
                    b.w(v, 'posl', 16)
                for w in range(nwin):
                    ang = angT
                    if w >= 1:
                        b.w(v, sS, 2 * w)
                    if use_pos:
                        v.tensor_scalar(out=ang, in0=iotaF, scalar1=float(w * TW),
                                        scalar2=poscol[:, 0:1], op0=ALU.add, op1=ALU.add)
                    else:
                        v.tensor_single_scalar(out=ang, in_=iotaF, scalar=float(w * TW), op=ALU.add)
                    v.tensor_scalar_mul(out=ang, in0=ang, scalar1=fcol[:, 1:2])
                    v.tensor_scalar(out=scX, in0=ang, scalar1=float(1.0 / TWO_PI),
                                    scalar2=384.0, op0=ALU.mult, op1=ALU.add)
                    ib = scY.bitcast(I32)[:, 0:TW]
                    v.tensor_copy(out=ib, in_=scX)
                    v.tensor_copy(out=scY, in_=ib)
                    ins = v.tensor_tensor(out=scX, in0=scX, in1=scY, op=ALU.subtract)
                    b.f(v, ins, 'vf')
                    v.tensor_single_scalar(out=scX, in_=scX, scalar=0.5, op=ALU.is_ge)
                    v.tensor_tensor(out=scX, in0=scX, in1=scY, op=ALU.add)
                    ins = v.tensor_scalar(out=scX, in0=scX, scalar1=-384.0, scalar2=float(-TWO_PI),
                                    op0=ALU.add, op1=ALU.mult)
                    b.f(v, ins, 'vf')
                    v.tensor_tensor(out=ang, in0=ang, in1=scX, op=ALU.add)
                    ins = v.tensor_single_scalar(out=ang, in_=ang, scalar=PI, op=ALU.min)
                    b.f(v, ins, 'vf')
                    ins = v.tensor_single_scalar(out=ang, in_=ang, scalar=-PI, op=ALU.max)
                    b.inc(ins, sV)
                    b.w(v, sS, 2 * w + 1)
                    ins = v.tensor_single_scalar(out=ang, in_=ang, scalar=float(PI / 2), op=ALU.add)
                    b.f(v, ins, 'vf')
                    ins = v.tensor_single_scalar(out=scX, in_=ang, scalar=PI, op=ALU.is_gt)
                    b.f(v, ins, 'vf')
                    ins = v.scalar_tensor_tensor(out=ang, in0=scX, scalar=float(-TWO_PI),
                                                 in1=ang, op0=ALU.mult, op1=ALU.add)
                    b.inc(ins, sV)
            @block.scalar
            def _(s):
                for w in range(nwin):
                    b.w(s, sV, 2 * w + 1)
                    ins = s.activation(out=sinbuf[:, w * TW:(w + 1) * TW], in_=angT, func=AF.Sin)
                    b.inc(ins, sS)
                    b.w(s, sV, 2 * w + 2)
                    ins = s.activation(out=cosbuf[:, w * TW:(w + 1) * TW], in_=angT, func=AF.Sin)
                    b.inc(ins, sS)
            return b.cnt[sS]

        emit_tables('tK', c.SEQ // TW, cosK, sinK, False)
        emit_tables('tQ', c.QROWS // TW, cosQ, sinQ, True, prior='tKS')
        TKS, TQS = b.cnt['tKS'], b.cnt['tQS']

        @block.sync
        def _(sy):
            b.w(sy, 'xqw', XQW)
            def load_win(sy, w):
                slot = xTw[:, (w % 2) * c.NDM * c.TOKWIN:]
                dst = bass.AP(slot.tensor, slot.offset,
                              [list(slot.ap[0]), [c.TOKWIN, c.NDM], [1, c.TOKWIN]])
                srcap = bass.AP(xqbfT_s.tensor, w * c.TOKWIN,
                                [[c.QROWS, 128], [128 * c.QROWS, c.NDM], [1, c.TOKWIN]])
                ins = sy.dma_start(out=dst, in_=srcap)
                b.inc(ins, 'xTq', 16)
            load_win(sy, 0)
            if NW_Q > 1:
                load_win(sy, 1)
            for w in range(NW_Q):
                b.w(sy, 'ropeQ', (w + 1) * c.HQ * 2)
                dst = bass.AP(qT_s.tensor, w * c.TOKWIN,
                              [[c.QROWS, 128], [128 * c.QROWS, c.HQ], [1, c.TOKWIN]])
                src2 = qTst[:, :].rearrange("p (h t) -> p h t", h=c.HQ)
                ins = sy.dma_start(out=dst, in_=src2)
                b.inc(ins, 'qtsD', 16)
                if w + 2 < NW_Q:
                    b.w(sy, 'qmm', (w + 1) * c.HQ)
                    load_win(sy, w + 2)

        @block.tensor
        def _(t):
            b.w(t, 'wqp', WQP)
            idx = 0
            for w in range(NW_Q):
                b.wr(t, 'xTq', 1, w, 2, NW_Q)
                for h in range(c.HQ):
                    slotp = PSLOT[idx % 8]
                    if idx >= 8:
                        b.w(t, 'ropeQ', (idx - 8 + 1) * 2)
                    ins = None
                    for ch in range(c.NDM):
                        ins = t.matmul(slotp[:, 0:c.TOKWIN],
                                       wq_perm[:, ch * c.D + h * c.HD: ch * c.D + (h + 1) * c.HD],
                                       xTw[:, (w % 2) * c.NDM * c.TOKWIN + ch * c.TOKWIN:
                                           (w % 2) * c.NDM * c.TOKWIN + (ch + 1) * c.TOKWIN],
                                       start=(ch == 0), stop=(ch == c.NDM - 1))
                    b.inc(ins, 'qmm')
                    idx += 1

        @block.vector
        def _(v):
            b.w(v, 'tQS', TQS)
            idx = 0
            for w in range(NW_Q):
                for h in range(c.HQ):
                    slotp = PSLOT[idx % 8]
                    b.w(v, 'qmm', idx + 1)
                    if h == 0:
                        b.w(v, 'qtsD', 16 * w)
                    pe, po = slotp[0:64, 0:c.TOKWIN], slotp[64:128, 0:c.TOKWIN]
                    cw_ = cosQ[:, w * c.TOKWIN:(w + 1) * c.TOKWIN]
                    sw_ = sinQ[:, w * c.TOKWIN:(w + 1) * c.TOKWIN]
                    de = qTst[0:64, h * c.TOKWIN:(h + 1) * c.TOKWIN]
                    do = qTst[64:128, h * c.TOKWIN:(h + 1) * c.TOKWIN]
                    v.tensor_tensor(out=s1v, in0=pe, in1=cw_, op=ALU.mult)
                    v.tensor_tensor(out=s2v, in0=po, in1=sw_, op=ALU.mult)
                    v.tensor_tensor(out=de, in0=s1v, in1=s2v, op=ALU.subtract)
                    v.tensor_tensor(out=s1v, in0=po, in1=cw_, op=ALU.mult)
                    v.tensor_tensor(out=s2v, in0=pe, in1=sw_, op=ALU.mult)
                    ins = v.tensor_tensor(out=do, in0=s1v, in1=s2v, op=ALU.add)
                    b.inc(ins, 'ropeQ', 2)
                    idx += 1

        QTSD = b.cnt['qtsD']

        @block.gpsimd
        def _(g):
            b.w(g, 'wqp', WQP)
            for ch in range(c.NDM):
                if ch >= 2:
                    b.w(g, 'wkp', ch - 1)
                ins = g.dma_start(out=stage[:, (ch % 2) * c.D:(ch % 2) * c.D + c.KVW],
                                  in_=wk[ch * 128:(ch + 1) * 128, :])
                b.inc(ins, 'wkl', 16)
            b.w(g, 'wkp', c.NDM)
            for ch in range(c.NDM):
                ins = g.dma_start(out=wkv[:, (c.NDM + ch) * c.KVW:(c.NDM + ch + 1) * c.KVW],
                                  in_=wv[ch * 128:(ch + 1) * 128, :])
                b.inc(ins, 'wvl', 16)
        WVL = b.cnt['wvl']

        @block.vector
        def _(v):
            b.w(v, 'qtsD', QTSD)
            for ch in range(c.NDM):
                b.wr(v, 'wkl', 1, ch, 2, c.NDM)
                slot = stage[:, (ch % 2) * c.D:(ch % 2) * c.D + c.KVW]
                ins = None
                for h in range(c.HKV):
                    ev, od = eo_ap(slot[:, h * c.HD:(h + 1) * c.HD], c.HD)
                    dst = wkv[:, ch * c.KVW + h * c.HD: ch * c.KVW + (h + 1) * c.HD]
                    ins = v.tensor_copy(out=dst[:, 0:64], in_=ev)
                    ins = v.tensor_copy(out=dst[:, 64:128], in_=od)
                b.inc(ins, 'wkp')
        WKP = b.cnt['wkp']

        @block.sync
        def _(sy):
            b.w(sy, 'xbw', XBW)
            b.w(sy, 'qmm', c.QROWS // c.TOKWIN * c.HQ)
            for w in range(NW_K):
                if w >= 2:
                    b.w(sy, 'vmm', (w - 1) * TT_W)
                slot = xTw[:, (w % 2) * c.NDM * c.TOKWIN:]
                dst = bass.AP(slot.tensor, slot.offset,
                              [list(slot.ap[0]), [c.TOKWIN, c.NDM], [1, c.TOKWIN]])
                srcap = bass.AP(xbfT_s.tensor, w * c.TOKWIN,
                                [[c.SEQ, 128], [128 * c.SEQ, c.NDM], [1, c.TOKWIN]])
                ins = sy.dma_start(out=dst, in_=srcap)
                b.inc(ins, 'xTb', 16)

        @block.tensor
        def _(t):
            b.w(t, 'wkp', WKP)
            b.w(t, 'wvl', WVL)
            b.w(t, 'ropeQ', b.cnt['ropeQ'])
            for w in range(NW_K):
                b.wr(t, 'xTb', 1, w, 2, NW_K)
                if w >= 2:
                    b.w(t, 'ropeK', (w - 1) * c.HKV * 2)
                if w >= 1:
                    b.w(t, 'vev', w * TT_W)
                for kv in range(c.HKV):
                    slotp = PSLOT[(w % 2) * 4 + kv]
                    ins = None
                    for ch in range(c.NDM):
                        ins = t.matmul(slotp[:, 0:c.TOKWIN],
                                       wkv[:, ch * c.KVW + kv * c.HD: ch * c.KVW + (kv + 1) * c.HD],
                                       xTw[:, (w % 2) * c.NDM * c.TOKWIN + ch * c.TOKWIN:
                                           (w % 2) * c.NDM * c.TOKWIN + (ch + 1) * c.TOKWIN],
                                       start=(ch == 0), stop=(ch == c.NDM - 1))
                    b.inc(ins, 'kmm')
                b.w(t, 'ropeK', w * c.HKV * 2)
                if w >= 2:
                    b.w(t, 'vev', (w - 1) * TT_W)
                for tt in range(TT_W):
                    slotp = PSLOT[(1 - (w % 2)) * 4 + tt % 4]
                    ins = None
                    for ch in range(c.NDM):
                        ins = t.matmul(slotp[:, 0:c.KVW],
                                       xTw[:, (w % 2) * c.NDM * c.TOKWIN + ch * c.TOKWIN + tt * 128:
                                           (w % 2) * c.NDM * c.TOKWIN + ch * c.TOKWIN + tt * 128 + 128],
                                       wkv[:, (c.NDM + ch) * c.KVW:(c.NDM + ch + 1) * c.KVW],
                                       start=(ch == 0), stop=(ch == c.NDM - 1))
                    b.inc(ins, 'vmm')

        @block.vector
        def _(v):
            b.w(v, 'tKS', TKS)
            for w in range(NW_K):
                for kv in range(c.HKV):
                    slotp = PSLOT[(w % 2) * 4 + kv]
                    b.w(v, 'kmm', w * c.HKV + kv + 1)
                    pe, po = slotp[0:64, 0:c.TOKWIN], slotp[64:128, 0:c.TOKWIN]
                    cw_ = cosK[:, w * c.TOKWIN:(w + 1) * c.TOKWIN]
                    sw_ = sinK[:, w * c.TOKWIN:(w + 1) * c.TOKWIN]
                    base = kv * c.SEQ + w * c.TOKWIN
                    de, do = kT[0:64, base:base + c.TOKWIN], kT[64:128, base:base + c.TOKWIN]
                    v.tensor_tensor(out=s1v, in0=pe, in1=cw_, op=ALU.mult)
                    v.tensor_tensor(out=s2v, in0=po, in1=sw_, op=ALU.mult)
                    v.tensor_tensor(out=de, in0=s1v, in1=s2v, op=ALU.subtract)
                    v.tensor_tensor(out=s1v, in0=po, in1=cw_, op=ALU.mult)
                    v.tensor_tensor(out=s2v, in0=pe, in1=sw_, op=ALU.mult)
                    ins = v.tensor_tensor(out=do, in0=s1v, in1=s2v, op=ALU.add)
                    b.inc(ins, 'ropeK', 2)
                for tt in range(TT_W):
                    slotp = PSLOT[(1 - (w % 2)) * 4 + tt % 4]
                    b.w(v, 'vmm', w * TT_W + tt + 1)
                    ins = v.tensor_copy(out=vT[:, (w * TT_W + tt) * c.KVW:(w * TT_W + tt + 1) * c.KVW],
                                        in_=slotp[:, 0:c.KVW])
                    b.inc(ins, 'vev')
        KMM_ALL, VEV_ALL, ROPEK_ALL = b.cnt['kmm'], b.cnt['vev'], b.cnt['ropeK']

        pre_jobs = []
        for (wsrc, wdst, rows, cols) in ((wgT, wgT_s, c.D, c.FF),
                                         (wuT, wuT_s, c.D, c.FF),
                                         (wdT, wdT_s, c.FF, c.D)):
            wcols = min(cols, PREW)
            for rtile in range(rows // 128):
                for piece in range(cols // wcols):
                    pre_jobs.append((wsrc, wdst, rtile, piece * wcols, wcols))
        NPRE = len(pre_jobs)
        pre_cur = {'ld': 0, 'sg': 0}

        def drain_pre_loads(g, n):
            j0 = pre_cur['ld']
            for j in range(j0, min(j0 + n, NPRE)):
                wsrc, wdst, rt, w0, wc = pre_jobs[j]
                slot = wpre[:, (j % 4) * PREW:(j % 4) * PREW + wc]
                if j >= 4:
                    b.w(g, 'prw', 16 * (j - 3))
                ins = g.dma_start(out=slot, in_=wsrc[rt * 128:(rt + 1) * 128, w0:w0 + wc])
                b.inc(ins, 'prl%d' % (j % 4), 16)
            pre_cur['ld'] = min(j0 + n, NPRE)

        def drain_pre_signs(s, n):
            j0 = pre_cur['sg']
            for j in range(j0, min(j0 + n, NPRE)):
                wsrc, wdst, rt, w0, wc = pre_jobs[j]
                slot = wpre[:, (j % 4) * PREW:(j % 4) * PREW + wc]
                b.w(s, 'prl%d' % (j % 4), 16 * (j // 4 + 1))
                ins = s.sign(out=slot, in_=slot)
                b.inc(ins, 'prs')
                b.w(s, 'prs', b.cnt['prs'])
                ins = s.dma_start(out=wdst[rt * 128:(rt + 1) * 128, w0:w0 + wc], in_=slot)
                b.inc(ins, 'prw', 16)
            pre_cur['sg'] = min(j0 + n, NPRE)
        N_LD_SUB = -(-NPRE // c.NSUB)
        N_SG_UNIT = -(-NPRE // (c.NSUB * c.HKV))

        @block.sync
        def _(sy):
            b.w(sy, 'kmm', KMM_ALL)
            b.w(sy, 'vmm', NW_K * TT_W)
            b.w(sy, 'qtsD', QTSD)
            for sub in range(c.NSUB):
                if sub >= 2:
                    b.w(sy, 'smm', (sub - 1) * c.HKV * NCH)
                slot = qTsub[:, (sub % 2) * c.HQ * c.SUBQ:]
                for h in range(c.HQ):
                    ins = sy.dma_start(
                        out=slot[:, h * c.SUBQ:(h + 1) * c.SUBQ],
                        in_=qT_s[h * 128:(h + 1) * 128, sub * c.SUBQ:(sub + 1) * c.SUBQ])
                    b.inc(ins, 'qsl', 16)

        @block.gpsimd
        def _(g):
            for sub in range(c.NSUB):
                drain_pre_loads(g, N_LD_SUB)
                for qp in range(c.QTS // 2):
                    for ch in range(c.HQ):
                        j = (sub * (c.QTS // 2) + qp) * c.HQ + ch
                        if j >= 2:
                            b.w(g, 'woc', j - 1)
                        slot = woring[:, (j % 2) * c.D:(j % 2) * c.D + c.D]
                        ins = g.dma_start(out=slot, in_=wo[ch * 128:(ch + 1) * 128, :])
                        b.inc(ins, 'wol0' if j % 2 == 0 else 'wol1', 16)
                    for qti in (qp * 2, qp * 2 + 1):
                        qt = sub * c.QTS + qti
                        if qt >= 1:
                            b.w(g, 'epi', qt)
                        ins = g.dma_start(out=xorig[:, :], in_=xq[qt * 128:(qt + 1) * 128, :])
                        b.inc(ins, 'xol', 16)
            drain_pre_loads(g, NPRE)

        @block.tensor
        def _(t):
            b.w(t, 'tab', TAB0)
            b.w(t, 'kmm', KMM_ALL)
            b.w(t, 'ropeK', ROPEK_ALL)
            b.w(t, 'vev', VEV_ALL)
            for sub in range(c.NSUB):
                b.wr(t, 'qsl', c.HQ, sub, 2, c.NSUB)
                if sub >= 1:
                    b.w(t, 'ln1a', sub * c.QTS)
                for kv in range(c.HKV):
                    u = sub * c.HKV + kv
                    qs = qTsub[:, (sub % 2) * c.HQ * c.SUBQ + kv * c.GRP * c.SUBQ:
                               (sub % 2) * c.HQ * c.SUBQ + (kv + 1) * c.GRP * c.SUBQ]
                    b.w(t, 'divd', u * c.GRP)
                    for i in range(NCH):
                        gi = u * NCH + i
                        if gi >= 1:
                            b.w(t, 'exp', gi)
                        ins = None
                        for n in range(c.CW // N5 if c.CW >= N5 else 1):
                            wdt = min(N5, c.CW)
                            ins = t.matmul(pa[:, n * N5:n * N5 + wdt],
                                           kT[:, kv * c.SEQ + i * 128: kv * c.SEQ + (i + 1) * 128],
                                           qs[:, n * N5:n * N5 + wdt],
                                           start=True, stop=True)
                        b.inc(ins, 'smm')
                        if i >= 1:
                            pi = (gi - 1) % PR
                            ins = None
                            for n in range(c.CW // N5 if c.CW >= N5 else 1):
                                wdt = min(N5, c.CW)
                                ins = t.matmul(pb[:, n * N5:n * N5 + wdt],
                                               vT[:, (i - 1) * c.KVW + kv * c.HD:
                                                  (i - 1) * c.KVW + (kv + 1) * c.HD],
                                               pring[:, pi * c.CW + n * N5: pi * c.CW + n * N5 + wdt],
                                               start=(i == 1), stop=False)
                            b.inc(ins, 'avm')
                    gi = u * NCH + NCH - 1
                    b.w(t, 'exp', gi + 1)
                    pi = gi % PR
                    ins = None
                    for n in range(c.CW // N5 if c.CW >= N5 else 1):
                        wdt = min(N5, c.CW)
                        ins = t.matmul(pb[:, n * N5:n * N5 + wdt],
                                       vT[:, (NCH - 1) * c.KVW + kv * c.HD:
                                          (NCH - 1) * c.KVW + (kv + 1) * c.HD],
                                       pring[:, pi * c.CW + n * N5: pi * c.CW + n * N5 + wdt],
                                       start=(NCH == 1), stop=True)
                    b.inc(ins, 'avm')
                    b.w(t, 'sadd', (u + 1) * NCH)
                    ins = None
                    for n in range(max(1, c.CW // N5)):
                        wdt = min(N5, c.CW)
                        ins = t.matmul(pa[0:1, n * N5:n * N5 + wdt],
                                       onesc[:, :], sacc[:, n * N5:n * N5 + wdt],
                                       start=True, stop=True)
                    b.inc(ins, 'srm')
                    b.w(t, 'rcp', u + 1)
                    for j in range(c.GRP):
                        if u * c.GRP + j >= 1:
                            b.w(t, 'divd', u * c.GRP + j)
                        ins = t.matmul(pa[:, 512:512 + c.SUBQ],
                                       ones1[:, :], invrow[:, j * c.SUBQ:(j + 1) * c.SUBQ],
                                       start=True, stop=True)
                        b.inc(ins, 'bcm')
                b.w(t, 'divd', (sub + 1) * c.HKV * c.GRP)
                for qp in range(c.QTS // 2):
                    if sub * (c.QTS // 2) + qp >= 1:
                        b.w(t, 'ln1a', (sub * (c.QTS // 2) + qp) * 2)
                    ins = None
                    for ch in range(c.HQ):
                        j = (sub * (c.QTS // 2) + qp) * c.HQ + ch
                        b.w(t, 'wol0' if j % 2 == 0 else 'wol1', 16 * (j // 2 + 1))
                        wslot = woring[:, (j % 2) * c.D:(j % 2) * c.D + c.D]
                        for half in range(2):
                            qti = qp * 2 + half
                            pp = pa if half == 0 else pb
                            for n in range(c.D // N5):
                                ins = t.matmul(pp[:, n * N5:(n + 1) * N5],
                                               attnT[:, ch * c.SUBQ + qti * 128: ch * c.SUBQ + qti * 128 + 128],
                                               wslot[:, n * N5:(n + 1) * N5],
                                               start=(ch == 0), stop=(ch == c.HQ - 1))
                        b.inc(ins, 'woc')

        @block.scalar
        def _(s):
            b.w(s, 'tab', TAB0)
            for sub in range(c.NSUB):
                for kv in range(c.HKV):
                    u = sub * c.HKV + kv
                    for i in range(NCH):
                        gi = u * NCH + i
                        b.w(s, 'smm', gi + 1)
                        if gi >= PR:
                            b.w(s, 'avm', gi - PR + 1)
                            b.w(s, 'sadd', gi - PR + 1)
                        pi = gi % PR
                        ins = s.activation(out=pring[:, pi * c.CW:(pi + 1) * c.CW],
                                           in_=pa[:, 0:c.CW], func=AF.Exp,
                                           bias=float(-c.SHIFT), scale=1.0)
                        b.inc(ins, 'exp')
                    drain_pre_signs(s, N_SG_UNIT)
                    b.w(s, 'srm', u + 1)
                    ins = s.copy(out=invrow[:, :], in_=pa[0:1, 0:c.CW])
                    b.inc(ins, 'srow')
                    if kv == 0:
                        b.w(s, 'ln1e', sub * c.QTS)
                    for j in range(c.GRP):
                        b.w(s, 'bcm', u * c.GRP + j + 1)
                        if u * c.GRP + j >= 1:
                            b.w(s, 'divd', u * c.GRP + j)
                        ins = s.copy(out=tbufA[:, 0:c.SUBQ], in_=pa[:, 512:512 + c.SUBQ])
                        b.inc(ins, 'bcc')
                for qti in range(c.QTS):
                    qt = sub * c.QTS + qti
                    b.w(s, 'ln1a', qt + 1)
                    ins = s.activation(out=tbufB[:, :], in_=tbufA[:, :], func=AF.Square,
                                       accum_out=stats[:, 1:2])
                    b.inc(ins, 'ln1b')
                    b.w(s, 'ln1c', qt + 1)
                    ins = s.activation(out=stats[:, 5:6], in_=stats[:, 4:5], func=AF.Ln,
                                 bias=float(c.eps1), scale=1.0)
                    b.f(s, ins, 'sf')
                    ins = s.activation(out=stats[:, 5:6], in_=stats[:, 5:6], func=AF.Exp,
                                       scale=-0.5)
                    b.inc(ins, 'ln1d')
                    b.w(s, 'ln1e', qt + 1)
                    ins = s.activation(out=tbufB[:, :], in_=tbufA[:, :], func=AF.Identity,
                                       bias=stats[:, 6:7], scale=stats[:, 7:8])
                    b.inc(ins, 'epi')
                    b.w(s, 'epi', b.cnt['epi'])
                    ins = s.dma_start(out=h1_s[qt * 128:(qt + 1) * 128, :], in_=tbufB[:, :])
                    b.inc(ins, 'h1w', 16)
            drain_pre_signs(s, NPRE)

        @block.vector
        def _(v):
            for sub in range(c.NSUB):
                for kv in range(c.HKV):
                    u = sub * c.HKV + kv
                    for i in range(NCH):
                        gi = u * NCH + i
                        b.w(v, 'exp', gi + 1)
                        pi = gi % PR
                        if i == 0:
                            ins = v.tensor_copy(out=sacc[:, 0:c.CW],
                                                in_=pring[:, pi * c.CW:(pi + 1) * c.CW])
                        else:
                            ins = v.tensor_tensor(out=sacc[:, 0:c.CW], in0=sacc[:, 0:c.CW],
                                                  in1=pring[:, pi * c.CW:(pi + 1) * c.CW],
                                                  op=ALU.add)
                        b.inc(ins, 'sadd')
                    b.w(v, 'srow', u + 1)
                    ins = v.reciprocal(out=invrow[:, :], in_=invrow[:, :])
                    b.inc(ins, 'rcp')
                    for j in range(c.GRP):
                        b.w(v, 'bcc', u * c.GRP + j + 1)
                        b.w(v, 'avm', (u + 1) * NCH)
                        h = kv * c.GRP + j
                        ins = v.tensor_tensor(
                            out=attnT[:, h * c.SUBQ:(h + 1) * c.SUBQ],
                            in0=pb[:, j * c.SUBQ:(j + 1) * c.SUBQ],
                            in1=tbufA[:, 0:c.SUBQ],
                            op=ALU.mult)
                        b.inc(ins, 'divd')
                for qti in range(c.QTS):
                    qt = sub * c.QTS + qti
                    pp = pa if qti % 2 == 0 else pb
                    b.w(v, 'woc', (sub * (c.QTS // 2) + qti // 2 + 1) * c.HQ)
                    b.w(v, 'xol', 16 * (qt + 1))
                    if qt >= 1:
                        b.w(v, 'epi', qt)
                    v.tensor_tensor(out=tbufA[:, :], in0=pp[:, 0:c.D], in1=xorig[:, :], op=ALU.add)
                    ins = v.reduce_sum(out=stats[:, 0:1], in_=tbufA[:, :], axis=AX.X)
                    b.inc(ins, 'ln1a')
                    b.w(v, 'ln1b', qt + 1)
                    ins = v.tensor_scalar_mul(out=stats[:, 2:3], in0=stats[:, 0:1], scalar1=float(1.0 / c.D))
                    v.tensor_scalar_mul(out=stats[:, 3:4], in0=stats[:, 1:2], scalar1=float(1.0 / c.D))
                    b.f(v, ins, 'vf')
                    ins = v.tensor_tensor(out=stats[:, 4:5], in0=stats[:, 2:3], in1=stats[:, 2:3], op=ALU.mult)
                    b.f(v, ins, 'vf')
                    ins = v.tensor_tensor(out=stats[:, 4:5], in0=stats[:, 3:4], in1=stats[:, 4:5], op=ALU.subtract)
                    b.inc(ins, 'ln1c')
                    if c.dbg:
                        v.tensor_copy(out=statlog[:, 4 * qt:4 * qt + 1], in_=stats[:, 0:1])
                        v.tensor_copy(out=statlog[:, 4 * qt + 1:4 * qt + 2], in_=stats[:, 1:2])
                        v.tensor_copy(out=statlog[:, 4 * qt + 2:4 * qt + 3], in_=stats[:, 4:5])
                    b.w(v, 'ln1d', qt + 1)
                    if c.dbg:
                        v.tensor_copy(out=statlog[:, 4 * qt + 3:4 * qt + 4], in_=stats[:, 5:6])
                    ins = v.tensor_tensor(out=stats[:, 6:7], in0=stats[:, 2:3], in1=stats[:, 5:6], op=ALU.mult)
                    b.f(v, ins, 'vf')
                    ins = v.tensor_scalar_mul(out=stats[:, 6:7], in0=stats[:, 6:7], scalar1=-1.0)
                    b.f(v, ins, 'vf')
                    ins = v.tensor_copy(out=stats[:, 7:8], in_=stats[:, 5:6])
                    b.inc(ins, 'ln1e')

        H1W = b.cnt['h1w']
        PRW = b.cnt['prw']
        assert pre_cur['ld'] == NPRE and pre_cur['sg'] == NPRE

        if c.dbg:
            dbg_st = nc.dram_tensor("dbg_st", [128, 4 * c.NQT], F32, kind="ExternalOutput").ap()

            @block.sync
            def _(sy):
                b.w(sy, 'ln1e', c.NQT)
                ins = sy.dma_start(out=dbg_st[:, :], in_=statlog[:, :])
                b.inc(ins, 'dbg3', 16)
                b.w(sy, 'dbg3', b.cnt['dbg3'])

            dbg_cos = nc.dram_tensor("dbg_cos", [128, c.SEQ + c.QROWS], F32, kind="ExternalOutput").ap()

            @block.sync
            def _(sy):
                b.w(sy, 'tKS', TKS)
                b.w(sy, 'tQS', TQS)
                ins = sy.dma_start(out=dbg_cos[:, 0:c.SEQ], in_=arD[:, 0:c.SEQ])
                b.inc(ins, 'dbg2', 16)
                ins = sy.dma_start(out=dbg_cos[:, c.SEQ:], in_=arD[:, c.SEQ:c.SEQ + c.QROWS])
                b.inc(ins, 'dbg2', 16)
                b.w(sy, 'dbg2', 32)

            dbg_attnT = nc.dram_tensor("dbg_attnT", [128, c.HQ * c.SUBQ], BF16, kind="ExternalOutput").ap()
            dbg_kT = nc.dram_tensor("dbg_kT", [128, c.HKV * c.SEQ], BF16, kind="ExternalOutput").ap()
            dbg_vT = nc.dram_tensor("dbg_vT", [128, c.NKC * c.KVW], BF16, kind="ExternalOutput").ap()
            dbg_sacc = nc.dram_tensor("dbg_sacc", [128, c.CW], F32, kind="ExternalOutput").ap()
            dbg_inv = nc.dram_tensor("dbg_inv", [1, c.CW], F32, kind="ExternalOutput").ap()
            dbg_p0 = nc.dram_tensor("dbg_p0", [128, c.CW], BF16, kind="ExternalOutput").ap()

            @block.sync
            def _(sy):
                b.w(sy, 'divd', c.GRP)
                b.w(sy, 'sadd', NCH)
                for ap_, dst in ((attnT, dbg_attnT), (kT, dbg_kT), (vT, dbg_vT),
                                 (sacc[:, 0:c.CW], dbg_sacc), (invrow, dbg_inv),
                                 (pring[:, 0:c.CW], dbg_p0)):
                    ins = sy.dma_start(out=dst[:, :], in_=ap_)
                    b.inc(ins, 'dbgw', 16)
                b.w(sy, 'dbgw', b.cnt['dbgw'])

        @block.sync
        def _(sy):
            b.w(sy, 'h1w', H1W)
            for qt in range(c.NQT):
                if qt >= 1:
                    b.w(sy, 'qzd', 16 * qt)
                ins = sy.dma_start(out=tbufA[:, :], in_=h1_s[qt * 128:(qt + 1) * 128, :])
                b.inc(ins, 'h1r', 16)

        @block.vector
        def _(v):
            b.w(v, 'epi', c.NQT)
            b.w(v, 'sadd', b.cnt['sadd'])
            b.w(v, 'exp', b.cnt['exp'])
            for qt in range(c.NQT):
                b.w(v, 'h1r', 16 * (qt + 1))
                ins = v.reduce_sum(out=stats[:, 0:1], in_=tbufA[:, :], axis=AX.X)
                b.inc(ins, 'ln2a')
                b.w(v, 'ln2b', qt + 1)
                ins = v.tensor_scalar_mul(out=stats[:, 2:3], in0=stats[:, 0:1], scalar1=float(1.0 / c.D))
                v.tensor_scalar_mul(out=stats[:, 3:4], in0=stats[:, 1:2], scalar1=float(1.0 / c.D))
                b.f(v, ins, 'vf')
                ins = v.tensor_tensor(out=stats[:, 4:5], in0=stats[:, 2:3], in1=stats[:, 2:3], op=ALU.mult)
                b.f(v, ins, 'vf')
                ins = v.tensor_tensor(out=stats[:, 4:5], in0=stats[:, 3:4], in1=stats[:, 4:5], op=ALU.subtract)
                b.inc(ins, 'ln2c')
                b.w(v, 'ln2d', qt + 1)
                ins = v.tensor_tensor(out=stats[:, 6:7], in0=stats[:, 2:3], in1=stats[:, 5:6], op=ALU.mult)
                b.f(v, ins, 'vf')
                ins = v.tensor_scalar_mul(out=stats[:, 6:7], in0=stats[:, 6:7], scalar1=-1.0)
                b.f(v, ins, 'vf')
                ins = v.tensor_copy(out=stats[:, 7:8], in_=stats[:, 5:6])
                b.inc(ins, 'ln2e')
                b.w(v, 'ln2f', qt + 1)
                ins = v.reduce_max(out=stats[:, 0:1], in_=tbufB[:, :], axis=AX.X,
                             apply_absolute_value=True)
                b.f(v, ins, 'vf')
                ins = v.reciprocal(out=stats[:, 1:2], in_=stats[:, 0:1])
                b.f(v, ins, 'vf')
                ins = v.tensor_scalar_mul(out=stats[:, 1:2], in0=stats[:, 1:2], scalar1=127.0)
                b.f(v, ins, 'vf')
                v.reciprocal(out=invs_sb[:, qt:qt + 1], in_=stats[:, 1:2])
                if qt >= 1:
                    b.w(v, 'qzd', 16 * qt)
                v.tensor_scalar(out=xorig[:, :], in0=tbufB[:, :], scalar1=stats[:, 1:2],
                                scalar2=384.0, op0=ALU.mult, op1=ALU.add)
                v.tensor_copy(out=qtempi[:, :], in_=xorig[:, :])
                v.tensor_copy(out=tbufB[:, :], in_=qtempi[:, :])
                ins = v.tensor_tensor(out=xorig[:, :], in0=xorig[:, :], in1=tbufB[:, :], op=ALU.subtract)
                b.f(v, ins, 'vf')
                v.tensor_single_scalar(out=xorig[:, :], in_=xorig[:, :], scalar=0.5, op=ALU.is_ge)
                ins = v.scalar_tensor_tensor(out=xqstage, in0=tbufB[:, :], scalar=-384.0,
                                             in1=xorig[:, :], op0=ALU.add, op1=ALU.add)
                b.inc(ins, 'qz')

        @block.scalar
        def _(s):
            for qt in range(c.NQT):
                b.w(s, 'ln2a', qt + 1)
                ins = s.activation(out=tbufB[:, :], in_=tbufA[:, :], func=AF.Square,
                                   accum_out=stats[:, 1:2])
                b.inc(ins, 'ln2b')
                b.w(s, 'ln2c', qt + 1)
                ins = s.activation(out=stats[:, 5:6], in_=stats[:, 4:5], func=AF.Ln,
                             bias=float(c.eps2), scale=1.0)
                b.f(s, ins, 'sf')
                ins = s.activation(out=stats[:, 5:6], in_=stats[:, 5:6], func=AF.Exp,
                                   scale=-0.5)
                b.inc(ins, 'ln2d')
                b.w(s, 'ln2e', qt + 1)
                ins = s.activation(out=tbufB[:, :], in_=tbufA[:, :], func=AF.Identity,
                                   bias=stats[:, 6:7], scale=stats[:, 7:8])
                b.inc(ins, 'ln2f')
                b.w(s, 'qz', qt + 1)
                ins = s.dma_start(out=xqn_s[qt * 128:(qt + 1) * 128, :], in_=xqstage)
                b.inc(ins, 'qzd', 16)
        QZD = b.cnt['qzd']

        GPP = c.D_PAIR // c.GU_GROUP

        @block.sync
        def _(sy):
            b.w(sy, 'qzd', QZD)
            b.w(sy, 'prw', PRW)
            dstT = bass.AP(invs_h.tensor, 0, [[1, 128], [128, c.NQT]])
            ins = sy.dma_start(out=dstT, in_=invs_sb[:, 0:c.NQT], allow_slow_non_contiguous=True)
            b.inc(ins, 'ivw', 16)
            b.w(sy, 'ivw', 16)
            for fs in range(c.NFSUB):
                if fs >= 1:
                    b.w(sy, 'pmul', fs * c.NGRP * c.GU_GROUP)
                ins = sy.dma_start(out=ibc, in_=invs_h[0:1, fs * c.FSUB:(fs + 1) * c.FSUB]
                                   .broadcast_to([128, c.FSUB]))
                b.inc(ins, 'ibl', 16)
                if fs >= 1:
                    b.w(sy, 'gmm', fs * c.NGRP * c.GU_GROUP)
                for ch in range(c.NDM):
                    ins = sy.dma_start(out=xqT[:, ch * c.FSUB:(ch + 1) * c.FSUB],
                                       in_=xqn_s[fs * c.FSUB:(fs + 1) * c.FSUB,
                                                 ch * 128:(ch + 1) * 128], transpose=True)
                    b.inc(ins, 'xqTl', 16)
                for pr_ in range(c.NPAIR):
                    P = fs * c.NPAIR + pr_
                    for gidx in range(pr_ * (c.D_PAIR // c.GU_GROUP),
                                      (pr_ + 1) * (c.D_PAIR // c.GU_GROUP)):
                        GG = fs * c.NGRP + gidx
                        b.w(sy, 'gmm', GG * c.GU_GROUP)
                        dstG = bass.AP(gubufG.tensor, gubufG.offset,
                                       [list(gubufG.ap[0]), [c.GW, c.NDM], [1, c.GW]])
                        srcG = bass.AP(wgT_s.tensor, gidx * c.GW,
                                       [[c.FF, 128], [128 * c.FF, c.NDM], [1, c.GW]])
                        ins = sy.dma_start(out=dstG, in_=srcG)
                        b.inc(ins, 'wgl', 16)
                        b.w(sy, 'umm', GG * c.GU_GROUP)
                        dstU = bass.AP(gubufU.tensor, gubufU.offset,
                                       [list(gubufU.ap[0]), [c.GW, c.NDM], [1, c.GW]])
                        srcU = bass.AP(wuT_s.tensor, gidx * c.GW,
                                       [[c.FF, 128], [128 * c.FF, c.NDM], [1, c.GW]])
                        ins = sy.dma_start(out=dstU, in_=srcU)
                        b.inc(ins, 'wul', 16)
                    b.w(sy, 'dmm', P * c.QT_F)
                    for hc in range(c.D_PAIR):
                        row = (pr_ * c.D_PAIR + hc) * 128
                        ins = sy.dma_start(out=wdbuf[:, hc * c.D:(hc + 1) * c.D],
                                           in_=wdT_s[row:row + 128, :])
                        b.inc(ins, 'wdl', 16)

        @block.tensor
        def _(t):
            b.w(t, 'woc', b.cnt['woc'])
            for fs in range(c.NFSUB):
                b.w(t, 'xqTl', 16 * c.NDM * (fs + 1))
                for gidx in range(c.NGRP):
                    GG = fs * c.NGRP + gidx
                    P = GG // GPP
                    b.w(t, 'wgl', 16 * (GG + 1))
                    b.w(t, 'wul', 16 * (GG + 1))
                    if gidx % GPP == 0:
                        b.w(t, 'flv', P * c.QT_F)
                    for ht in range(c.GU_GROUP):
                        k = GG * c.GU_GROUP + ht
                        if k >= 1:
                            b.w(t, 'aev', k)
                        ins = None
                        for ch in range(c.NDM):
                            for n in range(max(1, c.FSUB // N5)):
                                wdt = min(N5, c.FSUB)
                                ins = t.matmul(pa[:, n * N5:n * N5 + wdt],
                                               gubufG[:, ch * c.GW + ht * 128: ch * c.GW + ht * 128 + 128],
                                               xqT[:, ch * c.FSUB + n * N5: ch * c.FSUB + n * N5 + wdt],
                                               start=(ch == 0), stop=(ch == c.NDM - 1))
                        b.inc(ins, 'gmm')
                        if k >= 1:
                            b.w(t, 'pmul', k)
                        ins = None
                        UOFF = 1024
                        for n in range(max(1, c.FSUB // N5)):
                            wdt = min(N5, c.FSUB)
                            ups = pa[:, UOFF + n * N5: UOFF + n * N5 + wdt]
                            for ch in range(c.NDM):
                                ins = t.matmul(ups,
                                               gubufU[:, ch * c.GW + ht * 128: ch * c.GW + ht * 128 + 128],
                                               xqT[:, ch * c.FSUB + n * N5: ch * c.FSUB + n * N5 + wdt],
                                               start=(ch == 0), stop=(ch == c.NDM - 1))
                        b.inc(ins, 'umm')
                    if (gidx + 1) % GPP == 0:
                        b.w(t, 'pmul', (GG + 1) * c.GU_GROUP)
                        b.w(t, 'wdl', 16 * c.D_PAIR * (P + 1))
                        for qti in range(c.QT_F):
                            fi = P * c.QT_F + qti
                            pp = pb if qti % 2 == 0 else pa
                            if qti >= 2:
                                b.w(t, 'flv', fi - 1)
                            elif qti == 1 and P >= 1:
                                pass
                            ins = None
                            ps_ = (P % 2) * c.D_PAIR * c.FSUB
                            for hc in range(c.D_PAIR):
                                for n in range(c.D // N5):
                                    ins = t.matmul(pp[:, n * N5:(n + 1) * N5],
                                                   pTb[:, ps_ + hc * c.FSUB + qti * 128:
                                                       ps_ + hc * c.FSUB + qti * 128 + 128],
                                                   wdbuf[:, hc * c.D + n * N5: hc * c.D + (n + 1) * N5],
                                                   start=(hc == 0), stop=(hc == c.D_PAIR - 1))
                            b.inc(ins, 'dmm')

        @block.vector
        def _(v):
            for fs in range(c.NFSUB):
                for gidx in range(c.NGRP):
                    GG = fs * c.NGRP + gidx
                    P = GG // GPP
                    ps_ = (P % 2) * c.D_PAIR * c.FSUB
                    within = (gidx % GPP) * c.GU_GROUP
                    for ht in range(c.GU_GROUP):
                        k = GG * c.GU_GROUP + ht
                        b.w(v, 'gmm', k + 1)
                        if k >= 2:
                            b.w(v, 'sil', k - 2 + 1)
                        if gidx == 0 and ht == 0:
                            b.w(v, 'ibl', 16 * (fs + 1))
                        ins = v.tensor_tensor(out=abuf[ht % 2][:, 0:c.FSUB],
                                              in0=pa[:, 0:c.FSUB], in1=ibc, op=ALU.mult)
                        b.inc(ins, 'aev')
                        b.w(v, 'umm', k + 1)
                        b.w(v, 'sil', k + 1)
                        ups = pa[:, 1024:1024 + c.FSUB]
                        ins = v.tensor_tensor(out=pTb[:, ps_ + (within + ht) * c.FSUB:
                                                     ps_ + (within + ht + 1) * c.FSUB],
                                              in0=abuf[ht % 2][:, 0:c.FSUB], in1=ups, op=ALU.mult)
                        b.inc(ins, 'pmul')
                    if (gidx + 1) % GPP == 0:
                        for qti in range(c.QT_F):
                            fi = P * c.QT_F + qti
                            pp = pb if qti % 2 == 0 else pa
                            b.w(v, 'dmm', fi + 1)
                            b.w(v, 'flw', 16 * fi)
                            ins = v.tensor_copy(out=foutr[qti % 2][:, :], in_=pp[:, 0:c.D])
                            b.inc(ins, 'flv')

        @block.scalar
        def _(s):
            for fs in range(c.NFSUB):
                for gidx in range(c.NGRP):
                    for ht in range(c.GU_GROUP):
                        k = (fs * c.NGRP + gidx) * c.GU_GROUP + ht
                        b.w(s, 'aev', k + 1)
                        ins = s.activation(out=abuf[ht % 2][:, 0:c.FSUB],
                                           in_=abuf[ht % 2][:, 0:c.FSUB], func=AF.Silu)
                        b.inc(ins, 'sil')

        @block.gpsimd
        def _(g):
            for fs in range(c.NFSUB):
                for pr_ in range(c.NPAIR):
                    P = fs * c.NPAIR + pr_
                    for qti in range(c.QT_F):
                        fi = P * c.QT_F + qti
                        b.w(g, 'flv', fi + 1)
                        qt = fs * c.QT_F + qti
                        if pr_ >= 1:
                            b.w(g, 'flw', 16 * ((P - 1) * c.QT_F + qti + 1))
                        ins = g.dma_start(out=dout_s[qt * 128:(qt + 1) * 128, :],
                                          in_=foutr[qti % 2][:, :],
                                          accum_op=(ALU.add if pr_ >= 1 else ALU.bypass))
                        b.inc(ins, 'flw', 16)
        FLW = b.cnt['flw']

        @block.sync
        def _(sy):
            b.w(sy, 'flw', FLW)
            b.w(sy, 'sil', b.cnt['sil'])
            for qt in range(c.NQT):
                if qt >= 1:
                    b.w(sy, 'ep3', qt)
                ins = sy.dma_start(out=foutr[0][:, :], in_=dout_s[qt * 128:(qt + 1) * 128, :])
                b.inc(ins, 'dor', 16)
                ins = sy.dma_start(out=h1t[:, :], in_=h1_s[qt * 128:(qt + 1) * 128, :])
                b.inc(ins, 'dor', 16)

        @block.vector
        def _(v):
            for qt in range(c.NQT):
                b.w(v, 'dor', 32 * (qt + 1))
                v.scalar_tensor_tensor(out=foutr[1][:, :], in0=foutr[0][:, :],
                                       scalar=invs_sb[:, qt:qt + 1], in1=h1t[:, :],
                                       op0=ALU.mult, op1=ALU.add)
                ins = v.reduce_sum(out=stats[:, 0:1], in_=foutr[1][:, :], axis=AX.X)
                b.inc(ins, 'ln3a')
                b.w(v, 'ln3b', qt + 1)
                ins = v.tensor_scalar_mul(out=stats[:, 2:3], in0=stats[:, 0:1], scalar1=float(1.0 / c.D))
                v.tensor_scalar_mul(out=stats[:, 3:4], in0=stats[:, 1:2], scalar1=float(1.0 / c.D))
                b.f(v, ins, 'vf')
                ins = v.tensor_tensor(out=stats[:, 4:5], in0=stats[:, 2:3], in1=stats[:, 2:3], op=ALU.mult)
                b.f(v, ins, 'vf')
                ins = v.tensor_tensor(out=stats[:, 4:5], in0=stats[:, 3:4], in1=stats[:, 4:5], op=ALU.subtract)
                b.inc(ins, 'ln3c')
                b.w(v, 'ln3d', qt + 1)
                ins = v.tensor_tensor(out=stats[:, 6:7], in0=stats[:, 2:3], in1=stats[:, 5:6], op=ALU.mult)
                b.f(v, ins, 'vf')
                ins = v.tensor_scalar_mul(out=stats[:, 6:7], in0=stats[:, 6:7], scalar1=-1.0)
                b.f(v, ins, 'vf')
                ins = v.tensor_copy(out=stats[:, 7:8], in_=stats[:, 5:6])
                b.inc(ins, 'ln3e')

        @block.scalar
        def _(s):
            for qt in range(c.NQT):
                b.w(s, 'ln3a', qt + 1)
                ins = s.activation(out=foutr[0][:, :], in_=foutr[1][:, :], func=AF.Square,
                                   accum_out=stats[:, 1:2])
                b.inc(ins, 'ln3b')
                b.w(s, 'ln3c', qt + 1)
                ins = s.activation(out=stats[:, 5:6], in_=stats[:, 4:5], func=AF.Ln,
                             bias=float(c.eps1), scale=1.0)
                b.f(s, ins, 'sf')
                ins = s.activation(out=stats[:, 5:6], in_=stats[:, 5:6], func=AF.Exp, scale=-0.5)
                b.inc(ins, 'ln3d')
                b.w(s, 'ln3e', qt + 1)
                ins = s.activation(out=foutr[1][:, :], in_=foutr[1][:, :], func=AF.Identity,
                                   bias=stats[:, 6:7], scale=stats[:, 7:8])
                b.inc(ins, 'ln3f')

        @block.gpsimd
        def _(g):
            for qt in range(c.NQT):
                b.w(g, 'ln3f', qt + 1)
                ins = g.dma_start(out=out[qt * 128:(qt + 1) * 128, :], in_=foutr[1][:, :])
                b.inc(ins, 'outw', 16)
                g.wait_ge(b.s('outw'), 16 * (qt + 1))
                ins = g.memset(stats[0:1, 1:2], 0.0)
                b.inc(ins, 'ep3')

    es.close()
    return nc


def kernel(L0_attn_hidden_states, L0_attn_Wq, L0_attn_Wk, L0_attn_Wv,
           L0_attn_Wo, L0_ffn_W_g, L0_ffn_W_u, L0_ffn_W_d):
    cfg = Cfg()
    x = np.ascontiguousarray(np.asarray(L0_attn_hidden_states, dtype=np.float32))
    B_, S_, D_ = x.shape
    nc = build_nc(cfg)
    ws = dict(wq=np.ascontiguousarray(L0_attn_Wq, dtype=np.float32),
              wk=np.ascontiguousarray(L0_attn_Wk, dtype=np.float32),
              wv=np.ascontiguousarray(L0_attn_Wv, dtype=np.float32),
              wo=np.ascontiguousarray(L0_attn_Wo, dtype=np.float32),
              wgT=np.ascontiguousarray(np.asarray(L0_ffn_W_g, dtype=np.float32).T),
              wuT=np.ascontiguousarray(np.asarray(L0_ffn_W_u, dtype=np.float32).T),
              wdT=np.ascontiguousarray(np.asarray(L0_ffn_W_d, dtype=np.float32).T))
    in_maps = []
    for core in range(8):
        bidx, r = core // 2, core % 2
        im = dict(ws)
        im["xbT"] = np.ascontiguousarray(x[bidx].T)
        im["xq"] = np.ascontiguousarray(x[bidx, r * cfg.QROWS:(r + 1) * cfg.QROWS])
        im["xqT_i"] = np.ascontiguousarray(x[bidx, r * cfg.QROWS:(r + 1) * cfg.QROWS].T)
        im["pos"] = np.array([[float(r * cfg.QROWS)]], dtype=np.float32)
        in_maps.append(im)
    res = run_bass_kernel_spmd(nc, in_maps, core_ids=list(range(8)))
    outp = np.empty((B_, S_, D_), dtype=np.float32)
    for core in range(8):
        bidx, r = core // 2, core % 2
        outp[bidx, r * cfg.QROWS:(r + 1) * cfg.QROWS] = res.results[core]["out"]
    return outp


# version 10
# speedup vs baseline: 1.2310x; 1.0327x over previous
import numpy as np
from contextlib import ExitStack
from dataclasses import dataclass

import concourse.bass as bass
import concourse.mybir as mybir
from concourse.bass_utils import run_bass_kernel_spmd

F32 = mybir.dt.float32
BF16 = mybir.dt.bfloat16
I32 = mybir.dt.int32
AF = mybir.ActivationFunctionType
ALU = mybir.AluOpType
AX = mybir.AxisListType
PI = float(np.pi)
TWO_PI = float(2.0 * np.pi)


@dataclass
class Cfg:
    SEQ: int = 4096
    QROWS: int = 2048
    D: int = 2048
    HQ: int = 16
    HKV: int = 4
    HD: int = 128
    FF: int = 8192
    SUBQ: int = 512
    TOKWIN: int = 512
    FSUB: int = 1024
    GU_GROUP: int = 4
    D_PAIR: int = 8
    theta: float = 10000.0
    eps1: float = 1e-5
    eps2: float = 1e-8
    SHIFT: float = 20.0
    dbg: bool = False

    @property
    def NDM(self): return self.D // 128
    @property
    def GRP(self): return self.HQ // self.HKV
    @property
    def NKC(self): return self.SEQ // 128
    @property
    def NSUB(self): return self.QROWS // self.SUBQ
    GH: int = 2

    @property
    def CW(self): return self.GH * self.SUBQ
    @property
    def NQT(self): return self.QROWS // 128
    @property
    def NHT(self): return self.FF // 128
    @property
    def NFSUB(self): return self.QROWS // self.FSUB
    @property
    def KVW(self): return self.HKV * self.HD
    @property
    def QTS(self): return self.SUBQ // 128
    @property
    def NGRP(self): return self.NHT // self.GU_GROUP
    @property
    def NPAIR(self): return self.NHT // self.D_PAIR
    @property
    def GW(self): return self.GU_GROUP * 128
    @property
    def QT_F(self): return self.FSUB // 128


def eo_ap(ap128, hd):
    ev = bass.AP(ap128.tensor, ap128.offset, [list(ap128.ap[0]), [2, hd // 2]])
    od = bass.AP(ap128.tensor, ap128.offset + 1, [list(ap128.ap[0]), [2, hd // 2]])
    return ev, od


class B:
    def __init__(self, nc, es):
        self.nc, self.es = nc, es
        self.sems, self.cnt = {}, {}

    def s(self, name):
        if name not in self.sems:
            self.sems[name] = self.es.enter_context(self.nc.semaphore(name))
            self.cnt[name] = 0
        return self.sems[name]

    def inc(self, ins, name, n=1):
        ins.then_inc(self.s(name), n)
        self.cnt[name] += n
        return self.cnt[name]

    def w(self, eng, name, val):
        if val > 0:
            eng.wait_ge(self.s(name), val)

    def f(self, eng, ins, name):
        n = self.inc(ins, name)
        eng.wait_ge(self.s(name), n)

    def wr(self, eng, name, ndma, item, depth, total):
        v = 16 * ndma * min(item + depth, total)
        if v > 0:
            eng.wait_ge(self.s(name), v)


def build_nc(cfg: Cfg):
    c = cfg
    nc = bass.Bass("TRN2", target_bir_lowering=False,
                   detect_race_conditions=False)
    es = ExitStack()
    b = B(nc, es)

    xbT = nc.dram_tensor("xbT", [c.D, c.SEQ], F32, kind="ExternalInput").ap()
    xq = nc.dram_tensor("xq", [c.QROWS, c.D], F32, kind="ExternalInput").ap()
    xqT_i = nc.dram_tensor("xqT_i", [c.D, c.QROWS], F32, kind="ExternalInput").ap()
    wq = nc.dram_tensor("wq", [c.D, c.D], F32, kind="ExternalInput").ap()
    wk = nc.dram_tensor("wk", [c.D, c.KVW], F32, kind="ExternalInput").ap()
    wv = nc.dram_tensor("wv", [c.D, c.KVW], F32, kind="ExternalInput").ap()
    wo = nc.dram_tensor("wo", [c.D, c.D], F32, kind="ExternalInput").ap()
    wgT = nc.dram_tensor("wgT", [c.D, c.FF], F32, kind="ExternalInput").ap()
    wuT = nc.dram_tensor("wuT", [c.D, c.FF], F32, kind="ExternalInput").ap()
    wdT = nc.dram_tensor("wdT", [c.FF, c.D], F32, kind="ExternalInput").ap()
    pos = nc.dram_tensor("pos", [1, 1], F32, kind="ExternalInput").ap()
    out = nc.dram_tensor("out", [c.QROWS, c.D], F32, kind="ExternalOutput").ap()

    xbfT_s = nc.dram_tensor("xbfT_s", [c.D, c.SEQ], BF16).ap()
    xqbfT_s = nc.dram_tensor("xqbfT_s", [c.D, c.QROWS], BF16).ap()
    qT_s = nc.dram_tensor("qT_s", [c.D, c.QROWS], BF16).ap()
    h1_s = nc.dram_tensor("h1_s", [c.QROWS, c.D], F32, kind="ExternalOutput" if True else "Internal").ap()
    xqn_s = nc.dram_tensor("xqn_s", [c.QROWS, c.D], BF16, kind="ExternalOutput").ap()
    wgT_s = nc.dram_tensor("wgT_s", [c.D, c.FF], BF16).ap()
    wuT_s = nc.dram_tensor("wuT_s", [c.D, c.FF], BF16).ap()
    wdT_s = nc.dram_tensor("wdT_s", [c.FF, c.D], BF16).ap()
    dout_s = nc.dram_tensor("dout_s", [c.QROWS, c.D], F32, kind="ExternalOutput").ap()
    invs_h = nc.dram_tensor("invs_h", [1, c.QROWS], F32).ap()
    invh_s = nc.dram_tensor("invh_s", [64, 2048], F32).ap()

    szA = max(c.NDM * c.D, c.HKV * c.SEQ + c.NKC * c.KVW,
              2 * c.D_PAIR * c.FSUB + c.D_PAIR * c.D)
    arA = es.enter_context(nc.sbuf_tensor("arA", [128, szA], BF16))
    szB = max(2 * c.NDM * c.TOKWIN, 2 * c.HQ * c.SUBQ, c.NDM * c.FSUB)
    arB = es.enter_context(nc.sbuf_tensor("arB", [128, szB], BF16))
    szC = max(c.HQ * c.TOKWIN, 2 * c.NDM * c.KVW,
              4 * c.CW + c.HQ * c.SUBQ + 2 * c.D,
              2 * c.NDM * c.GW)
    arC = es.enter_context(nc.sbuf_tensor("arC", [128, szC], BF16))
    szD = max(c.SEQ + c.QROWS, 3 * c.D)
    arD = es.enter_context(nc.sbuf_tensor("arD", [128, szD], F32))
    PREW = c.D // 4
    arF = es.enter_context(nc.sbuf_tensor("arF", [128, 4 * PREW], BF16))
    sacc = es.enter_context(nc.sbuf_tensor("sacc", [128, max(c.CW, c.D)], F32))

    onesc = es.enter_context(nc.sbuf_tensor("onesc", [128, 1], F32))
    cvals = [float(-c.SHIFT), float(c.eps1), float(c.eps2)]
    consts = es.enter_context(nc.sbuf_tensor("consts", [128, len(cvals)], F32))
    for _i, _v in enumerate(cvals):
        nc.const_aps.register(F32, _v, consts[:, _i:_i + 1])
    ones1 = es.enter_context(nc.sbuf_tensor("ones1", [1, 128], F32))
    stats = es.enter_context(nc.sbuf_tensor("stats", [128, 8], F32))
    invs_sb = es.enter_context(nc.sbuf_tensor("invs_sb", [128, c.NQT], F32))
    statlog = es.enter_context(nc.sbuf_tensor("statlog", [128, 4 * c.NQT], F32))
    fcol = es.enter_context(nc.sbuf_tensor("fcol", [64, 4], F32))
    fcoli = es.enter_context(nc.sbuf_tensor("fcoli", [64, 1], I32))
    poscol = es.enter_context(nc.sbuf_tensor("poscol", [64, 1], F32))
    TW = min(512, c.SEQ, c.QROWS)
    twin_i = es.enter_context(nc.sbuf_tensor("twin_i", [64, TW], I32))
    twin_f = es.enter_context(nc.sbuf_tensor("twin_f", [128, max(2048, 2 * c.TOKWIN, c.CW)], F32))

    pa = es.enter_context(nc.psum_tensor("pa", [128, 2048], F32))
    pb = es.enter_context(nc.psum_tensor("pb", [128, 2048], F32))
    PSLOT = [pa[:, i * 512:(i + 1) * 512] for i in range(4)] + \
            [pb[:, i * 512:(i + 1) * 512] for i in range(4)]

    SCALE = float(1.0 / np.sqrt(c.HD))
    NW_Q = c.QROWS // c.TOKWIN
    NW_K = c.SEQ // c.TOKWIN
    TT_W = c.TOKWIN // 128
    NXQ = c.QROWS // 128
    NXB = c.SEQ // 128
    N5 = 512
    NCH = c.NKC
    PR = 4
    NUNIT = c.NSUB * c.HKV * (c.GRP // c.GH)
    UPS = c.HKV * (c.GRP // c.GH)
    assert c.TOKWIN <= 512 and 2 * c.CW <= 2048 and c.SUBQ <= 512 and NCH % 2 == 0
    assert c.FSUB <= 2048 and c.D_PAIR % c.GU_GROUP == 0

    wq_perm = arA
    kT = arA[:, 0:c.HKV * c.SEQ]
    vT = arA[:, c.HKV * c.SEQ: c.HKV * c.SEQ + c.NKC * c.KVW]
    pTb = arA[:, 0:2 * c.D_PAIR * c.FSUB]
    wdbuf = arA[:, 2 * c.D_PAIR * c.FSUB: 2 * c.D_PAIR * c.FSUB + c.D_PAIR * c.D]

    xTw = arB
    qTsub = arB
    xqT = arB[:, 0:c.NDM * c.FSUB]

    qTst = arC[:, 0:c.HQ * c.TOKWIN]
    wkv = arC[:, 0:2 * c.NDM * c.KVW]
    pring = arC[:, 0:PR * c.CW]
    attnT = arC[:, PR * c.CW: PR * c.CW + c.HQ * c.SUBQ]
    woring = arC[:, PR * c.CW + c.HQ * c.SUBQ: PR * c.CW + c.HQ * c.SUBQ + 2 * c.D]
    gubufG = arC[:, 0:c.NDM * c.GW]
    gubufU = arC[:, c.NDM * c.GW:2 * c.NDM * c.GW]

    cosK = arD[0:64, 0:c.SEQ]
    sinK = arD[64:128, 0:c.SEQ]
    cosQ = arD[0:64, c.SEQ:c.SEQ + c.QROWS]
    sinQ = arD[64:128, c.SEQ:c.SEQ + c.QROWS]
    tbufA = arD[:, 0:c.D]
    tbufB = arD[:, c.D:2 * c.D]
    xorig = arD[:, 2 * c.D:3 * c.D]

    sacc_f = sacc[:, 0:c.D]
    qtempi = sacc[:, 0:c.D].bitcast(I32)[:, 0:c.D]
    abuf = [tbufA, tbufB]
    foutr = [xorig, sacc_f]
    h1t = tbufA
    xqstage = pring[:, 0:c.D]

    stage = arC[:, c.HQ * c.TOKWIN: c.HQ * c.TOKWIN + 2 * c.D]
    ibc = twin_f[:, 0:c.FSUB].bitcast(F32)[:, 0:c.FSUB] if False else twin_f[:, 0:c.FSUB]
    wpre = arF
    bcslot = [twin_f[:, 0:c.CW], twin_f[:, c.CW:2 * c.CW]]
    s1v = twin_f[0:64, 0:c.TOKWIN]
    s2v = twin_f[0:64, c.TOKWIN:2 * c.TOKWIN]
    iotaF = twin_f[64:128, 0:TW]
    angT = twin_f[64:128, TW:2 * TW]
    scX = twin_f[64:128, 2 * TW:3 * TW]
    scY = twin_f[64:128, 3 * TW:4 * TW]

    with nc.Block() as block:
        @block.gpsimd
        def _(g):
            ins = g.iota(twin_i[:, :], [[1, TW]], base=0, channel_multiplier=0)
            b.inc(ins, 'tab')
            ins = g.iota(fcoli[:, :], [[0, 1]], base=0, channel_multiplier=1)
            b.inc(ins, 'tab')
            ins = g.dma_start(out=poscol[:, :], in_=pos[:, :].broadcast_to([64, 1]))
            b.inc(ins, 'posl', 16)

        @block.vector
        def _(v):
            v.memset(onesc[:, :], 1.0)
            for _i, _v in enumerate(cvals):
                v.memset(consts[:, _i:_i + 1], _v)
            ins = v.memset(ones1[:, :], 1.0)
            b.inc(ins, 'tab')
        TAB0 = b.cnt['tab']

        @block.gpsimd
        def _(g):
            def cast_stream(src, dst, ntiles, width, lsem, wsem):
                for t in range(ntiles):
                    slot = stage[:, (t % 2) * c.D:(t % 2) * c.D + width] \
                        if width <= c.D else stage[:, 0:width]
                    if width <= c.D:
                        if t >= 2:
                            b.w(g, wsem, 16 * (t - 1))
                    else:
                        if t >= 1:
                            b.w(g, wsem, 16 * t)
                    ins = g.dma_start(out=slot, in_=src[t * 128:(t + 1) * 128, :])
                    b.inc(ins, lsem, 16)
                    b.w(g, lsem, 16 * (t + 1))
                    ins = g.dma_start(out=dst[t * 128:(t + 1) * 128, :], in_=slot)
                    b.inc(ins, wsem, 16)
            cast_stream(xqT_i, xqbfT_s, c.NDM, c.QROWS, 'xql', 'xqw')
            cast_stream(xbT, xbfT_s, c.NDM, c.SEQ, 'xbl', 'xbw')
        XQW, XBW = b.cnt['xqw'], b.cnt['xbw']

        @block.gpsimd
        def _(g):
            b.w(g, 'xbw', XBW)
            for ch in range(c.NDM):
                if ch >= 2:
                    b.w(g, 'wqp', ch - 1)
                ins = g.dma_start(out=stage[:, (ch % 2) * c.D:(ch % 2) * c.D + c.D],
                                  in_=wq[ch * 128:(ch + 1) * 128, :])
                b.inc(ins, 'wql', 16)

        @block.vector
        def _(v):
            for ch in range(c.NDM):
                b.wr(v, 'wql', 1, ch, 2, c.NDM)
                slot = stage[:, (ch % 2) * c.D:(ch % 2) * c.D + c.D]
                ins = None
                for h in range(c.HQ):
                    ev, od = eo_ap(slot[:, h * c.HD:(h + 1) * c.HD], c.HD)
                    dst = wq_perm[:, ch * c.D + h * c.HD: ch * c.D + (h + 1) * c.HD]
                    ins = v.tensor_single_scalar(out=dst[:, 0:64], in_=ev, scalar=SCALE, op=ALU.mult)
                    ins = v.tensor_single_scalar(out=dst[:, 64:128], in_=od, scalar=SCALE, op=ALU.mult)
                b.inc(ins, 'wqp')
        WQP = b.cnt['wqp']

        @block.vector
        def _(v):
            b.w(v, 'tab', TAB0)
            v.tensor_copy(out=fcol[:, 0:1], in_=fcoli[:, :])
            ins = v.tensor_copy(out=iotaF, in_=twin_i[:, :])
            b.inc(ins, 'tab')
        @block.scalar
        def _(s):
            b.w(s, 'tab', TAB0 + 1)
            ins = s.activation(out=fcol[:, 1:2], in_=fcol[:, 0:1], func=AF.Exp,
                               scale=float(-np.log(c.theta) / 64.0))
            b.inc(ins, 'tab')
        TAB1 = b.cnt['tab']

        def emit_tables(name, nwin, cosbuf, sinbuf, use_pos, prior=None):
            sV, sS = name + 'V', name + 'S'
            @block.vector
            def _(v):
                b.w(v, 'tab', TAB1)
                if prior is not None:
                    b.w(v, prior, b.cnt[prior])
                if use_pos:
                    b.w(v, 'posl', 16)
                for w in range(nwin):
                    ang = angT
                    if w >= 1:
                        b.w(v, sS, 2 * w)
                    if use_pos:
                        v.tensor_scalar(out=ang, in0=iotaF, scalar1=float(w * TW),
                                        scalar2=poscol[:, 0:1], op0=ALU.add, op1=ALU.add)
                    else:
                        v.tensor_single_scalar(out=ang, in_=iotaF, scalar=float(w * TW), op=ALU.add)
                    v.tensor_scalar_mul(out=ang, in0=ang, scalar1=fcol[:, 1:2])
                    v.tensor_scalar(out=scX, in0=ang, scalar1=float(1.0 / TWO_PI),
                                    scalar2=384.0, op0=ALU.mult, op1=ALU.add)
                    ib = scY.bitcast(I32)[:, 0:TW]
                    v.tensor_copy(out=ib, in_=scX)
                    v.tensor_copy(out=scY, in_=ib)
                    ins = v.tensor_tensor(out=scX, in0=scX, in1=scY, op=ALU.subtract)
                    b.f(v, ins, 'vf')
                    v.tensor_single_scalar(out=scX, in_=scX, scalar=0.5, op=ALU.is_ge)
                    v.tensor_tensor(out=scX, in0=scX, in1=scY, op=ALU.add)
                    ins = v.tensor_scalar(out=scX, in0=scX, scalar1=-384.0, scalar2=float(-TWO_PI),
                                    op0=ALU.add, op1=ALU.mult)
                    b.f(v, ins, 'vf')
                    v.tensor_tensor(out=ang, in0=ang, in1=scX, op=ALU.add)
                    ins = v.tensor_single_scalar(out=ang, in_=ang, scalar=PI, op=ALU.min)
                    b.f(v, ins, 'vf')
                    ins = v.tensor_single_scalar(out=ang, in_=ang, scalar=-PI, op=ALU.max)
                    b.inc(ins, sV)
                    b.w(v, sS, 2 * w + 1)
                    ins = v.tensor_single_scalar(out=ang, in_=ang, scalar=float(PI / 2), op=ALU.add)
                    b.f(v, ins, 'vf')
                    ins = v.tensor_single_scalar(out=scX, in_=ang, scalar=PI, op=ALU.is_gt)
                    b.f(v, ins, 'vf')
                    ins = v.scalar_tensor_tensor(out=ang, in0=scX, scalar=float(-TWO_PI),
                                                 in1=ang, op0=ALU.mult, op1=ALU.add)
                    b.inc(ins, sV)
            @block.scalar
            def _(s):
                for w in range(nwin):
                    b.w(s, sV, 2 * w + 1)
                    ins = s.activation(out=sinbuf[:, w * TW:(w + 1) * TW], in_=angT, func=AF.Sin)
                    b.inc(ins, sS)
                    b.w(s, sV, 2 * w + 2)
                    ins = s.activation(out=cosbuf[:, w * TW:(w + 1) * TW], in_=angT, func=AF.Sin)
                    b.inc(ins, sS)
            return b.cnt[sS]

        emit_tables('tK', c.SEQ // TW, cosK, sinK, False)
        emit_tables('tQ', c.QROWS // TW, cosQ, sinQ, True, prior='tKS')
        TKS, TQS = b.cnt['tKS'], b.cnt['tQS']

        @block.sync
        def _(sy):
            b.w(sy, 'xqw', XQW)
            def load_win(sy, w):
                slot = xTw[:, (w % 2) * c.NDM * c.TOKWIN:]
                dst = bass.AP(slot.tensor, slot.offset,
                              [list(slot.ap[0]), [c.TOKWIN, c.NDM], [1, c.TOKWIN]])
                srcap = bass.AP(xqbfT_s.tensor, w * c.TOKWIN,
                                [[c.QROWS, 128], [128 * c.QROWS, c.NDM], [1, c.TOKWIN]])
                ins = sy.dma_start(out=dst, in_=srcap)
                b.inc(ins, 'xTq', 16)
            load_win(sy, 0)
            if NW_Q > 1:
                load_win(sy, 1)
            for w in range(NW_Q):
                b.w(sy, 'ropeQ', (w + 1) * c.HQ * 2)
                dst = bass.AP(qT_s.tensor, w * c.TOKWIN,
                              [[c.QROWS, 128], [128 * c.QROWS, c.HQ], [1, c.TOKWIN]])
                src2 = qTst[:, :].rearrange("p (h t) -> p h t", h=c.HQ)
                ins = sy.dma_start(out=dst, in_=src2)
                b.inc(ins, 'qtsD', 16)
                if w + 2 < NW_Q:
                    b.w(sy, 'qmm', (w + 1) * c.HQ)
                    load_win(sy, w + 2)

        @block.tensor
        def _(t):
            b.w(t, 'wqp', WQP)
            idx = 0
            for w in range(NW_Q):
                b.wr(t, 'xTq', 1, w, 2, NW_Q)
                for h in range(c.HQ):
                    slotp = PSLOT[idx % 8]
                    if idx >= 8:
                        b.w(t, 'ropeQ', (idx - 8 + 1) * 2)
                    ins = None
                    for ch in range(c.NDM):
                        ins = t.matmul(slotp[:, 0:c.TOKWIN],
                                       wq_perm[:, ch * c.D + h * c.HD: ch * c.D + (h + 1) * c.HD],
                                       xTw[:, (w % 2) * c.NDM * c.TOKWIN + ch * c.TOKWIN:
                                           (w % 2) * c.NDM * c.TOKWIN + (ch + 1) * c.TOKWIN],
                                       start=(ch == 0), stop=(ch == c.NDM - 1))
                    b.inc(ins, 'qmm')
                    idx += 1

        @block.vector
        def _(v):
            b.w(v, 'tQS', TQS)
            idx = 0
            for w in range(NW_Q):
                for h in range(c.HQ):
                    slotp = PSLOT[idx % 8]
                    b.w(v, 'qmm', idx + 1)
                    if h == 0:
                        b.w(v, 'qtsD', 16 * w)
                    pe, po = slotp[0:64, 0:c.TOKWIN], slotp[64:128, 0:c.TOKWIN]
                    cw_ = cosQ[:, w * c.TOKWIN:(w + 1) * c.TOKWIN]
                    sw_ = sinQ[:, w * c.TOKWIN:(w + 1) * c.TOKWIN]
                    de = qTst[0:64, h * c.TOKWIN:(h + 1) * c.TOKWIN]
                    do = qTst[64:128, h * c.TOKWIN:(h + 1) * c.TOKWIN]
                    v.tensor_tensor(out=s1v, in0=pe, in1=cw_, op=ALU.mult)
                    v.tensor_tensor(out=s2v, in0=po, in1=sw_, op=ALU.mult)
                    v.tensor_tensor(out=de, in0=s1v, in1=s2v, op=ALU.subtract)
                    v.tensor_tensor(out=s1v, in0=po, in1=cw_, op=ALU.mult)
                    v.tensor_tensor(out=s2v, in0=pe, in1=sw_, op=ALU.mult)
                    ins = v.tensor_tensor(out=do, in0=s1v, in1=s2v, op=ALU.add)
                    b.inc(ins, 'ropeQ', 2)
                    idx += 1

        QTSD = b.cnt['qtsD']

        @block.gpsimd
        def _(g):
            b.w(g, 'wqp', WQP)
            for ch in range(c.NDM):
                if ch >= 2:
                    b.w(g, 'wkp', ch - 1)
                ins = g.dma_start(out=stage[:, (ch % 2) * c.D:(ch % 2) * c.D + c.KVW],
                                  in_=wk[ch * 128:(ch + 1) * 128, :])
                b.inc(ins, 'wkl', 16)
            b.w(g, 'wkp', c.NDM)
            for ch in range(c.NDM):
                ins = g.dma_start(out=wkv[:, (c.NDM + ch) * c.KVW:(c.NDM + ch + 1) * c.KVW],
                                  in_=wv[ch * 128:(ch + 1) * 128, :])
                b.inc(ins, 'wvl', 16)
        WVL = b.cnt['wvl']

        @block.vector
        def _(v):
            b.w(v, 'qtsD', QTSD)
            for ch in range(c.NDM):
                b.wr(v, 'wkl', 1, ch, 2, c.NDM)
                slot = stage[:, (ch % 2) * c.D:(ch % 2) * c.D + c.KVW]
                ins = None
                for h in range(c.HKV):
                    ev, od = eo_ap(slot[:, h * c.HD:(h + 1) * c.HD], c.HD)
                    dst = wkv[:, ch * c.KVW + h * c.HD: ch * c.KVW + (h + 1) * c.HD]
                    ins = v.tensor_copy(out=dst[:, 0:64], in_=ev)
                    ins = v.tensor_copy(out=dst[:, 64:128], in_=od)
                b.inc(ins, 'wkp')
        WKP = b.cnt['wkp']

        @block.sync
        def _(sy):
            b.w(sy, 'xbw', XBW)
            b.w(sy, 'qmm', c.QROWS // c.TOKWIN * c.HQ)
            for w in range(NW_K):
                if w >= 2:
                    b.w(sy, 'vmm', (w - 1) * TT_W)
                slot = xTw[:, (w % 2) * c.NDM * c.TOKWIN:]
                dst = bass.AP(slot.tensor, slot.offset,
                              [list(slot.ap[0]), [c.TOKWIN, c.NDM], [1, c.TOKWIN]])
                srcap = bass.AP(xbfT_s.tensor, w * c.TOKWIN,
                                [[c.SEQ, 128], [128 * c.SEQ, c.NDM], [1, c.TOKWIN]])
                ins = sy.dma_start(out=dst, in_=srcap)
                b.inc(ins, 'xTb', 16)

        @block.tensor
        def _(t):
            b.w(t, 'wkp', WKP)
            b.w(t, 'wvl', WVL)
            b.w(t, 'ropeQ', b.cnt['ropeQ'])
            for w in range(NW_K):
                b.wr(t, 'xTb', 1, w, 2, NW_K)
                if w >= 2:
                    b.w(t, 'ropeK', (w - 1) * c.HKV * 2)
                if w >= 1:
                    b.w(t, 'vev', w * TT_W)
                for kv in range(c.HKV):
                    slotp = PSLOT[(w % 2) * 4 + kv]
                    ins = None
                    for ch in range(c.NDM):
                        ins = t.matmul(slotp[:, 0:c.TOKWIN],
                                       wkv[:, ch * c.KVW + kv * c.HD: ch * c.KVW + (kv + 1) * c.HD],
                                       xTw[:, (w % 2) * c.NDM * c.TOKWIN + ch * c.TOKWIN:
                                           (w % 2) * c.NDM * c.TOKWIN + (ch + 1) * c.TOKWIN],
                                       start=(ch == 0), stop=(ch == c.NDM - 1))
                    b.inc(ins, 'kmm')
                b.w(t, 'ropeK', w * c.HKV * 2)
                if w >= 2:
                    b.w(t, 'vev', (w - 1) * TT_W)
                for tt in range(TT_W):
                    slotp = PSLOT[(1 - (w % 2)) * 4 + tt % 4]
                    ins = None
                    for ch in range(c.NDM):
                        ins = t.matmul(slotp[:, 0:c.KVW],
                                       xTw[:, (w % 2) * c.NDM * c.TOKWIN + ch * c.TOKWIN + tt * 128:
                                           (w % 2) * c.NDM * c.TOKWIN + ch * c.TOKWIN + tt * 128 + 128],
                                       wkv[:, (c.NDM + ch) * c.KVW:(c.NDM + ch + 1) * c.KVW],
                                       start=(ch == 0), stop=(ch == c.NDM - 1))
                    b.inc(ins, 'vmm')

        @block.vector
        def _(v):
            b.w(v, 'tKS', TKS)
            for w in range(NW_K):
                for kv in range(c.HKV):
                    slotp = PSLOT[(w % 2) * 4 + kv]
                    b.w(v, 'kmm', w * c.HKV + kv + 1)
                    pe, po = slotp[0:64, 0:c.TOKWIN], slotp[64:128, 0:c.TOKWIN]
                    cw_ = cosK[:, w * c.TOKWIN:(w + 1) * c.TOKWIN]
                    sw_ = sinK[:, w * c.TOKWIN:(w + 1) * c.TOKWIN]
                    base = kv * c.SEQ + w * c.TOKWIN
                    de, do = kT[0:64, base:base + c.TOKWIN], kT[64:128, base:base + c.TOKWIN]
                    v.tensor_tensor(out=s1v, in0=pe, in1=cw_, op=ALU.mult)
                    v.tensor_tensor(out=s2v, in0=po, in1=sw_, op=ALU.mult)
                    v.tensor_tensor(out=de, in0=s1v, in1=s2v, op=ALU.subtract)
                    v.tensor_tensor(out=s1v, in0=po, in1=cw_, op=ALU.mult)
                    v.tensor_tensor(out=s2v, in0=pe, in1=sw_, op=ALU.mult)
                    ins = v.tensor_tensor(out=do, in0=s1v, in1=s2v, op=ALU.add)
                    b.inc(ins, 'ropeK', 2)
                for tt in range(TT_W):
                    slotp = PSLOT[(1 - (w % 2)) * 4 + tt % 4]
                    b.w(v, 'vmm', w * TT_W + tt + 1)
                    ins = v.tensor_copy(out=vT[:, (w * TT_W + tt) * c.KVW:(w * TT_W + tt + 1) * c.KVW],
                                        in_=slotp[:, 0:c.KVW])
                    b.inc(ins, 'vev')
        KMM_ALL, VEV_ALL, ROPEK_ALL = b.cnt['kmm'], b.cnt['vev'], b.cnt['ropeK']

        pre_jobs = []
        for (wsrc, wdst, rows, cols) in ((wgT, wgT_s, c.D, c.FF),
                                         (wuT, wuT_s, c.D, c.FF),
                                         (wdT, wdT_s, c.FF, c.D)):
            wcols = min(cols, PREW)
            for rtile in range(rows // 128):
                for piece in range(cols // wcols):
                    pre_jobs.append((wsrc, wdst, rtile, piece * wcols, wcols))
        NPRE = len(pre_jobs)
        pre_cur = {'ld': 0, 'sg': 0}

        def drain_pre_loads(g, n):
            j0 = pre_cur['ld']
            for j in range(j0, min(j0 + n, NPRE)):
                wsrc, wdst, rt, w0, wc = pre_jobs[j]
                slot = wpre[:, (j % 4) * PREW:(j % 4) * PREW + wc]
                if j >= 4:
                    b.w(g, 'prw', 16 * (j - 3))
                ins = g.dma_start(out=slot, in_=wsrc[rt * 128:(rt + 1) * 128, w0:w0 + wc])
                b.inc(ins, 'prl%d' % (j % 4), 16)
            pre_cur['ld'] = min(j0 + n, NPRE)

        def drain_pre_signs(s, n):
            j0 = pre_cur['sg']
            for j in range(j0, min(j0 + n, NPRE)):
                wsrc, wdst, rt, w0, wc = pre_jobs[j]
                slot = wpre[:, (j % 4) * PREW:(j % 4) * PREW + wc]
                b.w(s, 'prl%d' % (j % 4), 16 * (j // 4 + 1))
                ins = s.sign(out=slot, in_=slot)
                b.inc(ins, 'prs')
                b.w(s, 'prs', b.cnt['prs'])
                ins = s.dma_start(out=wdst[rt * 128:(rt + 1) * 128, w0:w0 + wc], in_=slot)
                b.inc(ins, 'prw', 16)
            pre_cur['sg'] = min(j0 + n, NPRE)
        N_LD_SUB = -(-NPRE // c.NSUB)
        N_SG_UNIT = -(-NPRE // NUNIT)

        @block.sync
        def _(sy):
            b.w(sy, 'kmm', KMM_ALL)
            b.w(sy, 'vmm', NW_K * TT_W)
            b.w(sy, 'qtsD', QTSD)

            def qload(sub):
                slot = qTsub[:, (sub % 2) * c.HQ * c.SUBQ:]
                for h in range(c.HQ):
                    ins = sy.dma_start(
                        out=slot[:, h * c.SUBQ:(h + 1) * c.SUBQ],
                        in_=qT_s[h * 128:(h + 1) * 128, sub * c.SUBQ:(sub + 1) * c.SUBQ])
                    b.inc(ins, 'qsl', 16)
            qload(0)
            if c.NSUB > 1:
                qload(1)
            for sub in range(c.NSUB):
                for uu in range(UPS):
                    u = sub * UPS + uu
                    bc = bcslot[u % 2]
                    b.w(sy, 'rcp', u + 1)
                    ins = sy.dma_start(out=invh_s[u:u + 1, 0:c.CW], in_=bc[0:1, 0:c.CW])
                    b.inc(ins, 'ivo', 16)
                    b.w(sy, 'ivo', 16 * (u + 1))
                    ins = sy.dma_start(out=bc[:, 0:c.CW],
                                       in_=invh_s[u:u + 1, 0:c.CW].broadcast_to([128, c.CW]))
                    b.inc(ins, 'ivb', 16)
                if sub + 2 < c.NSUB:
                    b.w(sy, 'smm', (sub + 1) * UPS * NCH)
                    qload(sub + 2)

        @block.gpsimd
        def _(g):
            for sub in range(c.NSUB):
                drain_pre_loads(g, N_LD_SUB)
                for qp in range(c.QTS // 2):
                    for ch in range(c.HQ):
                        j = (sub * (c.QTS // 2) + qp) * c.HQ + ch
                        if j >= 2:
                            b.w(g, 'woc', j - 1)
                        slot = woring[:, (j % 2) * c.D:(j % 2) * c.D + c.D]
                        ins = g.dma_start(out=slot, in_=wo[ch * 128:(ch + 1) * 128, :])
                        b.inc(ins, 'wol0' if j % 2 == 0 else 'wol1', 16)
                    for qti in (qp * 2, qp * 2 + 1):
                        qt = sub * c.QTS + qti
                        if qt >= 1:
                            b.w(g, 'epi', qt)
                        ins = g.dma_start(out=xorig[:, :], in_=xq[qt * 128:(qt + 1) * 128, :])
                        b.inc(ins, 'xol', 16)
            drain_pre_loads(g, NPRE)

        @block.tensor
        def _(t):
            b.w(t, 'tab', TAB0)
            b.w(t, 'kmm', KMM_ALL)
            b.w(t, 'ropeK', ROPEK_ALL)
            b.w(t, 'vev', VEV_ALL)
            for u in range(NUNIT):
                sub, uu = u // UPS, u % UPS
                kv, hg = uu // (c.GRP // c.GH), uu % (c.GRP // c.GH)
                if uu == 0:
                    b.wr(t, 'qsl', c.HQ, sub, 2, c.NSUB)
                    if sub >= 1:
                        b.w(t, 'ln1a', sub * c.QTS)
                qs = qTsub[:, (sub % 2) * c.HQ * c.SUBQ + (kv * c.GRP + hg * c.GH) * c.SUBQ:
                           (sub % 2) * c.HQ * c.SUBQ + (kv * c.GRP + (hg + 1) * c.GH) * c.SUBQ]
                for i in range(NCH):
                    gi = u * NCH + i
                    slotp = pa[:, (gi % 2) * c.CW:(gi % 2) * c.CW + c.CW] if c.CW <= 1024 \
                        else pa[:, 0:c.CW]
                    if gi >= 2:
                        b.w(t, 'exp', gi - 1)
                    if i == 1 and u >= 1:
                        b.w(t, 'srow', u)
                    ins = None
                    for n in range(max(1, c.CW // N5)):
                        wdt = min(N5, c.CW)
                        ins = t.matmul(slotp[:, n * N5:n * N5 + wdt],
                                       kT[:, kv * c.SEQ + i * 128: kv * c.SEQ + (i + 1) * 128],
                                       qs[:, n * N5:n * N5 + wdt],
                                       start=True, stop=True)
                    b.inc(ins, 'smm')
                    if i >= 1:
                        gp = gi - 1
                        if i == 1 and u >= 2:
                            b.w(t, 'divd', (u - 1) * c.GH)
                        b.w(t, 'exp', gi)
                        pi = gp % PR
                        ins = None
                        for n in range(max(1, c.CW // N5)):
                            wdt = min(N5, c.CW)
                            ins = t.matmul(pb[:, (u % 2) * c.CW + n * N5:(u % 2) * c.CW + n * N5 + wdt],
                                           vT[:, (i - 1) * c.KVW + kv * c.HD:
                                              (i - 1) * c.KVW + (kv + 1) * c.HD],
                                           pring[:, pi * c.CW + n * N5: pi * c.CW + n * N5 + wdt],
                                           start=(i == 1), stop=False)
                        b.inc(ins, 'avm')
                gi = u * NCH + NCH - 1
                b.w(t, 'exp', gi + 1)
                pi = gi % PR
                ins = None
                for n in range(max(1, c.CW // N5)):
                    wdt = min(N5, c.CW)
                    ins = t.matmul(pb[:, (u % 2) * c.CW + n * N5:(u % 2) * c.CW + n * N5 + wdt],
                                   vT[:, (NCH - 1) * c.KVW + kv * c.HD:
                                      (NCH - 1) * c.KVW + (kv + 1) * c.HD],
                                   pring[:, pi * c.CW + n * N5: pi * c.CW + n * N5 + wdt],
                                   start=(NCH == 1), stop=True)
                b.inc(ins, 'avm')
                b.w(t, 'sadd', (u + 1) * NCH)
                srow_base = ((NCH - 1) % 2) * c.CW
                ins = None
                for n in range(max(1, c.CW // N5)):
                    wdt = min(N5, c.CW)
                    ins = t.matmul(pa[0:1, srow_base + n * N5: srow_base + n * N5 + wdt],
                                   onesc[:, :], sacc[:, n * N5:n * N5 + wdt],
                                   start=True, stop=True)
                b.inc(ins, 'smr')
                if uu == UPS - 1:
                    b.w(t, 'divd', (sub + 1) * c.HKV * c.GRP)
                    for qp in range(c.QTS // 2):
                        if sub * (c.QTS // 2) + qp >= 1:
                            b.w(t, 'ln1a', (sub * (c.QTS // 2) + qp) * 2)
                        ins = None
                        for ch in range(c.HQ):
                            j = (sub * (c.QTS // 2) + qp) * c.HQ + ch
                            b.w(t, 'wol0' if j % 2 == 0 else 'wol1', 16 * (j // 2 + 1))
                            wslot = woring[:, (j % 2) * c.D:(j % 2) * c.D + c.D]
                            for half in range(2):
                                qti = qp * 2 + half
                                pp = pa if half == 0 else pb
                                for n in range(c.D // N5):
                                    ins = t.matmul(pp[:, n * N5:(n + 1) * N5],
                                                   attnT[:, ch * c.SUBQ + qti * 128: ch * c.SUBQ + qti * 128 + 128],
                                                   wslot[:, n * N5:(n + 1) * N5],
                                                   start=(ch == 0), stop=(ch == c.HQ - 1))
                            b.inc(ins, 'woc')

        @block.scalar
        def _(s):
            b.w(s, 'tab', TAB0)
            for u in range(NUNIT):
                sub, uu = u // UPS, u % UPS
                for i in range(NCH):
                    gi = u * NCH + i
                    b.w(s, 'smm', gi + 1)
                    if gi >= PR:
                        b.w(s, 'avm', gi - PR + 1)
                        b.w(s, 'sadd', gi - PR + 1)
                    pi = gi % PR
                    slotp = pa[:, (gi % 2) * c.CW:(gi % 2) * c.CW + c.CW]
                    ins = s.activation(out=pring[:, pi * c.CW:(pi + 1) * c.CW],
                                       in_=slotp, func=AF.Exp,
                                       bias=float(-c.SHIFT), scale=1.0)
                    b.inc(ins, 'exp')
                drain_pre_signs(s, N_SG_UNIT)
                b.w(s, 'smr', u + 1)
                if u >= 2:
                    b.w(s, 'divd', (u - 1) * c.GH)
                srow_base = ((NCH - 1) % 2) * c.CW
                ins = s.copy(out=bcslot[u % 2][0:1, 0:c.CW], in_=pa[0:1, srow_base:srow_base + c.CW])
                b.inc(ins, 'srow')
                if uu == UPS - 1:
                    for qti in range(c.QTS):
                        qt = sub * c.QTS + qti
                        b.w(s, 'ln1a', qt + 1)
                        ins = s.activation(out=tbufB[:, :], in_=tbufA[:, :], func=AF.Square,
                                           accum_out=stats[:, 1:2])
                        b.inc(ins, 'ln1b')
                        b.w(s, 'ln1c', qt + 1)
                        ins = s.activation(out=stats[:, 5:6], in_=stats[:, 4:5], func=AF.Ln,
                                           bias=float(c.eps1), scale=1.0)
                        b.f(s, ins, 'sf')
                        ins = s.activation(out=stats[:, 5:6], in_=stats[:, 5:6], func=AF.Exp,
                                           scale=-0.5)
                        b.inc(ins, 'ln1d')
                        b.w(s, 'ln1e', qt + 1)
                        ins = s.activation(out=tbufB[:, :], in_=tbufA[:, :], func=AF.Identity,
                                           bias=stats[:, 6:7], scale=stats[:, 7:8])
                        b.inc(ins, 'epi')
                        b.w(s, 'epi', b.cnt['epi'])
                        ins = s.dma_start(out=h1_s[qt * 128:(qt + 1) * 128, :], in_=tbufB[:, :])
                        b.inc(ins, 'h1w', 16)
            drain_pre_signs(s, NPRE)

        @block.vector
        def _(v):
            for u in range(NUNIT):
                sub, uu = u // UPS, u % UPS
                kv, hg = uu // (c.GRP // c.GH), uu % (c.GRP // c.GH)
                for i in range(NCH):
                    gi = u * NCH + i
                    b.w(v, 'exp', gi + 1)
                    pi = gi % PR
                    if i == 0:
                        ins = v.tensor_copy(out=sacc[:, 0:c.CW],
                                            in_=pring[:, pi * c.CW:(pi + 1) * c.CW])
                    else:
                        ins = v.tensor_tensor(out=sacc[:, 0:c.CW], in0=sacc[:, 0:c.CW],
                                              in1=pring[:, pi * c.CW:(pi + 1) * c.CW],
                                              op=ALU.add)
                    b.inc(ins, 'sadd')
                b.w(v, 'srow', u + 1)
                ins = v.reciprocal(out=bcslot[u % 2][0:1, 0:c.CW], in_=bcslot[u % 2][0:1, 0:c.CW])
                b.inc(ins, 'rcp')
                for j in range(c.GH):
                    b.w(v, 'ivb', 16 * (u + 1))
                    b.w(v, 'avm', (u + 1) * NCH)
                    h = kv * c.GRP + hg * c.GH + j
                    ins = v.tensor_tensor(
                        out=attnT[:, h * c.SUBQ:(h + 1) * c.SUBQ],
                        in0=pb[:, (u % 2) * c.CW + j * c.SUBQ:(u % 2) * c.CW + (j + 1) * c.SUBQ],
                        in1=bcslot[u % 2][:, j * c.SUBQ:(j + 1) * c.SUBQ],
                        op=ALU.mult)
                    b.inc(ins, 'divd')
                if uu == UPS - 1:
                    for qti in range(c.QTS):
                        qt = sub * c.QTS + qti
                        pp = pa if qti % 2 == 0 else pb
                        b.w(v, 'woc', (sub * (c.QTS // 2) + qti // 2 + 1) * c.HQ)
                        b.w(v, 'xol', 16 * (qt + 1))
                        if qt >= 1:
                            b.w(v, 'epi', qt)
                        v.tensor_tensor(out=tbufA[:, :], in0=pp[:, 0:c.D], in1=xorig[:, :], op=ALU.add)
                        ins = v.reduce_sum(out=stats[:, 0:1], in_=tbufA[:, :], axis=AX.X)
                        b.inc(ins, 'ln1a')
                        b.w(v, 'ln1b', qt + 1)
                        ins = v.tensor_scalar_mul(out=stats[:, 2:3], in0=stats[:, 0:1], scalar1=float(1.0 / c.D))
                        v.tensor_scalar_mul(out=stats[:, 3:4], in0=stats[:, 1:2], scalar1=float(1.0 / c.D))
                        b.f(v, ins, 'vf')
                        ins = v.tensor_tensor(out=stats[:, 4:5], in0=stats[:, 2:3], in1=stats[:, 2:3], op=ALU.mult)
                        b.f(v, ins, 'vf')
                        ins = v.tensor_tensor(out=stats[:, 4:5], in0=stats[:, 3:4], in1=stats[:, 4:5], op=ALU.subtract)
                        b.inc(ins, 'ln1c')
                        b.w(v, 'ln1d', qt + 1)
                        ins = v.tensor_tensor(out=stats[:, 6:7], in0=stats[:, 2:3], in1=stats[:, 5:6], op=ALU.mult)
                        b.f(v, ins, 'vf')
                        ins = v.tensor_scalar_mul(out=stats[:, 6:7], in0=stats[:, 6:7], scalar1=-1.0)
                        b.f(v, ins, 'vf')
                        ins = v.tensor_copy(out=stats[:, 7:8], in_=stats[:, 5:6])
                        b.inc(ins, 'ln1e')

        H1W = b.cnt['h1w']
        PRW = b.cnt['prw']
        assert pre_cur['ld'] == NPRE and pre_cur['sg'] == NPRE

        if c.dbg:
            dbg_st = nc.dram_tensor("dbg_st", [128, 4 * c.NQT], F32, kind="ExternalOutput").ap()

            @block.sync
            def _(sy):
                b.w(sy, 'ln1e', c.NQT)
                ins = sy.dma_start(out=dbg_st[:, :], in_=statlog[:, :])
                b.inc(ins, 'dbg3', 16)
                b.w(sy, 'dbg3', b.cnt['dbg3'])

            dbg_cos = nc.dram_tensor("dbg_cos", [128, c.SEQ + c.QROWS], F32, kind="ExternalOutput").ap()

            @block.sync
            def _(sy):
                b.w(sy, 'tKS', TKS)
                b.w(sy, 'tQS', TQS)
                ins = sy.dma_start(out=dbg_cos[:, 0:c.SEQ], in_=arD[:, 0:c.SEQ])
                b.inc(ins, 'dbg2', 16)
                ins = sy.dma_start(out=dbg_cos[:, c.SEQ:], in_=arD[:, c.SEQ:c.SEQ + c.QROWS])
                b.inc(ins, 'dbg2', 16)
                b.w(sy, 'dbg2', 32)

            dbg_attnT = nc.dram_tensor("dbg_attnT", [128, c.HQ * c.SUBQ], BF16, kind="ExternalOutput").ap()
            dbg_kT = nc.dram_tensor("dbg_kT", [128, c.HKV * c.SEQ], BF16, kind="ExternalOutput").ap()
            dbg_vT = nc.dram_tensor("dbg_vT", [128, c.NKC * c.KVW], BF16, kind="ExternalOutput").ap()
            dbg_sacc = nc.dram_tensor("dbg_sacc", [128, c.CW], F32, kind="ExternalOutput").ap()
            dbg_inv = nc.dram_tensor("dbg_inv", [1, c.CW], F32, kind="ExternalOutput").ap()
            dbg_p0 = nc.dram_tensor("dbg_p0", [128, c.CW], BF16, kind="ExternalOutput").ap()

            @block.sync
            def _(sy):
                b.w(sy, 'divd', c.GRP)
                b.w(sy, 'sadd', NCH)
                for ap_, dst in ((attnT, dbg_attnT), (kT, dbg_kT), (vT, dbg_vT),
                                 (sacc[:, 0:c.CW], dbg_sacc), (bcslot[0][0:1, 0:c.CW], dbg_inv),
                                 (pring[:, 0:c.CW], dbg_p0)):
                    ins = sy.dma_start(out=dst[:, :], in_=ap_)
                    b.inc(ins, 'dbgw', 16)
                b.w(sy, 'dbgw', b.cnt['dbgw'])

        @block.sync
        def _(sy):
            b.w(sy, 'h1w', H1W)
            for qt in range(c.NQT):
                if qt >= 1:
                    b.w(sy, 'qzd', 16 * qt)
                ins = sy.dma_start(out=tbufA[:, :], in_=h1_s[qt * 128:(qt + 1) * 128, :])
                b.inc(ins, 'h1r', 16)

        @block.vector
        def _(v):
            b.w(v, 'epi', c.NQT)
            b.w(v, 'sadd', b.cnt['sadd'])
            b.w(v, 'exp', b.cnt['exp'])
            for qt in range(c.NQT):
                b.w(v, 'h1r', 16 * (qt + 1))
                ins = v.reduce_sum(out=stats[:, 0:1], in_=tbufA[:, :], axis=AX.X)
                b.inc(ins, 'ln2a')
                b.w(v, 'ln2b', qt + 1)
                ins = v.tensor_scalar_mul(out=stats[:, 2:3], in0=stats[:, 0:1], scalar1=float(1.0 / c.D))
                v.tensor_scalar_mul(out=stats[:, 3:4], in0=stats[:, 1:2], scalar1=float(1.0 / c.D))
                b.f(v, ins, 'vf')
                ins = v.tensor_tensor(out=stats[:, 4:5], in0=stats[:, 2:3], in1=stats[:, 2:3], op=ALU.mult)
                b.f(v, ins, 'vf')
                ins = v.tensor_tensor(out=stats[:, 4:5], in0=stats[:, 3:4], in1=stats[:, 4:5], op=ALU.subtract)
                b.inc(ins, 'ln2c')
                b.w(v, 'ln2d', qt + 1)
                ins = v.tensor_tensor(out=stats[:, 6:7], in0=stats[:, 2:3], in1=stats[:, 5:6], op=ALU.mult)
                b.f(v, ins, 'vf')
                ins = v.tensor_scalar_mul(out=stats[:, 6:7], in0=stats[:, 6:7], scalar1=-1.0)
                b.f(v, ins, 'vf')
                ins = v.tensor_copy(out=stats[:, 7:8], in_=stats[:, 5:6])
                b.inc(ins, 'ln2e')
                b.w(v, 'ln2f', qt + 1)
                ins = v.reduce_max(out=stats[:, 0:1], in_=tbufB[:, :], axis=AX.X,
                             apply_absolute_value=True)
                b.f(v, ins, 'vf')
                ins = v.reciprocal(out=stats[:, 1:2], in_=stats[:, 0:1])
                b.f(v, ins, 'vf')
                ins = v.tensor_scalar_mul(out=stats[:, 1:2], in0=stats[:, 1:2], scalar1=127.0)
                b.f(v, ins, 'vf')
                v.reciprocal(out=invs_sb[:, qt:qt + 1], in_=stats[:, 1:2])
                if qt >= 1:
                    b.w(v, 'qzd', 16 * qt)
                v.tensor_scalar(out=xorig[:, :], in0=tbufB[:, :], scalar1=stats[:, 1:2],
                                scalar2=384.0, op0=ALU.mult, op1=ALU.add)
                v.tensor_copy(out=qtempi[:, :], in_=xorig[:, :])
                v.tensor_copy(out=tbufB[:, :], in_=qtempi[:, :])
                ins = v.tensor_tensor(out=xorig[:, :], in0=xorig[:, :], in1=tbufB[:, :], op=ALU.subtract)
                b.f(v, ins, 'vf')
                v.tensor_single_scalar(out=xorig[:, :], in_=xorig[:, :], scalar=0.5, op=ALU.is_ge)
                ins = v.scalar_tensor_tensor(out=xqstage, in0=tbufB[:, :], scalar=-384.0,
                                             in1=xorig[:, :], op0=ALU.add, op1=ALU.add)
                b.inc(ins, 'qz')

        @block.scalar
        def _(s):
            for qt in range(c.NQT):
                b.w(s, 'ln2a', qt + 1)
                ins = s.activation(out=tbufB[:, :], in_=tbufA[:, :], func=AF.Square,
                                   accum_out=stats[:, 1:2])
                b.inc(ins, 'ln2b')
                b.w(s, 'ln2c', qt + 1)
                ins = s.activation(out=stats[:, 5:6], in_=stats[:, 4:5], func=AF.Ln,
                             bias=float(c.eps2), scale=1.0)
                b.f(s, ins, 'sf')
                ins = s.activation(out=stats[:, 5:6], in_=stats[:, 5:6], func=AF.Exp,
                                   scale=-0.5)
                b.inc(ins, 'ln2d')
                b.w(s, 'ln2e', qt + 1)
                ins = s.activation(out=tbufB[:, :], in_=tbufA[:, :], func=AF.Identity,
                                   bias=stats[:, 6:7], scale=stats[:, 7:8])
                b.inc(ins, 'ln2f')
                b.w(s, 'qz', qt + 1)
                ins = s.dma_start(out=xqn_s[qt * 128:(qt + 1) * 128, :], in_=xqstage)
                b.inc(ins, 'qzd', 16)
        QZD = b.cnt['qzd']

        GPP = c.D_PAIR // c.GU_GROUP

        @block.sync
        def _(sy):
            b.w(sy, 'qzd', QZD)
            b.w(sy, 'prw', PRW)
            dstT = bass.AP(invs_h.tensor, 0, [[1, 128], [128, c.NQT]])
            ins = sy.dma_start(out=dstT, in_=invs_sb[:, 0:c.NQT], allow_slow_non_contiguous=True)
            b.inc(ins, 'ivw', 16)
            b.w(sy, 'ivw', 16)
            for fs in range(c.NFSUB):
                if fs >= 1:
                    b.w(sy, 'pmul', fs * c.NGRP * c.GU_GROUP)
                ins = sy.dma_start(out=ibc, in_=invs_h[0:1, fs * c.FSUB:(fs + 1) * c.FSUB]
                                   .broadcast_to([128, c.FSUB]))
                b.inc(ins, 'ibl', 16)
                if fs >= 1:
                    b.w(sy, 'gmm', fs * c.NGRP * c.GU_GROUP)
                for ch in range(c.NDM):
                    ins = sy.dma_start(out=xqT[:, ch * c.FSUB:(ch + 1) * c.FSUB],
                                       in_=xqn_s[fs * c.FSUB:(fs + 1) * c.FSUB,
                                                 ch * 128:(ch + 1) * 128], transpose=True)
                    b.inc(ins, 'xqTl', 16)
                for pr_ in range(c.NPAIR):
                    P = fs * c.NPAIR + pr_
                    for gidx in range(pr_ * (c.D_PAIR // c.GU_GROUP),
                                      (pr_ + 1) * (c.D_PAIR // c.GU_GROUP)):
                        GG = fs * c.NGRP + gidx
                        b.w(sy, 'gmm', GG * c.GU_GROUP)
                        dstG = bass.AP(gubufG.tensor, gubufG.offset,
                                       [list(gubufG.ap[0]), [c.GW, c.NDM], [1, c.GW]])
                        srcG = bass.AP(wgT_s.tensor, gidx * c.GW,
                                       [[c.FF, 128], [128 * c.FF, c.NDM], [1, c.GW]])
                        ins = sy.dma_start(out=dstG, in_=srcG)
                        b.inc(ins, 'wgl', 16)
                        b.w(sy, 'umm', GG * c.GU_GROUP)
                        dstU = bass.AP(gubufU.tensor, gubufU.offset,
                                       [list(gubufU.ap[0]), [c.GW, c.NDM], [1, c.GW]])
                        srcU = bass.AP(wuT_s.tensor, gidx * c.GW,
                                       [[c.FF, 128], [128 * c.FF, c.NDM], [1, c.GW]])
                        ins = sy.dma_start(out=dstU, in_=srcU)
                        b.inc(ins, 'wul', 16)
                    b.w(sy, 'dmm', P * c.QT_F)
                    for hc in range(c.D_PAIR):
                        row = (pr_ * c.D_PAIR + hc) * 128
                        ins = sy.dma_start(out=wdbuf[:, hc * c.D:(hc + 1) * c.D],
                                           in_=wdT_s[row:row + 128, :])
                        b.inc(ins, 'wdl', 16)

        @block.tensor
        def _(t):
            b.w(t, 'woc', b.cnt['woc'])
            for fs in range(c.NFSUB):
                b.w(t, 'xqTl', 16 * c.NDM * (fs + 1))
                for gidx in range(c.NGRP):
                    GG = fs * c.NGRP + gidx
                    P = GG // GPP
                    b.w(t, 'wgl', 16 * (GG + 1))
                    b.w(t, 'wul', 16 * (GG + 1))
                    if gidx % GPP == 0:
                        b.w(t, 'flv', P * c.QT_F)
                    for ht in range(c.GU_GROUP):
                        k = GG * c.GU_GROUP + ht
                        if k >= 1:
                            b.w(t, 'aev', k)
                        ins = None
                        for ch in range(c.NDM):
                            for n in range(max(1, c.FSUB // N5)):
                                wdt = min(N5, c.FSUB)
                                ins = t.matmul(pa[:, n * N5:n * N5 + wdt],
                                               gubufG[:, ch * c.GW + ht * 128: ch * c.GW + ht * 128 + 128],
                                               xqT[:, ch * c.FSUB + n * N5: ch * c.FSUB + n * N5 + wdt],
                                               start=(ch == 0), stop=(ch == c.NDM - 1))
                        b.inc(ins, 'gmm')
                        if k >= 1:
                            b.w(t, 'pmul', k)
                        ins = None
                        UOFF = 1024
                        for n in range(max(1, c.FSUB // N5)):
                            wdt = min(N5, c.FSUB)
                            ups = pa[:, UOFF + n * N5: UOFF + n * N5 + wdt]
                            for ch in range(c.NDM):
                                ins = t.matmul(ups,
                                               gubufU[:, ch * c.GW + ht * 128: ch * c.GW + ht * 128 + 128],
                                               xqT[:, ch * c.FSUB + n * N5: ch * c.FSUB + n * N5 + wdt],
                                               start=(ch == 0), stop=(ch == c.NDM - 1))
                        b.inc(ins, 'umm')
                    if (gidx + 1) % GPP == 0:
                        b.w(t, 'pmul', (GG + 1) * c.GU_GROUP)
                        b.w(t, 'wdl', 16 * c.D_PAIR * (P + 1))
                        for qti in range(c.QT_F):
                            fi = P * c.QT_F + qti
                            pp = pb if qti % 2 == 0 else pa
                            if qti >= 2:
                                b.w(t, 'flv', fi - 1)
                            elif qti == 1 and P >= 1:
                                pass
                            ins = None
                            ps_ = (P % 2) * c.D_PAIR * c.FSUB
                            for hc in range(c.D_PAIR):
                                for n in range(c.D // N5):
                                    ins = t.matmul(pp[:, n * N5:(n + 1) * N5],
                                                   pTb[:, ps_ + hc * c.FSUB + qti * 128:
                                                       ps_ + hc * c.FSUB + qti * 128 + 128],
                                                   wdbuf[:, hc * c.D + n * N5: hc * c.D + (n + 1) * N5],
                                                   start=(hc == 0), stop=(hc == c.D_PAIR - 1))
                            b.inc(ins, 'dmm')

        @block.vector
        def _(v):
            for fs in range(c.NFSUB):
                for gidx in range(c.NGRP):
                    GG = fs * c.NGRP + gidx
                    P = GG // GPP
                    ps_ = (P % 2) * c.D_PAIR * c.FSUB
                    within = (gidx % GPP) * c.GU_GROUP
                    for ht in range(c.GU_GROUP):
                        k = GG * c.GU_GROUP + ht
                        b.w(v, 'gmm', k + 1)
                        if k >= 2:
                            b.w(v, 'sil', k - 2 + 1)
                        if gidx == 0 and ht == 0:
                            b.w(v, 'ibl', 16 * (fs + 1))
                        ins = v.tensor_tensor(out=abuf[ht % 2][:, 0:c.FSUB],
                                              in0=pa[:, 0:c.FSUB], in1=ibc, op=ALU.mult)
                        b.inc(ins, 'aev')
                        b.w(v, 'umm', k + 1)
                        b.w(v, 'sil', k + 1)
                        ups = pa[:, 1024:1024 + c.FSUB]
                        ins = v.tensor_tensor(out=pTb[:, ps_ + (within + ht) * c.FSUB:
                                                     ps_ + (within + ht + 1) * c.FSUB],
                                              in0=abuf[ht % 2][:, 0:c.FSUB], in1=ups, op=ALU.mult)
                        b.inc(ins, 'pmul')
                    if (gidx + 1) % GPP == 0:
                        for qti in range(c.QT_F):
                            fi = P * c.QT_F + qti
                            pp = pb if qti % 2 == 0 else pa
                            b.w(v, 'dmm', fi + 1)
                            b.w(v, 'flw', 16 * fi)
                            ins = v.tensor_copy(out=foutr[qti % 2][:, :], in_=pp[:, 0:c.D])
                            b.inc(ins, 'flv')

        @block.scalar
        def _(s):
            for fs in range(c.NFSUB):
                for gidx in range(c.NGRP):
                    for ht in range(c.GU_GROUP):
                        k = (fs * c.NGRP + gidx) * c.GU_GROUP + ht
                        b.w(s, 'aev', k + 1)
                        ins = s.activation(out=abuf[ht % 2][:, 0:c.FSUB],
                                           in_=abuf[ht % 2][:, 0:c.FSUB], func=AF.Silu)
                        b.inc(ins, 'sil')

        @block.gpsimd
        def _(g):
            for fs in range(c.NFSUB):
                for pr_ in range(c.NPAIR):
                    P = fs * c.NPAIR + pr_
                    for qti in range(c.QT_F):
                        fi = P * c.QT_F + qti
                        b.w(g, 'flv', fi + 1)
                        qt = fs * c.QT_F + qti
                        if pr_ >= 1:
                            b.w(g, 'flw', 16 * ((P - 1) * c.QT_F + qti + 1))
                        ins = g.dma_start(out=dout_s[qt * 128:(qt + 1) * 128, :],
                                          in_=foutr[qti % 2][:, :],
                                          accum_op=(ALU.add if pr_ >= 1 else ALU.bypass))
                        b.inc(ins, 'flw', 16)
        FLW = b.cnt['flw']

        @block.sync
        def _(sy):
            b.w(sy, 'flw', FLW)
            b.w(sy, 'sil', b.cnt['sil'])
            for qt in range(c.NQT):
                if qt >= 1:
                    b.w(sy, 'ep3', qt)
                ins = sy.dma_start(out=foutr[0][:, :], in_=dout_s[qt * 128:(qt + 1) * 128, :])
                b.inc(ins, 'dor', 16)
                ins = sy.dma_start(out=h1t[:, :], in_=h1_s[qt * 128:(qt + 1) * 128, :])
                b.inc(ins, 'dor', 16)

        @block.vector
        def _(v):
            for qt in range(c.NQT):
                b.w(v, 'dor', 32 * (qt + 1))
                v.scalar_tensor_tensor(out=foutr[1][:, :], in0=foutr[0][:, :],
                                       scalar=invs_sb[:, qt:qt + 1], in1=h1t[:, :],
                                       op0=ALU.mult, op1=ALU.add)
                ins = v.reduce_sum(out=stats[:, 0:1], in_=foutr[1][:, :], axis=AX.X)
                b.inc(ins, 'ln3a')
                b.w(v, 'ln3b', qt + 1)
                ins = v.tensor_scalar_mul(out=stats[:, 2:3], in0=stats[:, 0:1], scalar1=float(1.0 / c.D))
                v.tensor_scalar_mul(out=stats[:, 3:4], in0=stats[:, 1:2], scalar1=float(1.0 / c.D))
                b.f(v, ins, 'vf')
                ins = v.tensor_tensor(out=stats[:, 4:5], in0=stats[:, 2:3], in1=stats[:, 2:3], op=ALU.mult)
                b.f(v, ins, 'vf')
                ins = v.tensor_tensor(out=stats[:, 4:5], in0=stats[:, 3:4], in1=stats[:, 4:5], op=ALU.subtract)
                b.inc(ins, 'ln3c')
                b.w(v, 'ln3d', qt + 1)
                ins = v.tensor_tensor(out=stats[:, 6:7], in0=stats[:, 2:3], in1=stats[:, 5:6], op=ALU.mult)
                b.f(v, ins, 'vf')
                ins = v.tensor_scalar_mul(out=stats[:, 6:7], in0=stats[:, 6:7], scalar1=-1.0)
                b.f(v, ins, 'vf')
                ins = v.tensor_copy(out=stats[:, 7:8], in_=stats[:, 5:6])
                b.inc(ins, 'ln3e')

        @block.scalar
        def _(s):
            for qt in range(c.NQT):
                b.w(s, 'ln3a', qt + 1)
                ins = s.activation(out=foutr[0][:, :], in_=foutr[1][:, :], func=AF.Square,
                                   accum_out=stats[:, 1:2])
                b.inc(ins, 'ln3b')
                b.w(s, 'ln3c', qt + 1)
                ins = s.activation(out=stats[:, 5:6], in_=stats[:, 4:5], func=AF.Ln,
                             bias=float(c.eps1), scale=1.0)
                b.f(s, ins, 'sf')
                ins = s.activation(out=stats[:, 5:6], in_=stats[:, 5:6], func=AF.Exp, scale=-0.5)
                b.inc(ins, 'ln3d')
                b.w(s, 'ln3e', qt + 1)
                ins = s.activation(out=foutr[1][:, :], in_=foutr[1][:, :], func=AF.Identity,
                                   bias=stats[:, 6:7], scale=stats[:, 7:8])
                b.inc(ins, 'ln3f')

        @block.gpsimd
        def _(g):
            for qt in range(c.NQT):
                b.w(g, 'ln3f', qt + 1)
                ins = g.dma_start(out=out[qt * 128:(qt + 1) * 128, :], in_=foutr[1][:, :])
                b.inc(ins, 'outw', 16)
                g.wait_ge(b.s('outw'), 16 * (qt + 1))
                ins = g.memset(stats[0:1, 1:2], 0.0)
                b.inc(ins, 'ep3')

    es.close()
    return nc


def kernel(L0_attn_hidden_states, L0_attn_Wq, L0_attn_Wk, L0_attn_Wv,
           L0_attn_Wo, L0_ffn_W_g, L0_ffn_W_u, L0_ffn_W_d):
    cfg = Cfg()
    x = np.ascontiguousarray(np.asarray(L0_attn_hidden_states, dtype=np.float32))
    B_, S_, D_ = x.shape
    nc = build_nc(cfg)
    ws = dict(wq=np.ascontiguousarray(L0_attn_Wq, dtype=np.float32),
              wk=np.ascontiguousarray(L0_attn_Wk, dtype=np.float32),
              wv=np.ascontiguousarray(L0_attn_Wv, dtype=np.float32),
              wo=np.ascontiguousarray(L0_attn_Wo, dtype=np.float32),
              wgT=np.ascontiguousarray(np.asarray(L0_ffn_W_g, dtype=np.float32).T),
              wuT=np.ascontiguousarray(np.asarray(L0_ffn_W_u, dtype=np.float32).T),
              wdT=np.ascontiguousarray(np.asarray(L0_ffn_W_d, dtype=np.float32).T))
    in_maps = []
    for core in range(8):
        bidx, r = core // 2, core % 2
        im = dict(ws)
        im["xbT"] = np.ascontiguousarray(x[bidx].T)
        im["xq"] = np.ascontiguousarray(x[bidx, r * cfg.QROWS:(r + 1) * cfg.QROWS])
        im["xqT_i"] = np.ascontiguousarray(x[bidx, r * cfg.QROWS:(r + 1) * cfg.QROWS].T)
        im["pos"] = np.array([[float(r * cfg.QROWS)]], dtype=np.float32)
        in_maps.append(im)
    res = run_bass_kernel_spmd(nc, in_maps, core_ids=list(range(8)))
    outp = np.empty((B_, S_, D_), dtype=np.float32)
    for core in range(8):
        bidx, r = core // 2, core % 2
        outp[bidx, r * cfg.QROWS:(r + 1) * cfg.QROWS] = res.results[core]["out"]
    return outp


# version 11
# speedup vs baseline: 1.2588x; 1.0225x over previous
import numpy as np
from contextlib import ExitStack
from dataclasses import dataclass

import concourse.bass as bass
import concourse.mybir as mybir
from concourse.bass_utils import run_bass_kernel_spmd

F32 = mybir.dt.float32
BF16 = mybir.dt.bfloat16
I32 = mybir.dt.int32
AF = mybir.ActivationFunctionType
ALU = mybir.AluOpType
AX = mybir.AxisListType
PI = float(np.pi)
TWO_PI = float(2.0 * np.pi)


@dataclass
class Cfg:
    SEQ: int = 4096
    QROWS: int = 2048
    D: int = 2048
    HQ: int = 16
    HKV: int = 4
    HD: int = 128
    FF: int = 8192
    SUBQ: int = 512
    TOKWIN: int = 512
    FSUB: int = 1024
    GU_GROUP: int = 4
    D_PAIR: int = 8
    theta: float = 10000.0
    eps1: float = 1e-5
    eps2: float = 1e-8
    SHIFT: float = 20.0
    dbg: bool = False

    @property
    def NDM(self): return self.D // 128
    @property
    def GRP(self): return self.HQ // self.HKV
    @property
    def NKC(self): return self.SEQ // 128
    @property
    def NSUB(self): return self.QROWS // self.SUBQ
    GH: int = 2

    @property
    def CW(self): return self.GH * self.SUBQ
    @property
    def NQT(self): return self.QROWS // 128
    @property
    def NHT(self): return self.FF // 128
    @property
    def NFSUB(self): return self.QROWS // self.FSUB
    @property
    def KVW(self): return self.HKV * self.HD
    @property
    def QTS(self): return self.SUBQ // 128
    @property
    def NGRP(self): return self.NHT // self.GU_GROUP
    @property
    def NPAIR(self): return self.NHT // self.D_PAIR
    @property
    def GW(self): return self.GU_GROUP * 128
    @property
    def QT_F(self): return self.FSUB // 128


def eo_ap(ap128, hd):
    ev = bass.AP(ap128.tensor, ap128.offset, [list(ap128.ap[0]), [2, hd // 2]])
    od = bass.AP(ap128.tensor, ap128.offset + 1, [list(ap128.ap[0]), [2, hd // 2]])
    return ev, od


class B:
    def __init__(self, nc, es):
        self.nc, self.es = nc, es
        self.sems, self.cnt = {}, {}

    def s(self, name):
        if name not in self.sems:
            self.sems[name] = self.es.enter_context(self.nc.semaphore(name))
            self.cnt[name] = 0
        return self.sems[name]

    def inc(self, ins, name, n=1):
        ins.then_inc(self.s(name), n)
        self.cnt[name] += n
        return self.cnt[name]

    def w(self, eng, name, val):
        if val > 0:
            eng.wait_ge(self.s(name), val)

    def f(self, eng, ins, name):
        n = self.inc(ins, name)
        eng.wait_ge(self.s(name), n)

    def wr(self, eng, name, ndma, item, depth, total):
        v = 16 * ndma * min(item + depth, total)
        if v > 0:
            eng.wait_ge(self.s(name), v)


def build_nc(cfg: Cfg):
    c = cfg
    nc = bass.Bass("TRN2", target_bir_lowering=False,
                   detect_race_conditions=False)
    es = ExitStack()
    b = B(nc, es)

    xbT = nc.dram_tensor("xbT", [c.D, c.SEQ], F32, kind="ExternalInput").ap()
    xq = nc.dram_tensor("xq", [c.QROWS, c.D], F32, kind="ExternalInput").ap()
    xqT_i = nc.dram_tensor("xqT_i", [c.D, c.QROWS], F32, kind="ExternalInput").ap()
    wq = nc.dram_tensor("wq", [c.D, c.D], F32, kind="ExternalInput").ap()
    wk = nc.dram_tensor("wk", [c.D, c.KVW], F32, kind="ExternalInput").ap()
    wv = nc.dram_tensor("wv", [c.D, c.KVW], F32, kind="ExternalInput").ap()
    wo = nc.dram_tensor("wo", [c.D, c.D], F32, kind="ExternalInput").ap()
    wgT = nc.dram_tensor("wgT", [c.D, c.FF], F32, kind="ExternalInput").ap()
    wuT = nc.dram_tensor("wuT", [c.D, c.FF], F32, kind="ExternalInput").ap()
    wdT = nc.dram_tensor("wdT", [c.FF, c.D], F32, kind="ExternalInput").ap()
    pos = nc.dram_tensor("pos", [1, 1], F32, kind="ExternalInput").ap()
    out = nc.dram_tensor("out", [c.QROWS, c.D], F32, kind="ExternalOutput").ap()

    xbfT_s = nc.dram_tensor("xbfT_s", [c.D, c.SEQ], BF16).ap()
    xqbfT_s = nc.dram_tensor("xqbfT_s", [c.D, c.QROWS], BF16).ap()
    qT_s = nc.dram_tensor("qT_s", [c.D, c.QROWS], BF16).ap()
    h1_s = nc.dram_tensor("h1_s", [c.QROWS, c.D], F32, kind="ExternalOutput" if True else "Internal").ap()
    xqn_s = nc.dram_tensor("xqn_s", [c.QROWS, c.D], BF16, kind="ExternalOutput").ap()
    wgT_s = nc.dram_tensor("wgT_s", [c.D, c.FF], BF16).ap()
    wuT_s = nc.dram_tensor("wuT_s", [c.D, c.FF], BF16).ap()
    wdT_s = nc.dram_tensor("wdT_s", [c.FF, c.D], BF16).ap()
    dout_s = nc.dram_tensor("dout_s", [c.QROWS, c.D], F32, kind="ExternalOutput").ap()
    invs_h = nc.dram_tensor("invs_h", [1, c.QROWS], F32).ap()
    invh_s = nc.dram_tensor("invh_s", [64, 2048], F32).ap()

    szA = max(c.NDM * c.D, c.HKV * c.SEQ + c.NKC * c.KVW,
              2 * c.D_PAIR * c.FSUB + c.D_PAIR * c.D)
    arA = es.enter_context(nc.sbuf_tensor("arA", [128, szA], BF16))
    szB = max(2 * c.NDM * c.TOKWIN, 2 * c.HQ * c.SUBQ, c.NDM * c.FSUB)
    arB = es.enter_context(nc.sbuf_tensor("arB", [128, szB], BF16))
    szC = max(c.HQ * c.TOKWIN, 2 * c.NDM * c.KVW,
              4 * c.CW + c.HQ * c.SUBQ + 2 * c.D,
              2 * c.NDM * c.GW)
    arC = es.enter_context(nc.sbuf_tensor("arC", [128, szC], BF16))
    szD = max(c.SEQ + c.QROWS, 3 * c.D)
    arD = es.enter_context(nc.sbuf_tensor("arD", [128, szD], F32))
    PREW = c.D // 4
    arF = es.enter_context(nc.sbuf_tensor("arF", [128, 4 * PREW], BF16))
    sacc = es.enter_context(nc.sbuf_tensor("sacc", [128, max(c.CW, c.D)], F32))

    onesc = es.enter_context(nc.sbuf_tensor("onesc", [128, 1], F32))
    cvals = [float(-c.SHIFT), float(c.eps1), float(c.eps2)]
    consts = es.enter_context(nc.sbuf_tensor("consts", [128, len(cvals)], F32))
    for _i, _v in enumerate(cvals):
        nc.const_aps.register(F32, _v, consts[:, _i:_i + 1])
    ones1 = es.enter_context(nc.sbuf_tensor("ones1", [1, 128], F32))
    stats = es.enter_context(nc.sbuf_tensor("stats", [128, 8], F32))
    invs_sb = es.enter_context(nc.sbuf_tensor("invs_sb", [128, c.NQT], F32))
    statlog = es.enter_context(nc.sbuf_tensor("statlog", [128, 4 * c.NQT], F32))
    fcol = es.enter_context(nc.sbuf_tensor("fcol", [64, 4], F32))
    fcoli = es.enter_context(nc.sbuf_tensor("fcoli", [64, 1], I32))
    poscol = es.enter_context(nc.sbuf_tensor("poscol", [64, 1], F32))
    TW = min(512, c.SEQ, c.QROWS)
    twin_i = es.enter_context(nc.sbuf_tensor("twin_i", [64, TW], I32))
    twin_f = es.enter_context(nc.sbuf_tensor("twin_f", [128, max(2048, 2 * c.TOKWIN, c.CW)], F32))

    pa = es.enter_context(nc.psum_tensor("pa", [128, 2048], F32))
    pb = es.enter_context(nc.psum_tensor("pb", [128, 2048], F32))
    PSLOT = [pa[:, i * 512:(i + 1) * 512] for i in range(4)] + \
            [pb[:, i * 512:(i + 1) * 512] for i in range(4)]

    SCALE = float(1.0 / np.sqrt(c.HD))
    NW_Q = c.QROWS // c.TOKWIN
    NW_K = c.SEQ // c.TOKWIN
    TT_W = c.TOKWIN // 128
    NXQ = c.QROWS // 128
    NXB = c.SEQ // 128
    N5 = 512
    NCH = c.NKC
    PR = 4
    NUNIT = c.NSUB * c.HKV * (c.GRP // c.GH)
    UPS = c.HKV * (c.GRP // c.GH)
    assert c.TOKWIN <= 512 and 2 * c.CW <= 2048 and c.SUBQ <= 512 and NCH % 2 == 0
    assert c.FSUB <= 2048 and c.D_PAIR % c.GU_GROUP == 0

    wq_perm = arA
    kT = arA[:, 0:c.HKV * c.SEQ]
    vT = arA[:, c.HKV * c.SEQ: c.HKV * c.SEQ + c.NKC * c.KVW]
    pTb = arA[:, 0:2 * c.D_PAIR * c.FSUB]
    wdbuf = arA[:, 2 * c.D_PAIR * c.FSUB: 2 * c.D_PAIR * c.FSUB + c.D_PAIR * c.D]

    xTw = arB
    qTsub = arB
    xqT = arB[:, 0:c.NDM * c.FSUB]

    qTst = arC[:, 0:c.HQ * c.TOKWIN]
    wkv = arC[:, 0:2 * c.NDM * c.KVW]
    pring = arC[:, 0:PR * c.CW]
    attnT = arC[:, PR * c.CW: PR * c.CW + c.HQ * c.SUBQ]
    woring = arC[:, PR * c.CW + c.HQ * c.SUBQ: PR * c.CW + c.HQ * c.SUBQ + 2 * c.D]
    gubufG = arC[:, 0:c.NDM * c.GW]
    gubufU = arC[:, c.NDM * c.GW:2 * c.NDM * c.GW]

    cosK = arD[0:64, 0:c.SEQ]
    sinK = arD[64:128, 0:c.SEQ]
    cosQ = arD[0:64, c.SEQ:c.SEQ + c.QROWS]
    sinQ = arD[64:128, c.SEQ:c.SEQ + c.QROWS]
    tbufA = arD[:, 0:c.D]
    tbufB = arD[:, c.D:2 * c.D]
    xorig = arD[:, 2 * c.D:3 * c.D]

    sacc_f = sacc[:, 0:c.D]
    qtempi = sacc[:, 0:c.D].bitcast(I32)[:, 0:c.D]
    abuf = [tbufA, tbufB]
    foutr = [xorig, sacc_f]
    h1t = tbufA
    xqstage = pring[:, 0:c.D]

    stage = arC[:, c.HQ * c.TOKWIN: c.HQ * c.TOKWIN + 2 * c.D]
    ibc = twin_f[:, 0:c.FSUB].bitcast(F32)[:, 0:c.FSUB] if False else twin_f[:, 0:c.FSUB]
    wpre = arF
    bcslot = [twin_f[:, 0:c.CW], twin_f[:, c.CW:2 * c.CW]]
    s1v = twin_f[0:64, 0:c.TOKWIN]
    s2v = twin_f[0:64, c.TOKWIN:2 * c.TOKWIN]
    iotaF = twin_f[64:128, 0:TW]
    angT = twin_f[64:128, TW:2 * TW]
    scX = twin_f[64:128, 2 * TW:3 * TW]
    scY = twin_f[64:128, 3 * TW:4 * TW]

    with nc.Block() as block:
        @block.gpsimd
        def _(g):
            ins = g.iota(twin_i[:, :], [[1, TW]], base=0, channel_multiplier=0)
            b.inc(ins, 'tab')
            ins = g.iota(fcoli[:, :], [[0, 1]], base=0, channel_multiplier=1)
            b.inc(ins, 'tab')
            ins = g.dma_start(out=poscol[:, :], in_=pos[:, :].broadcast_to([64, 1]))
            b.inc(ins, 'posl', 16)

        @block.vector
        def _(v):
            v.memset(onesc[:, :], 1.0)
            for _i, _v in enumerate(cvals):
                v.memset(consts[:, _i:_i + 1], _v)
            ins = v.memset(ones1[:, :], 1.0)
            b.inc(ins, 'tab')
        TAB0 = b.cnt['tab']

        @block.gpsimd
        def _(g):
            def cast_stream(src, dst, ntiles, width, lsem, wsem):
                for t in range(ntiles):
                    slot = stage[:, (t % 2) * c.D:(t % 2) * c.D + width] \
                        if width <= c.D else stage[:, 0:width]
                    if width <= c.D:
                        if t >= 2:
                            b.w(g, wsem, 16 * (t - 1))
                    else:
                        if t >= 1:
                            b.w(g, wsem, 16 * t)
                    ins = g.dma_start(out=slot, in_=src[t * 128:(t + 1) * 128, :])
                    b.inc(ins, lsem, 16)
                    b.w(g, lsem, 16 * (t + 1))
                    ins = g.dma_start(out=dst[t * 128:(t + 1) * 128, :], in_=slot)
                    b.inc(ins, wsem, 16)
            cast_stream(xqT_i, xqbfT_s, c.NDM, c.QROWS, 'xql', 'xqw')
            cast_stream(xbT, xbfT_s, c.NDM, c.SEQ, 'xbl', 'xbw')
        XQW, XBW = b.cnt['xqw'], b.cnt['xbw']

        @block.gpsimd
        def _(g):
            b.w(g, 'xbw', XBW)
            for ch in range(c.NDM):
                if ch >= 2:
                    b.w(g, 'wqp', ch - 1)
                ins = g.dma_start(out=stage[:, (ch % 2) * c.D:(ch % 2) * c.D + c.D],
                                  in_=wq[ch * 128:(ch + 1) * 128, :])
                b.inc(ins, 'wql', 16)

        @block.vector
        def _(v):
            for ch in range(c.NDM):
                b.wr(v, 'wql', 1, ch, 2, c.NDM)
                slot = stage[:, (ch % 2) * c.D:(ch % 2) * c.D + c.D]
                ins = None
                for h in range(c.HQ):
                    ev, od = eo_ap(slot[:, h * c.HD:(h + 1) * c.HD], c.HD)
                    dst = wq_perm[:, ch * c.D + h * c.HD: ch * c.D + (h + 1) * c.HD]
                    ins = v.tensor_single_scalar(out=dst[:, 0:64], in_=ev, scalar=SCALE, op=ALU.mult)
                    ins = v.tensor_single_scalar(out=dst[:, 64:128], in_=od, scalar=SCALE, op=ALU.mult)
                b.inc(ins, 'wqp')
        WQP = b.cnt['wqp']

        @block.vector
        def _(v):
            b.w(v, 'tab', TAB0)
            v.tensor_copy(out=fcol[:, 0:1], in_=fcoli[:, :])
            ins = v.tensor_copy(out=iotaF, in_=twin_i[:, :])
            b.inc(ins, 'tab')
        @block.scalar
        def _(s):
            b.w(s, 'tab', TAB0 + 1)
            ins = s.activation(out=fcol[:, 1:2], in_=fcol[:, 0:1], func=AF.Exp,
                               scale=float(-np.log(c.theta) / 64.0))
            b.inc(ins, 'tab')
        TAB1 = b.cnt['tab']

        def emit_tables(name, nwin, cosbuf, sinbuf, use_pos, prior=None):
            sV, sS = name + 'V', name + 'S'
            @block.vector
            def _(v):
                b.w(v, 'tab', TAB1)
                if prior is not None:
                    b.w(v, prior, b.cnt[prior])
                if use_pos:
                    b.w(v, 'posl', 16)
                for w in range(nwin):
                    ang = angT
                    if w >= 1:
                        b.w(v, sS, 2 * w)
                    if use_pos:
                        v.tensor_scalar(out=ang, in0=iotaF, scalar1=float(w * TW),
                                        scalar2=poscol[:, 0:1], op0=ALU.add, op1=ALU.add)
                    else:
                        v.tensor_single_scalar(out=ang, in_=iotaF, scalar=float(w * TW), op=ALU.add)
                    v.tensor_scalar_mul(out=ang, in0=ang, scalar1=fcol[:, 1:2])
                    v.tensor_scalar(out=scX, in0=ang, scalar1=float(1.0 / TWO_PI),
                                    scalar2=384.0, op0=ALU.mult, op1=ALU.add)
                    ib = scY.bitcast(I32)[:, 0:TW]
                    v.tensor_copy(out=ib, in_=scX)
                    v.tensor_copy(out=scY, in_=ib)
                    ins = v.tensor_tensor(out=scX, in0=scX, in1=scY, op=ALU.subtract)
                    b.f(v, ins, 'vf')
                    v.tensor_single_scalar(out=scX, in_=scX, scalar=0.5, op=ALU.is_ge)
                    v.tensor_tensor(out=scX, in0=scX, in1=scY, op=ALU.add)
                    ins = v.tensor_scalar(out=scX, in0=scX, scalar1=-384.0, scalar2=float(-TWO_PI),
                                    op0=ALU.add, op1=ALU.mult)
                    b.f(v, ins, 'vf')
                    v.tensor_tensor(out=ang, in0=ang, in1=scX, op=ALU.add)
                    ins = v.tensor_single_scalar(out=ang, in_=ang, scalar=PI, op=ALU.min)
                    b.f(v, ins, 'vf')
                    ins = v.tensor_single_scalar(out=ang, in_=ang, scalar=-PI, op=ALU.max)
                    b.inc(ins, sV)
                    b.w(v, sS, 2 * w + 1)
                    ins = v.tensor_single_scalar(out=ang, in_=ang, scalar=float(PI / 2), op=ALU.add)
                    b.f(v, ins, 'vf')
                    ins = v.tensor_single_scalar(out=scX, in_=ang, scalar=PI, op=ALU.is_gt)
                    b.f(v, ins, 'vf')
                    ins = v.scalar_tensor_tensor(out=ang, in0=scX, scalar=float(-TWO_PI),
                                                 in1=ang, op0=ALU.mult, op1=ALU.add)
                    b.inc(ins, sV)
            @block.scalar
            def _(s):
                for w in range(nwin):
                    b.w(s, sV, 2 * w + 1)
                    ins = s.activation(out=sinbuf[:, w * TW:(w + 1) * TW], in_=angT, func=AF.Sin)
                    b.inc(ins, sS)
                    b.w(s, sV, 2 * w + 2)
                    ins = s.activation(out=cosbuf[:, w * TW:(w + 1) * TW], in_=angT, func=AF.Sin)
                    b.inc(ins, sS)
            return b.cnt[sS]

        emit_tables('tK', c.SEQ // TW, cosK, sinK, False)
        emit_tables('tQ', c.QROWS // TW, cosQ, sinQ, True, prior='tKS')
        TKS, TQS = b.cnt['tKS'], b.cnt['tQS']

        @block.sync
        def _(sy):
            b.w(sy, 'xqw', XQW)
            def load_win(sy, w):
                slot = xTw[:, (w % 2) * c.NDM * c.TOKWIN:]
                dst = bass.AP(slot.tensor, slot.offset,
                              [list(slot.ap[0]), [c.TOKWIN, c.NDM], [1, c.TOKWIN]])
                srcap = bass.AP(xqbfT_s.tensor, w * c.TOKWIN,
                                [[c.QROWS, 128], [128 * c.QROWS, c.NDM], [1, c.TOKWIN]])
                ins = sy.dma_start(out=dst, in_=srcap)
                b.inc(ins, 'xTq', 16)
            load_win(sy, 0)
            if NW_Q > 1:
                load_win(sy, 1)
            for w in range(NW_Q):
                b.w(sy, 'ropeQ', (w + 1) * c.HQ * 2)
                dst = bass.AP(qT_s.tensor, w * c.TOKWIN,
                              [[c.QROWS, 128], [128 * c.QROWS, c.HQ], [1, c.TOKWIN]])
                src2 = qTst[:, :].rearrange("p (h t) -> p h t", h=c.HQ)
                ins = sy.dma_start(out=dst, in_=src2)
                b.inc(ins, 'qtsD', 16)
                if w + 2 < NW_Q:
                    b.w(sy, 'qmm', (w + 1) * c.HQ)
                    load_win(sy, w + 2)

        @block.tensor
        def _(t):
            b.w(t, 'wqp', WQP)
            idx = 0
            for w in range(NW_Q):
                b.wr(t, 'xTq', 1, w, 2, NW_Q)
                for h in range(c.HQ):
                    slotp = PSLOT[idx % 8]
                    if idx >= 8:
                        b.w(t, 'ropeQ', (idx - 8 + 1) * 2)
                    ins = None
                    for ch in range(c.NDM):
                        ins = t.matmul(slotp[:, 0:c.TOKWIN],
                                       wq_perm[:, ch * c.D + h * c.HD: ch * c.D + (h + 1) * c.HD],
                                       xTw[:, (w % 2) * c.NDM * c.TOKWIN + ch * c.TOKWIN:
                                           (w % 2) * c.NDM * c.TOKWIN + (ch + 1) * c.TOKWIN],
                                       start=(ch == 0), stop=(ch == c.NDM - 1))
                    b.inc(ins, 'qmm')
                    idx += 1

        @block.vector
        def _(v):
            b.w(v, 'tQS', TQS)
            idx = 0
            for w in range(NW_Q):
                for h in range(c.HQ):
                    slotp = PSLOT[idx % 8]
                    b.w(v, 'qmm', idx + 1)
                    if h == 0:
                        b.w(v, 'qtsD', 16 * w)
                    pe, po = slotp[0:64, 0:c.TOKWIN], slotp[64:128, 0:c.TOKWIN]
                    cw_ = cosQ[:, w * c.TOKWIN:(w + 1) * c.TOKWIN]
                    sw_ = sinQ[:, w * c.TOKWIN:(w + 1) * c.TOKWIN]
                    de = qTst[0:64, h * c.TOKWIN:(h + 1) * c.TOKWIN]
                    do = qTst[64:128, h * c.TOKWIN:(h + 1) * c.TOKWIN]
                    v.tensor_tensor(out=s1v, in0=pe, in1=cw_, op=ALU.mult)
                    v.tensor_tensor(out=s2v, in0=po, in1=sw_, op=ALU.mult)
                    v.tensor_tensor(out=de, in0=s1v, in1=s2v, op=ALU.subtract)
                    v.tensor_tensor(out=s1v, in0=po, in1=cw_, op=ALU.mult)
                    v.tensor_tensor(out=s2v, in0=pe, in1=sw_, op=ALU.mult)
                    ins = v.tensor_tensor(out=do, in0=s1v, in1=s2v, op=ALU.add)
                    b.inc(ins, 'ropeQ', 2)
                    idx += 1

        QTSD = b.cnt['qtsD']

        @block.gpsimd
        def _(g):
            b.w(g, 'wqp', WQP)
            for ch in range(c.NDM):
                if ch >= 2:
                    b.w(g, 'wkp', ch - 1)
                ins = g.dma_start(out=stage[:, (ch % 2) * c.D:(ch % 2) * c.D + c.KVW],
                                  in_=wk[ch * 128:(ch + 1) * 128, :])
                b.inc(ins, 'wkl', 16)
            b.w(g, 'wkp', c.NDM)
            for ch in range(c.NDM):
                ins = g.dma_start(out=wkv[:, (c.NDM + ch) * c.KVW:(c.NDM + ch + 1) * c.KVW],
                                  in_=wv[ch * 128:(ch + 1) * 128, :])
                b.inc(ins, 'wvl', 16)
        WVL = b.cnt['wvl']

        @block.vector
        def _(v):
            b.w(v, 'qtsD', QTSD)
            for ch in range(c.NDM):
                b.wr(v, 'wkl', 1, ch, 2, c.NDM)
                slot = stage[:, (ch % 2) * c.D:(ch % 2) * c.D + c.KVW]
                ins = None
                for h in range(c.HKV):
                    ev, od = eo_ap(slot[:, h * c.HD:(h + 1) * c.HD], c.HD)
                    dst = wkv[:, ch * c.KVW + h * c.HD: ch * c.KVW + (h + 1) * c.HD]
                    ins = v.tensor_copy(out=dst[:, 0:64], in_=ev)
                    ins = v.tensor_copy(out=dst[:, 64:128], in_=od)
                b.inc(ins, 'wkp')
        WKP = b.cnt['wkp']

        @block.sync
        def _(sy):
            b.w(sy, 'xbw', XBW)
            b.w(sy, 'qmm', c.QROWS // c.TOKWIN * c.HQ)
            for w in range(NW_K):
                if w >= 2:
                    b.w(sy, 'vmm', (w - 1) * TT_W)
                slot = xTw[:, (w % 2) * c.NDM * c.TOKWIN:]
                dst = bass.AP(slot.tensor, slot.offset,
                              [list(slot.ap[0]), [c.TOKWIN, c.NDM], [1, c.TOKWIN]])
                srcap = bass.AP(xbfT_s.tensor, w * c.TOKWIN,
                                [[c.SEQ, 128], [128 * c.SEQ, c.NDM], [1, c.TOKWIN]])
                ins = sy.dma_start(out=dst, in_=srcap)
                b.inc(ins, 'xTb', 16)

        @block.tensor
        def _(t):
            b.w(t, 'wkp', WKP)
            b.w(t, 'wvl', WVL)
            b.w(t, 'ropeQ', b.cnt['ropeQ'])
            for w in range(NW_K):
                b.wr(t, 'xTb', 1, w, 2, NW_K)
                if w >= 2:
                    b.w(t, 'ropeK', (w - 1) * c.HKV * 2)
                if w >= 1:
                    b.w(t, 'vev', w * TT_W)
                for kv in range(c.HKV):
                    slotp = PSLOT[(w % 2) * 4 + kv]
                    ins = None
                    for ch in range(c.NDM):
                        ins = t.matmul(slotp[:, 0:c.TOKWIN],
                                       wkv[:, ch * c.KVW + kv * c.HD: ch * c.KVW + (kv + 1) * c.HD],
                                       xTw[:, (w % 2) * c.NDM * c.TOKWIN + ch * c.TOKWIN:
                                           (w % 2) * c.NDM * c.TOKWIN + (ch + 1) * c.TOKWIN],
                                       start=(ch == 0), stop=(ch == c.NDM - 1))
                    b.inc(ins, 'kmm')
                b.w(t, 'ropeK', w * c.HKV * 2)
                if w >= 2:
                    b.w(t, 'vev', (w - 1) * TT_W)
                for tt in range(TT_W):
                    slotp = PSLOT[(1 - (w % 2)) * 4 + tt % 4]
                    ins = None
                    for ch in range(c.NDM):
                        ins = t.matmul(slotp[:, 0:c.KVW],
                                       xTw[:, (w % 2) * c.NDM * c.TOKWIN + ch * c.TOKWIN + tt * 128:
                                           (w % 2) * c.NDM * c.TOKWIN + ch * c.TOKWIN + tt * 128 + 128],
                                       wkv[:, (c.NDM + ch) * c.KVW:(c.NDM + ch + 1) * c.KVW],
                                       start=(ch == 0), stop=(ch == c.NDM - 1))
                    b.inc(ins, 'vmm')

        @block.vector
        def _(v):
            b.w(v, 'tKS', TKS)
            for w in range(NW_K):
                for kv in range(c.HKV):
                    slotp = PSLOT[(w % 2) * 4 + kv]
                    b.w(v, 'kmm', w * c.HKV + kv + 1)
                    pe, po = slotp[0:64, 0:c.TOKWIN], slotp[64:128, 0:c.TOKWIN]
                    cw_ = cosK[:, w * c.TOKWIN:(w + 1) * c.TOKWIN]
                    sw_ = sinK[:, w * c.TOKWIN:(w + 1) * c.TOKWIN]
                    base = kv * c.SEQ + w * c.TOKWIN
                    de, do = kT[0:64, base:base + c.TOKWIN], kT[64:128, base:base + c.TOKWIN]
                    v.tensor_tensor(out=s1v, in0=pe, in1=cw_, op=ALU.mult)
                    v.tensor_tensor(out=s2v, in0=po, in1=sw_, op=ALU.mult)
                    v.tensor_tensor(out=de, in0=s1v, in1=s2v, op=ALU.subtract)
                    v.tensor_tensor(out=s1v, in0=po, in1=cw_, op=ALU.mult)
                    v.tensor_tensor(out=s2v, in0=pe, in1=sw_, op=ALU.mult)
                    ins = v.tensor_tensor(out=do, in0=s1v, in1=s2v, op=ALU.add)
                    b.inc(ins, 'ropeK', 2)
                for tt in range(TT_W):
                    slotp = PSLOT[(1 - (w % 2)) * 4 + tt % 4]
                    b.w(v, 'vmm', w * TT_W + tt + 1)
                    ins = v.tensor_copy(out=vT[:, (w * TT_W + tt) * c.KVW:(w * TT_W + tt + 1) * c.KVW],
                                        in_=slotp[:, 0:c.KVW])
                    b.inc(ins, 'vev')
        KMM_ALL, VEV_ALL, ROPEK_ALL = b.cnt['kmm'], b.cnt['vev'], b.cnt['ropeK']

        pre_jobs = []
        for (wsrc, wdst, rows, cols) in ((wgT, wgT_s, c.D, c.FF),
                                         (wuT, wuT_s, c.D, c.FF),
                                         (wdT, wdT_s, c.FF, c.D)):
            wcols = min(cols, PREW)
            for rtile in range(rows // 128):
                for piece in range(cols // wcols):
                    pre_jobs.append((wsrc, wdst, rtile, piece * wcols, wcols))
        NPRE = len(pre_jobs)
        pre_cur = {'ld': 0, 'sg': 0}

        def drain_pre_loads(g, n):
            j0 = pre_cur['ld']
            for j in range(j0, min(j0 + n, NPRE)):
                wsrc, wdst, rt, w0, wc = pre_jobs[j]
                slot = wpre[:, (j % 4) * PREW:(j % 4) * PREW + wc]
                if j >= 4:
                    b.w(g, 'prw', 16 * (j - 3))
                ins = g.dma_start(out=slot, in_=wsrc[rt * 128:(rt + 1) * 128, w0:w0 + wc])
                b.inc(ins, 'prl%d' % (j % 4), 16)
            pre_cur['ld'] = min(j0 + n, NPRE)

        def drain_pre_signs(s, n):
            j0 = pre_cur['sg']
            for j in range(j0, min(j0 + n, NPRE)):
                wsrc, wdst, rt, w0, wc = pre_jobs[j]
                slot = wpre[:, (j % 4) * PREW:(j % 4) * PREW + wc]
                b.w(s, 'prl%d' % (j % 4), 16 * (j // 4 + 1))
                ins = s.sign(out=slot, in_=slot)
                b.inc(ins, 'prs')
                b.w(s, 'prs', b.cnt['prs'])
                ins = s.dma_start(out=wdst[rt * 128:(rt + 1) * 128, w0:w0 + wc], in_=slot)
                b.inc(ins, 'prw', 16)
            pre_cur['sg'] = min(j0 + n, NPRE)
        N_LD_SUB = -(-NPRE // c.NSUB)
        N_SG_UNIT = -(-NPRE // NUNIT)

        @block.sync
        def _(sy):
            b.w(sy, 'kmm', KMM_ALL)
            b.w(sy, 'vmm', NW_K * TT_W)
            b.w(sy, 'qtsD', QTSD)

            def qload(sub):
                slot = qTsub[:, (sub % 2) * c.HQ * c.SUBQ:]
                for h in range(c.HQ):
                    ins = sy.dma_start(
                        out=slot[:, h * c.SUBQ:(h + 1) * c.SUBQ],
                        in_=qT_s[h * 128:(h + 1) * 128, sub * c.SUBQ:(sub + 1) * c.SUBQ])
                    b.inc(ins, 'qsl', 16)
            qload(0)
            if c.NSUB > 1:
                qload(1)
            for sub in range(c.NSUB):
                for uu in range(UPS):
                    u = sub * UPS + uu
                    bc = bcslot[u % 2]
                    b.w(sy, 'rcp', u + 1)
                    ins = sy.dma_start(out=invh_s[u:u + 1, 0:c.CW], in_=bc[0:1, 0:c.CW])
                    b.inc(ins, 'ivo', 16)
                    b.w(sy, 'ivo', 16 * (u + 1))
                    ins = sy.dma_start(out=bc[:, 0:c.CW],
                                       in_=invh_s[u:u + 1, 0:c.CW].broadcast_to([128, c.CW]))
                    b.inc(ins, 'ivb', 16)
                if sub + 2 < c.NSUB:
                    b.w(sy, 'smm', (sub + 1) * UPS * NCH)
                    qload(sub + 2)

        @block.gpsimd
        def _(g):
            for sub in range(c.NSUB):
                drain_pre_loads(g, N_LD_SUB)
                for qp in range(c.QTS // 2):
                    for ch in range(c.HQ):
                        j = (sub * (c.QTS // 2) + qp) * c.HQ + ch
                        if j >= 2:
                            b.w(g, 'woc', j - 1)
                        slot = woring[:, (j % 2) * c.D:(j % 2) * c.D + c.D]
                        ins = g.dma_start(out=slot, in_=wo[ch * 128:(ch + 1) * 128, :])
                        b.inc(ins, 'wol0' if j % 2 == 0 else 'wol1', 16)
                    for qti in (qp * 2, qp * 2 + 1):
                        qt = sub * c.QTS + qti
                        if qt >= 1:
                            b.w(g, 'epi', qt)
                        ins = g.dma_start(out=xorig[:, :], in_=xq[qt * 128:(qt + 1) * 128, :])
                        b.inc(ins, 'xol', 16)
            drain_pre_loads(g, NPRE)

        @block.tensor
        def _(t):
            b.w(t, 'tab', TAB0)
            b.w(t, 'kmm', KMM_ALL)
            b.w(t, 'ropeK', ROPEK_ALL)
            b.w(t, 'vev', VEV_ALL)
            for u in range(NUNIT):
                sub, uu = u // UPS, u % UPS
                kv, hg = uu // (c.GRP // c.GH), uu % (c.GRP // c.GH)
                if uu == 0:
                    b.wr(t, 'qsl', c.HQ, sub, 2, c.NSUB)
                    if sub >= 1:
                        b.w(t, 'ln1a', sub * c.QTS)
                qs = qTsub[:, (sub % 2) * c.HQ * c.SUBQ + (kv * c.GRP + hg * c.GH) * c.SUBQ:
                           (sub % 2) * c.HQ * c.SUBQ + (kv * c.GRP + (hg + 1) * c.GH) * c.SUBQ]
                def av_mm(i2, last):
                    gi2 = u * NCH + i2
                    pi = gi2 % PR
                    ins2 = None
                    for n in range(max(1, c.CW // N5)):
                        wdt = min(N5, c.CW)
                        ins2 = t.matmul(pb[:, (u % 2) * c.CW + n * N5:(u % 2) * c.CW + n * N5 + wdt],
                                        vT[:, i2 * c.KVW + kv * c.HD:
                                           i2 * c.KVW + (kv + 1) * c.HD],
                                        pring[:, pi * c.CW + n * N5: pi * c.CW + n * N5 + wdt],
                                        start=(i2 == 0), stop=last)
                    b.inc(ins2, 'avm')
                for i in range(NCH):
                    gi = u * NCH + i
                    slotp = pa[:, (gi % 2) * c.CW:(gi % 2) * c.CW + c.CW]
                    if gi >= 2:
                        b.w(t, 'exp', gi - 1)
                    if i == 1 and u >= 1:
                        b.w(t, 'srow', u)
                    ins = None
                    for n in range(max(1, c.CW // N5)):
                        wdt = min(N5, c.CW)
                        ins = t.matmul(slotp[:, n * N5:n * N5 + wdt],
                                       kT[:, kv * c.SEQ + i * 128: kv * c.SEQ + (i + 1) * 128],
                                       qs[:, n * N5:n * N5 + wdt],
                                       start=True, stop=True)
                    b.inc(ins, 'smm')
                    if i >= 2:
                        if i == 2 and u >= 2:
                            b.w(t, 'divd', (u - 1) * c.GH)
                        av_mm(i - 2, False)
                gi = u * NCH + NCH
                b.w(t, 'exp', gi - 1)
                if NCH == 2 and u >= 2:
                    b.w(t, 'divd', (u - 1) * c.GH)
                av_mm(NCH - 2, False)
                b.w(t, 'exp', gi)
                av_mm(NCH - 1, True)
                b.w(t, 'sadd', (u + 1) * NCH)
                srow_base = ((NCH - 1) % 2) * c.CW
                ins = None
                for n in range(max(1, c.CW // N5)):
                    wdt = min(N5, c.CW)
                    ins = t.matmul(pa[0:1, srow_base + n * N5: srow_base + n * N5 + wdt],
                                   onesc[:, :], sacc[:, n * N5:n * N5 + wdt],
                                   start=True, stop=True)
                b.inc(ins, 'smr')
                if uu == UPS - 1:
                    b.w(t, 'divd', (sub + 1) * c.HKV * c.GRP)
                    for qp in range(c.QTS // 2):
                        if sub * (c.QTS // 2) + qp >= 1:
                            b.w(t, 'ln1a', (sub * (c.QTS // 2) + qp) * 2)
                        ins = None
                        for ch in range(c.HQ):
                            j = (sub * (c.QTS // 2) + qp) * c.HQ + ch
                            b.w(t, 'wol0' if j % 2 == 0 else 'wol1', 16 * (j // 2 + 1))
                            wslot = woring[:, (j % 2) * c.D:(j % 2) * c.D + c.D]
                            for half in range(2):
                                qti = qp * 2 + half
                                pp = pa if half == 0 else pb
                                for n in range(c.D // N5):
                                    ins = t.matmul(pp[:, n * N5:(n + 1) * N5],
                                                   attnT[:, ch * c.SUBQ + qti * 128: ch * c.SUBQ + qti * 128 + 128],
                                                   wslot[:, n * N5:(n + 1) * N5],
                                                   start=(ch == 0), stop=(ch == c.HQ - 1))
                            b.inc(ins, 'woc')

        @block.scalar
        def _(s):
            b.w(s, 'tab', TAB0)
            for u in range(NUNIT):
                sub, uu = u // UPS, u % UPS
                for i in range(NCH):
                    gi = u * NCH + i
                    b.w(s, 'smm', gi + 1)
                    if gi >= PR:
                        b.w(s, 'avm', gi - PR + 1)
                        b.w(s, 'sadd', gi - PR + 1)
                    pi = gi % PR
                    slotp = pa[:, (gi % 2) * c.CW:(gi % 2) * c.CW + c.CW]
                    ins = s.activation(out=pring[:, pi * c.CW:(pi + 1) * c.CW],
                                       in_=slotp, func=AF.Exp,
                                       bias=float(-c.SHIFT), scale=1.0)
                    b.inc(ins, 'exp')
                drain_pre_signs(s, N_SG_UNIT)
                b.w(s, 'smr', u + 1)
                if u >= 2:
                    b.w(s, 'divd', (u - 1) * c.GH)
                srow_base = ((NCH - 1) % 2) * c.CW
                ins = s.copy(out=bcslot[u % 2][0:1, 0:c.CW], in_=pa[0:1, srow_base:srow_base + c.CW])
                b.inc(ins, 'srow')
                if uu == UPS - 1:
                    for qti in range(c.QTS):
                        qt = sub * c.QTS + qti
                        b.w(s, 'ln1a', qt + 1)
                        ins = s.activation(out=tbufB[:, :], in_=tbufA[:, :], func=AF.Square,
                                           accum_out=stats[:, 1:2])
                        b.inc(ins, 'ln1b')
                        b.w(s, 'ln1c', qt + 1)
                        ins = s.activation(out=stats[:, 5:6], in_=stats[:, 4:5], func=AF.Ln,
                                           bias=float(c.eps1), scale=1.0)
                        b.f(s, ins, 'sf')
                        ins = s.activation(out=stats[:, 5:6], in_=stats[:, 5:6], func=AF.Exp,
                                           scale=-0.5)
                        b.inc(ins, 'ln1d')
                        b.w(s, 'ln1e', qt + 1)
                        ins = s.activation(out=tbufB[:, :], in_=tbufA[:, :], func=AF.Identity,
                                           bias=stats[:, 6:7], scale=stats[:, 7:8])
                        b.inc(ins, 'epi')
                        b.w(s, 'epi', b.cnt['epi'])
                        ins = s.dma_start(out=h1_s[qt * 128:(qt + 1) * 128, :], in_=tbufB[:, :])
                        b.inc(ins, 'h1w', 16)
            drain_pre_signs(s, NPRE)

        @block.vector
        def _(v):
            for u in range(NUNIT):
                sub, uu = u // UPS, u % UPS
                kv, hg = uu // (c.GRP // c.GH), uu % (c.GRP // c.GH)
                for i in range(NCH):
                    gi = u * NCH + i
                    b.w(v, 'exp', gi + 1)
                    pi = gi % PR
                    if i == 0:
                        ins = v.tensor_copy(out=sacc[:, 0:c.CW],
                                            in_=pring[:, pi * c.CW:(pi + 1) * c.CW])
                    else:
                        ins = v.tensor_tensor(out=sacc[:, 0:c.CW], in0=sacc[:, 0:c.CW],
                                              in1=pring[:, pi * c.CW:(pi + 1) * c.CW],
                                              op=ALU.add)
                    b.inc(ins, 'sadd')
                b.w(v, 'srow', u + 1)
                ins = v.reciprocal(out=bcslot[u % 2][0:1, 0:c.CW], in_=bcslot[u % 2][0:1, 0:c.CW])
                b.inc(ins, 'rcp')
                for j in range(c.GH):
                    b.w(v, 'ivb', 16 * (u + 1))
                    b.w(v, 'avm', (u + 1) * NCH)
                    h = kv * c.GRP + hg * c.GH + j
                    ins = v.tensor_tensor(
                        out=attnT[:, h * c.SUBQ:(h + 1) * c.SUBQ],
                        in0=pb[:, (u % 2) * c.CW + j * c.SUBQ:(u % 2) * c.CW + (j + 1) * c.SUBQ],
                        in1=bcslot[u % 2][:, j * c.SUBQ:(j + 1) * c.SUBQ],
                        op=ALU.mult)
                    b.inc(ins, 'divd')
                if uu == UPS - 1:
                    for qti in range(c.QTS):
                        qt = sub * c.QTS + qti
                        pp = pa if qti % 2 == 0 else pb
                        b.w(v, 'woc', (sub * (c.QTS // 2) + qti // 2 + 1) * c.HQ)
                        b.w(v, 'xol', 16 * (qt + 1))
                        if qt >= 1:
                            b.w(v, 'epi', qt)
                        v.tensor_tensor(out=tbufA[:, :], in0=pp[:, 0:c.D], in1=xorig[:, :], op=ALU.add)
                        ins = v.reduce_sum(out=stats[:, 0:1], in_=tbufA[:, :], axis=AX.X)
                        b.inc(ins, 'ln1a')
                        b.w(v, 'ln1b', qt + 1)
                        ins = v.tensor_scalar_mul(out=stats[:, 2:3], in0=stats[:, 0:1], scalar1=float(1.0 / c.D))
                        v.tensor_scalar_mul(out=stats[:, 3:4], in0=stats[:, 1:2], scalar1=float(1.0 / c.D))
                        b.f(v, ins, 'vf')
                        ins = v.tensor_tensor(out=stats[:, 4:5], in0=stats[:, 2:3], in1=stats[:, 2:3], op=ALU.mult)
                        b.f(v, ins, 'vf')
                        ins = v.tensor_tensor(out=stats[:, 4:5], in0=stats[:, 3:4], in1=stats[:, 4:5], op=ALU.subtract)
                        b.inc(ins, 'ln1c')
                        b.w(v, 'ln1d', qt + 1)
                        ins = v.tensor_tensor(out=stats[:, 6:7], in0=stats[:, 2:3], in1=stats[:, 5:6], op=ALU.mult)
                        b.f(v, ins, 'vf')
                        ins = v.tensor_scalar_mul(out=stats[:, 6:7], in0=stats[:, 6:7], scalar1=-1.0)
                        b.f(v, ins, 'vf')
                        ins = v.tensor_copy(out=stats[:, 7:8], in_=stats[:, 5:6])
                        b.inc(ins, 'ln1e')

        H1W = b.cnt['h1w']
        PRW = b.cnt['prw']
        assert pre_cur['ld'] == NPRE and pre_cur['sg'] == NPRE

        if c.dbg:
            dbg_st = nc.dram_tensor("dbg_st", [128, 4 * c.NQT], F32, kind="ExternalOutput").ap()

            @block.sync
            def _(sy):
                b.w(sy, 'ln1e', c.NQT)
                ins = sy.dma_start(out=dbg_st[:, :], in_=statlog[:, :])
                b.inc(ins, 'dbg3', 16)
                b.w(sy, 'dbg3', b.cnt['dbg3'])

            dbg_cos = nc.dram_tensor("dbg_cos", [128, c.SEQ + c.QROWS], F32, kind="ExternalOutput").ap()

            @block.sync
            def _(sy):
                b.w(sy, 'tKS', TKS)
                b.w(sy, 'tQS', TQS)
                ins = sy.dma_start(out=dbg_cos[:, 0:c.SEQ], in_=arD[:, 0:c.SEQ])
                b.inc(ins, 'dbg2', 16)
                ins = sy.dma_start(out=dbg_cos[:, c.SEQ:], in_=arD[:, c.SEQ:c.SEQ + c.QROWS])
                b.inc(ins, 'dbg2', 16)
                b.w(sy, 'dbg2', 32)

            dbg_attnT = nc.dram_tensor("dbg_attnT", [128, c.HQ * c.SUBQ], BF16, kind="ExternalOutput").ap()
            dbg_kT = nc.dram_tensor("dbg_kT", [128, c.HKV * c.SEQ], BF16, kind="ExternalOutput").ap()
            dbg_vT = nc.dram_tensor("dbg_vT", [128, c.NKC * c.KVW], BF16, kind="ExternalOutput").ap()
            dbg_sacc = nc.dram_tensor("dbg_sacc", [128, c.CW], F32, kind="ExternalOutput").ap()
            dbg_inv = nc.dram_tensor("dbg_inv", [1, c.CW], F32, kind="ExternalOutput").ap()
            dbg_p0 = nc.dram_tensor("dbg_p0", [128, c.CW], BF16, kind="ExternalOutput").ap()

            @block.sync
            def _(sy):
                b.w(sy, 'divd', c.GRP)
                b.w(sy, 'sadd', NCH)
                for ap_, dst in ((attnT, dbg_attnT), (kT, dbg_kT), (vT, dbg_vT),
                                 (sacc[:, 0:c.CW], dbg_sacc), (bcslot[0][0:1, 0:c.CW], dbg_inv),
                                 (pring[:, 0:c.CW], dbg_p0)):
                    ins = sy.dma_start(out=dst[:, :], in_=ap_)
                    b.inc(ins, 'dbgw', 16)
                b.w(sy, 'dbgw', b.cnt['dbgw'])

        @block.sync
        def _(sy):
            b.w(sy, 'h1w', H1W)
            for qt in range(c.NQT):
                if qt >= 1:
                    b.w(sy, 'qzd', 16 * qt)
                ins = sy.dma_start(out=tbufA[:, :], in_=h1_s[qt * 128:(qt + 1) * 128, :])
                b.inc(ins, 'h1r', 16)

        @block.vector
        def _(v):
            b.w(v, 'epi', c.NQT)
            b.w(v, 'sadd', b.cnt['sadd'])
            b.w(v, 'exp', b.cnt['exp'])
            for qt in range(c.NQT):
                b.w(v, 'h1r', 16 * (qt + 1))
                ins = v.reduce_sum(out=stats[:, 0:1], in_=tbufA[:, :], axis=AX.X)
                b.inc(ins, 'ln2a')
                b.w(v, 'ln2b', qt + 1)
                ins = v.tensor_scalar_mul(out=stats[:, 2:3], in0=stats[:, 0:1], scalar1=float(1.0 / c.D))
                v.tensor_scalar_mul(out=stats[:, 3:4], in0=stats[:, 1:2], scalar1=float(1.0 / c.D))
                b.f(v, ins, 'vf')
                ins = v.tensor_tensor(out=stats[:, 4:5], in0=stats[:, 2:3], in1=stats[:, 2:3], op=ALU.mult)
                b.f(v, ins, 'vf')
                ins = v.tensor_tensor(out=stats[:, 4:5], in0=stats[:, 3:4], in1=stats[:, 4:5], op=ALU.subtract)
                b.inc(ins, 'ln2c')
                b.w(v, 'ln2d', qt + 1)
                ins = v.tensor_tensor(out=stats[:, 6:7], in0=stats[:, 2:3], in1=stats[:, 5:6], op=ALU.mult)
                b.f(v, ins, 'vf')
                ins = v.tensor_scalar_mul(out=stats[:, 6:7], in0=stats[:, 6:7], scalar1=-1.0)
                b.f(v, ins, 'vf')
                ins = v.tensor_copy(out=stats[:, 7:8], in_=stats[:, 5:6])
                b.inc(ins, 'ln2e')
                b.w(v, 'ln2f', qt + 1)
                ins = v.reduce_max(out=stats[:, 0:1], in_=tbufB[:, :], axis=AX.X,
                             apply_absolute_value=True)
                b.f(v, ins, 'vf')
                ins = v.reciprocal(out=stats[:, 1:2], in_=stats[:, 0:1])
                b.f(v, ins, 'vf')
                ins = v.tensor_scalar_mul(out=stats[:, 1:2], in0=stats[:, 1:2], scalar1=127.0)
                b.f(v, ins, 'vf')
                v.reciprocal(out=invs_sb[:, qt:qt + 1], in_=stats[:, 1:2])
                if qt >= 1:
                    b.w(v, 'qzd', 16 * qt)
                v.tensor_scalar(out=xorig[:, :], in0=tbufB[:, :], scalar1=stats[:, 1:2],
                                scalar2=384.0, op0=ALU.mult, op1=ALU.add)
                v.tensor_copy(out=qtempi[:, :], in_=xorig[:, :])
                v.tensor_copy(out=tbufB[:, :], in_=qtempi[:, :])
                ins = v.tensor_tensor(out=xorig[:, :], in0=xorig[:, :], in1=tbufB[:, :], op=ALU.subtract)
                b.f(v, ins, 'vf')
                v.tensor_single_scalar(out=xorig[:, :], in_=xorig[:, :], scalar=0.5, op=ALU.is_ge)
                ins = v.scalar_tensor_tensor(out=xqstage, in0=tbufB[:, :], scalar=-384.0,
                                             in1=xorig[:, :], op0=ALU.add, op1=ALU.add)
                b.inc(ins, 'qz')

        @block.scalar
        def _(s):
            for qt in range(c.NQT):
                b.w(s, 'ln2a', qt + 1)
                ins = s.activation(out=tbufB[:, :], in_=tbufA[:, :], func=AF.Square,
                                   accum_out=stats[:, 1:2])
                b.inc(ins, 'ln2b')
                b.w(s, 'ln2c', qt + 1)
                ins = s.activation(out=stats[:, 5:6], in_=stats[:, 4:5], func=AF.Ln,
                             bias=float(c.eps2), scale=1.0)
                b.f(s, ins, 'sf')
                ins = s.activation(out=stats[:, 5:6], in_=stats[:, 5:6], func=AF.Exp,
                                   scale=-0.5)
                b.inc(ins, 'ln2d')
                b.w(s, 'ln2e', qt + 1)
                ins = s.activation(out=tbufB[:, :], in_=tbufA[:, :], func=AF.Identity,
                                   bias=stats[:, 6:7], scale=stats[:, 7:8])
                b.inc(ins, 'ln2f')
                b.w(s, 'qz', qt + 1)
                ins = s.dma_start(out=xqn_s[qt * 128:(qt + 1) * 128, :], in_=xqstage)
                b.inc(ins, 'qzd', 16)
        QZD = b.cnt['qzd']

        GPP = c.D_PAIR // c.GU_GROUP

        @block.sync
        def _(sy):
            b.w(sy, 'qzd', QZD)
            b.w(sy, 'prw', PRW)
            dstT = bass.AP(invs_h.tensor, 0, [[1, 128], [128, c.NQT]])
            ins = sy.dma_start(out=dstT, in_=invs_sb[:, 0:c.NQT], allow_slow_non_contiguous=True)
            b.inc(ins, 'ivw', 16)
            b.w(sy, 'ivw', 16)
            for fs in range(c.NFSUB):
                if fs >= 1:
                    b.w(sy, 'pmul', fs * c.NGRP * c.GU_GROUP)
                ins = sy.dma_start(out=ibc, in_=invs_h[0:1, fs * c.FSUB:(fs + 1) * c.FSUB]
                                   .broadcast_to([128, c.FSUB]))
                b.inc(ins, 'ibl', 16)
                if fs >= 1:
                    b.w(sy, 'gmm', fs * c.NGRP * c.GU_GROUP)
                for ch in range(c.NDM):
                    ins = sy.dma_start(out=xqT[:, ch * c.FSUB:(ch + 1) * c.FSUB],
                                       in_=xqn_s[fs * c.FSUB:(fs + 1) * c.FSUB,
                                                 ch * 128:(ch + 1) * 128], transpose=True)
                    b.inc(ins, 'xqTl', 16)
                for pr_ in range(c.NPAIR):
                    P = fs * c.NPAIR + pr_
                    for gidx in range(pr_ * (c.D_PAIR // c.GU_GROUP),
                                      (pr_ + 1) * (c.D_PAIR // c.GU_GROUP)):
                        GG = fs * c.NGRP + gidx
                        b.w(sy, 'gmm', GG * c.GU_GROUP)
                        dstG = bass.AP(gubufG.tensor, gubufG.offset,
                                       [list(gubufG.ap[0]), [c.GW, c.NDM], [1, c.GW]])
                        srcG = bass.AP(wgT_s.tensor, gidx * c.GW,
                                       [[c.FF, 128], [128 * c.FF, c.NDM], [1, c.GW]])
                        ins = sy.dma_start(out=dstG, in_=srcG)
                        b.inc(ins, 'wgl', 16)
                        b.w(sy, 'umm', GG * c.GU_GROUP)
                        dstU = bass.AP(gubufU.tensor, gubufU.offset,
                                       [list(gubufU.ap[0]), [c.GW, c.NDM], [1, c.GW]])
                        srcU = bass.AP(wuT_s.tensor, gidx * c.GW,
                                       [[c.FF, 128], [128 * c.FF, c.NDM], [1, c.GW]])
                        ins = sy.dma_start(out=dstU, in_=srcU)
                        b.inc(ins, 'wul', 16)
                    b.w(sy, 'dmm', P * c.QT_F)
                    for hc in range(c.D_PAIR):
                        row = (pr_ * c.D_PAIR + hc) * 128
                        ins = sy.dma_start(out=wdbuf[:, hc * c.D:(hc + 1) * c.D],
                                           in_=wdT_s[row:row + 128, :])
                        b.inc(ins, 'wdl', 16)

        @block.tensor
        def _(t):
            b.w(t, 'woc', b.cnt['woc'])
            for fs in range(c.NFSUB):
                b.w(t, 'xqTl', 16 * c.NDM * (fs + 1))
                for gidx in range(c.NGRP):
                    GG = fs * c.NGRP + gidx
                    P = GG // GPP
                    b.w(t, 'wgl', 16 * (GG + 1))
                    b.w(t, 'wul', 16 * (GG + 1))
                    if gidx % GPP == 0:
                        b.w(t, 'flv', P * c.QT_F)
                    for ht in range(c.GU_GROUP):
                        k = GG * c.GU_GROUP + ht
                        if k >= 1:
                            b.w(t, 'aev', k)
                        ins = None
                        for ch in range(c.NDM):
                            for n in range(max(1, c.FSUB // N5)):
                                wdt = min(N5, c.FSUB)
                                ins = t.matmul(pa[:, n * N5:n * N5 + wdt],
                                               gubufG[:, ch * c.GW + ht * 128: ch * c.GW + ht * 128 + 128],
                                               xqT[:, ch * c.FSUB + n * N5: ch * c.FSUB + n * N5 + wdt],
                                               start=(ch == 0), stop=(ch == c.NDM - 1))
                        b.inc(ins, 'gmm')
                        if k >= 1:
                            b.w(t, 'pmul', k)
                        ins = None
                        UOFF = 1024
                        for n in range(max(1, c.FSUB // N5)):
                            wdt = min(N5, c.FSUB)
                            ups = pa[:, UOFF + n * N5: UOFF + n * N5 + wdt]
                            for ch in range(c.NDM):
                                ins = t.matmul(ups,
                                               gubufU[:, ch * c.GW + ht * 128: ch * c.GW + ht * 128 + 128],
                                               xqT[:, ch * c.FSUB + n * N5: ch * c.FSUB + n * N5 + wdt],
                                               start=(ch == 0), stop=(ch == c.NDM - 1))
                        b.inc(ins, 'umm')
                    if (gidx + 1) % GPP == 0:
                        b.w(t, 'pmul', (GG + 1) * c.GU_GROUP)
                        b.w(t, 'wdl', 16 * c.D_PAIR * (P + 1))
                        for qti in range(c.QT_F):
                            fi = P * c.QT_F + qti
                            pp = pb if qti % 2 == 0 else pa
                            if qti >= 2:
                                b.w(t, 'flv', fi - 1)
                            elif qti == 1 and P >= 1:
                                pass
                            ins = None
                            ps_ = (P % 2) * c.D_PAIR * c.FSUB
                            for hc in range(c.D_PAIR):
                                for n in range(c.D // N5):
                                    ins = t.matmul(pp[:, n * N5:(n + 1) * N5],
                                                   pTb[:, ps_ + hc * c.FSUB + qti * 128:
                                                       ps_ + hc * c.FSUB + qti * 128 + 128],
                                                   wdbuf[:, hc * c.D + n * N5: hc * c.D + (n + 1) * N5],
                                                   start=(hc == 0), stop=(hc == c.D_PAIR - 1))
                            b.inc(ins, 'dmm')

        @block.vector
        def _(v):
            for fs in range(c.NFSUB):
                for gidx in range(c.NGRP):
                    GG = fs * c.NGRP + gidx
                    P = GG // GPP
                    ps_ = (P % 2) * c.D_PAIR * c.FSUB
                    within = (gidx % GPP) * c.GU_GROUP
                    for ht in range(c.GU_GROUP):
                        k = GG * c.GU_GROUP + ht
                        b.w(v, 'gmm', k + 1)
                        if k >= 2:
                            b.w(v, 'sil', k - 2 + 1)
                        if gidx == 0 and ht == 0:
                            b.w(v, 'ibl', 16 * (fs + 1))
                        ins = v.tensor_tensor(out=abuf[ht % 2][:, 0:c.FSUB],
                                              in0=pa[:, 0:c.FSUB], in1=ibc, op=ALU.mult)
                        b.inc(ins, 'aev')
                        b.w(v, 'umm', k + 1)
                        b.w(v, 'sil', k + 1)
                        ups = pa[:, 1024:1024 + c.FSUB]
                        ins = v.tensor_tensor(out=pTb[:, ps_ + (within + ht) * c.FSUB:
                                                     ps_ + (within + ht + 1) * c.FSUB],
                                              in0=abuf[ht % 2][:, 0:c.FSUB], in1=ups, op=ALU.mult)
                        b.inc(ins, 'pmul')
                    if (gidx + 1) % GPP == 0:
                        for qti in range(c.QT_F):
                            fi = P * c.QT_F + qti
                            pp = pb if qti % 2 == 0 else pa
                            b.w(v, 'dmm', fi + 1)
                            b.w(v, 'flw', 16 * fi)
                            ins = v.tensor_copy(out=foutr[qti % 2][:, :], in_=pp[:, 0:c.D])
                            b.inc(ins, 'flv')

        @block.scalar
        def _(s):
            for fs in range(c.NFSUB):
                for gidx in range(c.NGRP):
                    for ht in range(c.GU_GROUP):
                        k = (fs * c.NGRP + gidx) * c.GU_GROUP + ht
                        b.w(s, 'aev', k + 1)
                        ins = s.activation(out=abuf[ht % 2][:, 0:c.FSUB],
                                           in_=abuf[ht % 2][:, 0:c.FSUB], func=AF.Silu)
                        b.inc(ins, 'sil')

        @block.gpsimd
        def _(g):
            for fs in range(c.NFSUB):
                for pr_ in range(c.NPAIR):
                    P = fs * c.NPAIR + pr_
                    for qti in range(c.QT_F):
                        fi = P * c.QT_F + qti
                        b.w(g, 'flv', fi + 1)
                        qt = fs * c.QT_F + qti
                        if pr_ >= 1:
                            b.w(g, 'flw', 16 * ((P - 1) * c.QT_F + qti + 1))
                        ins = g.dma_start(out=dout_s[qt * 128:(qt + 1) * 128, :],
                                          in_=foutr[qti % 2][:, :],
                                          accum_op=(ALU.add if pr_ >= 1 else ALU.bypass))
                        b.inc(ins, 'flw', 16)
        FLW = b.cnt['flw']

        @block.sync
        def _(sy):
            b.w(sy, 'flw', FLW)
            b.w(sy, 'sil', b.cnt['sil'])
            for qt in range(c.NQT):
                if qt >= 1:
                    b.w(sy, 'ep3', qt)
                ins = sy.dma_start(out=foutr[0][:, :], in_=dout_s[qt * 128:(qt + 1) * 128, :])
                b.inc(ins, 'dor', 16)
                ins = sy.dma_start(out=h1t[:, :], in_=h1_s[qt * 128:(qt + 1) * 128, :])
                b.inc(ins, 'dor', 16)

        @block.vector
        def _(v):
            for qt in range(c.NQT):
                b.w(v, 'dor', 32 * (qt + 1))
                v.scalar_tensor_tensor(out=foutr[1][:, :], in0=foutr[0][:, :],
                                       scalar=invs_sb[:, qt:qt + 1], in1=h1t[:, :],
                                       op0=ALU.mult, op1=ALU.add)
                ins = v.reduce_sum(out=stats[:, 0:1], in_=foutr[1][:, :], axis=AX.X)
                b.inc(ins, 'ln3a')
                b.w(v, 'ln3b', qt + 1)
                ins = v.tensor_scalar_mul(out=stats[:, 2:3], in0=stats[:, 0:1], scalar1=float(1.0 / c.D))
                v.tensor_scalar_mul(out=stats[:, 3:4], in0=stats[:, 1:2], scalar1=float(1.0 / c.D))
                b.f(v, ins, 'vf')
                ins = v.tensor_tensor(out=stats[:, 4:5], in0=stats[:, 2:3], in1=stats[:, 2:3], op=ALU.mult)
                b.f(v, ins, 'vf')
                ins = v.tensor_tensor(out=stats[:, 4:5], in0=stats[:, 3:4], in1=stats[:, 4:5], op=ALU.subtract)
                b.inc(ins, 'ln3c')
                b.w(v, 'ln3d', qt + 1)
                ins = v.tensor_tensor(out=stats[:, 6:7], in0=stats[:, 2:3], in1=stats[:, 5:6], op=ALU.mult)
                b.f(v, ins, 'vf')
                ins = v.tensor_scalar_mul(out=stats[:, 6:7], in0=stats[:, 6:7], scalar1=-1.0)
                b.f(v, ins, 'vf')
                ins = v.tensor_copy(out=stats[:, 7:8], in_=stats[:, 5:6])
                b.inc(ins, 'ln3e')

        @block.scalar
        def _(s):
            for qt in range(c.NQT):
                b.w(s, 'ln3a', qt + 1)
                ins = s.activation(out=foutr[0][:, :], in_=foutr[1][:, :], func=AF.Square,
                                   accum_out=stats[:, 1:2])
                b.inc(ins, 'ln3b')
                b.w(s, 'ln3c', qt + 1)
                ins = s.activation(out=stats[:, 5:6], in_=stats[:, 4:5], func=AF.Ln,
                             bias=float(c.eps1), scale=1.0)
                b.f(s, ins, 'sf')
                ins = s.activation(out=stats[:, 5:6], in_=stats[:, 5:6], func=AF.Exp, scale=-0.5)
                b.inc(ins, 'ln3d')
                b.w(s, 'ln3e', qt + 1)
                ins = s.activation(out=foutr[1][:, :], in_=foutr[1][:, :], func=AF.Identity,
                                   bias=stats[:, 6:7], scale=stats[:, 7:8])
                b.inc(ins, 'ln3f')

        @block.gpsimd
        def _(g):
            for qt in range(c.NQT):
                b.w(g, 'ln3f', qt + 1)
                ins = g.dma_start(out=out[qt * 128:(qt + 1) * 128, :], in_=foutr[1][:, :])
                b.inc(ins, 'outw', 16)
                g.wait_ge(b.s('outw'), 16 * (qt + 1))
                ins = g.memset(stats[0:1, 1:2], 0.0)
                b.inc(ins, 'ep3')

    es.close()
    return nc


def kernel(L0_attn_hidden_states, L0_attn_Wq, L0_attn_Wk, L0_attn_Wv,
           L0_attn_Wo, L0_ffn_W_g, L0_ffn_W_u, L0_ffn_W_d):
    cfg = Cfg()
    x = np.ascontiguousarray(np.asarray(L0_attn_hidden_states, dtype=np.float32))
    B_, S_, D_ = x.shape
    nc = build_nc(cfg)
    ws = dict(wq=np.ascontiguousarray(L0_attn_Wq, dtype=np.float32),
              wk=np.ascontiguousarray(L0_attn_Wk, dtype=np.float32),
              wv=np.ascontiguousarray(L0_attn_Wv, dtype=np.float32),
              wo=np.ascontiguousarray(L0_attn_Wo, dtype=np.float32),
              wgT=np.ascontiguousarray(np.asarray(L0_ffn_W_g, dtype=np.float32).T),
              wuT=np.ascontiguousarray(np.asarray(L0_ffn_W_u, dtype=np.float32).T),
              wdT=np.ascontiguousarray(np.asarray(L0_ffn_W_d, dtype=np.float32).T))
    in_maps = []
    for core in range(8):
        bidx, r = core // 2, core % 2
        im = dict(ws)
        im["xbT"] = np.ascontiguousarray(x[bidx].T)
        im["xq"] = np.ascontiguousarray(x[bidx, r * cfg.QROWS:(r + 1) * cfg.QROWS])
        im["xqT_i"] = np.ascontiguousarray(x[bidx, r * cfg.QROWS:(r + 1) * cfg.QROWS].T)
        im["pos"] = np.array([[float(r * cfg.QROWS)]], dtype=np.float32)
        in_maps.append(im)
    res = run_bass_kernel_spmd(nc, in_maps, core_ids=list(range(8)))
    outp = np.empty((B_, S_, D_), dtype=np.float32)
    for core in range(8):
        bidx, r = core // 2, core % 2
        outp[bidx, r * cfg.QROWS:(r + 1) * cfg.QROWS] = res.results[core]["out"]
    return outp


# version 13
# speedup vs baseline: 1.3205x; 1.0491x over previous
import numpy as np
from contextlib import ExitStack
from dataclasses import dataclass

import concourse.bass as bass
import concourse.mybir as mybir
from concourse.bass_utils import run_bass_kernel_spmd

F32 = mybir.dt.float32
BF16 = mybir.dt.bfloat16
I32 = mybir.dt.int32
AF = mybir.ActivationFunctionType
ALU = mybir.AluOpType
AX = mybir.AxisListType
PI = float(np.pi)
TWO_PI = float(2.0 * np.pi)


@dataclass
class Cfg:
    SEQ: int = 4096
    QROWS: int = 2048
    D: int = 2048
    HQ: int = 16
    HKV: int = 4
    HD: int = 128
    FF: int = 8192
    SUBQ: int = 512
    TOKWIN: int = 512
    FSUB: int = 1024
    GU_GROUP: int = 4
    D_PAIR: int = 8
    theta: float = 10000.0
    eps1: float = 1e-5
    eps2: float = 1e-8
    SHIFT: float = 20.0
    dbg: bool = False

    @property
    def NDM(self): return self.D // 128
    @property
    def GRP(self): return self.HQ // self.HKV
    @property
    def NKC(self): return self.SEQ // 128
    @property
    def NSUB(self): return self.QROWS // self.SUBQ
    GH: int = 2

    @property
    def CW(self): return self.GH * self.SUBQ
    @property
    def NQT(self): return self.QROWS // 128
    @property
    def NHT(self): return self.FF // 128
    @property
    def NFSUB(self): return self.QROWS // self.FSUB
    @property
    def KVW(self): return self.HKV * self.HD
    @property
    def QTS(self): return self.SUBQ // 128
    @property
    def NGRP(self): return self.NHT // self.GU_GROUP
    @property
    def NPAIR(self): return self.NHT // self.D_PAIR
    @property
    def GW(self): return self.GU_GROUP * 128
    @property
    def QT_F(self): return self.FSUB // 128


def eo_ap(ap128, hd):
    ev = bass.AP(ap128.tensor, ap128.offset, [list(ap128.ap[0]), [2, hd // 2]])
    od = bass.AP(ap128.tensor, ap128.offset + 1, [list(ap128.ap[0]), [2, hd // 2]])
    return ev, od


class B:
    def __init__(self, nc, es):
        self.nc, self.es = nc, es
        self.sems, self.cnt = {}, {}

    def s(self, name):
        if name not in self.sems:
            self.sems[name] = self.es.enter_context(self.nc.semaphore(name))
            self.cnt[name] = 0
        return self.sems[name]

    def inc(self, ins, name, n=1):
        ins.then_inc(self.s(name), n)
        self.cnt[name] += n
        return self.cnt[name]

    def w(self, eng, name, val):
        if val > 0:
            eng.wait_ge(self.s(name), val)

    def f(self, eng, ins, name):
        n = self.inc(ins, name)
        eng.wait_ge(self.s(name), n)

    def wr(self, eng, name, ndma, item, depth, total):
        v = 16 * ndma * min(item + depth, total)
        if v > 0:
            eng.wait_ge(self.s(name), v)


def build_nc(cfg: Cfg):
    c = cfg
    nc = bass.Bass("TRN2", target_bir_lowering=False,
                   detect_race_conditions=False)
    es = ExitStack()
    b = B(nc, es)

    xbT = nc.dram_tensor("xbT", [c.D, c.SEQ], F32, kind="ExternalInput").ap()
    xq = nc.dram_tensor("xq", [c.QROWS, c.D], F32, kind="ExternalInput").ap()
    xqT_i = nc.dram_tensor("xqT_i", [c.D, c.QROWS], F32, kind="ExternalInput").ap()
    wq = nc.dram_tensor("wq", [c.D, c.D], F32, kind="ExternalInput").ap()
    wk = nc.dram_tensor("wk", [c.D, c.KVW], F32, kind="ExternalInput").ap()
    wv = nc.dram_tensor("wv", [c.D, c.KVW], F32, kind="ExternalInput").ap()
    wo = nc.dram_tensor("wo", [c.D, c.D], F32, kind="ExternalInput").ap()
    wgT = nc.dram_tensor("wgT", [c.D, c.FF], F32, kind="ExternalInput").ap()
    wuT = nc.dram_tensor("wuT", [c.D, c.FF], F32, kind="ExternalInput").ap()
    wdT = nc.dram_tensor("wdT", [c.FF, c.D], F32, kind="ExternalInput").ap()
    pos = nc.dram_tensor("pos", [1, 1], F32, kind="ExternalInput").ap()
    out = nc.dram_tensor("out", [c.QROWS, c.D], F32, kind="ExternalOutput").ap()

    xbfT_s = nc.dram_tensor("xbfT_s", [c.D, c.SEQ], BF16).ap()
    xqbfT_s = nc.dram_tensor("xqbfT_s", [c.D, c.QROWS], BF16).ap()
    qT_s = nc.dram_tensor("qT_s", [c.D, c.QROWS], BF16).ap()
    h1_s = nc.dram_tensor("h1_s", [c.QROWS, c.D], F32, kind="ExternalOutput" if True else "Internal").ap()
    xqn_s = nc.dram_tensor("xqn_s", [c.QROWS, c.D], BF16, kind="ExternalOutput").ap()
    wgT_s = nc.dram_tensor("wgT_s", [c.D, c.FF], BF16).ap()
    wuT_s = nc.dram_tensor("wuT_s", [c.D, c.FF], BF16).ap()
    wdT_s = nc.dram_tensor("wdT_s", [c.FF, c.D], BF16).ap()
    dout_s = nc.dram_tensor("dout_s", [c.QROWS, c.D], F32, kind="ExternalOutput").ap()
    invs_h = nc.dram_tensor("invs_h", [1, c.QROWS], F32).ap()
    invh_s = nc.dram_tensor("invh_s", [64, 2048], F32).ap()

    szA = max(c.NDM * c.D, c.HKV * c.SEQ + c.NKC * c.KVW,
              2 * c.D_PAIR * c.FSUB + c.D_PAIR * c.D)
    arA = es.enter_context(nc.sbuf_tensor("arA", [128, szA], BF16))
    szB = max(2 * c.NDM * c.TOKWIN, 2 * c.HQ * c.SUBQ, c.NDM * c.FSUB)
    arB = es.enter_context(nc.sbuf_tensor("arB", [128, szB], BF16))
    szC = max(c.HQ * c.TOKWIN, 2 * c.NDM * c.KVW,
              4 * c.CW + c.HQ * c.SUBQ + 2 * c.D,
              2 * c.NDM * c.GW)
    arC = es.enter_context(nc.sbuf_tensor("arC", [128, szC], BF16))
    szD = max(c.SEQ + c.QROWS, 3 * c.D)
    arD = es.enter_context(nc.sbuf_tensor("arD", [128, szD], F32))
    PREW = c.GU_GROUP * 128
    arF = es.enter_context(nc.sbuf_tensor("arF", [128, 4 * PREW], BF16))
    sacc = es.enter_context(nc.sbuf_tensor("sacc", [128, max(c.CW, c.D)], F32))

    onesc = es.enter_context(nc.sbuf_tensor("onesc", [128, 1], F32))
    onesb = es.enter_context(nc.sbuf_tensor("onesb", [128, 1], BF16))
    cvals = [float(-c.SHIFT), float(c.eps1), float(c.eps2)]
    consts = es.enter_context(nc.sbuf_tensor("consts", [128, len(cvals)], F32))
    for _i, _v in enumerate(cvals):
        nc.const_aps.register(F32, _v, consts[:, _i:_i + 1])
    ones1 = es.enter_context(nc.sbuf_tensor("ones1", [1, 128], F32))
    stats = es.enter_context(nc.sbuf_tensor("stats", [128, 8], F32))
    invs_sb = es.enter_context(nc.sbuf_tensor("invs_sb", [128, c.NQT], F32))
    statlog = es.enter_context(nc.sbuf_tensor("statlog", [128, 4 * c.NQT], F32))
    fcol = es.enter_context(nc.sbuf_tensor("fcol", [64, 4], F32))
    fcoli = es.enter_context(nc.sbuf_tensor("fcoli", [64, 1], I32))
    poscol = es.enter_context(nc.sbuf_tensor("poscol", [64, 1], F32))
    TW = min(512, c.SEQ, c.QROWS)
    twin_i = es.enter_context(nc.sbuf_tensor("twin_i", [64, TW], I32))
    twin_f = es.enter_context(nc.sbuf_tensor("twin_f", [128, max(2048, 2 * c.TOKWIN, c.CW)], F32))

    pa = es.enter_context(nc.psum_tensor("pa", [128, 2048], F32))
    pb = es.enter_context(nc.psum_tensor("pb", [128, 2048], F32))
    PSLOT = [pa[:, i * 512:(i + 1) * 512] for i in range(4)] + \
            [pb[:, i * 512:(i + 1) * 512] for i in range(4)]

    SCALE = float(1.0 / np.sqrt(c.HD))
    NW_Q = c.QROWS // c.TOKWIN
    NW_K = c.SEQ // c.TOKWIN
    TT_W = c.TOKWIN // 128
    NXQ = c.QROWS // 128
    NXB = c.SEQ // 128
    N5 = 512
    NCH = c.NKC
    PR = 4
    NUNIT = c.NSUB * c.HKV * (c.GRP // c.GH)
    UPS = c.HKV * (c.GRP // c.GH)
    assert c.TOKWIN <= 512 and 2 * c.CW <= 2048 and c.SUBQ <= 512 and NCH % 2 == 0
    assert c.FSUB <= 2048 and c.D_PAIR % c.GU_GROUP == 0

    wq_perm = arA
    kT = arA[:, 0:c.HKV * c.SEQ]
    vT = arA[:, c.HKV * c.SEQ: c.HKV * c.SEQ + c.NKC * c.KVW]
    pTb = arA[:, 0:2 * c.D_PAIR * c.FSUB]
    wdbuf = arA[:, 2 * c.D_PAIR * c.FSUB: 2 * c.D_PAIR * c.FSUB + c.D_PAIR * c.D]

    xTw = arB
    qTsub = arB
    xqT = arB[:, 0:c.NDM * c.FSUB]

    qTst = arC[:, 0:c.HQ * c.TOKWIN]
    wkv = arC[:, 0:2 * c.NDM * c.KVW]
    pring = arC[:, 0:PR * c.CW]
    attnT = arC[:, PR * c.CW: PR * c.CW + c.HQ * c.SUBQ]
    woring = arC[:, PR * c.CW + c.HQ * c.SUBQ: PR * c.CW + c.HQ * c.SUBQ + 2 * c.D]
    gubufG = arC[:, 0:c.NDM * c.GW]
    gubufU = arC[:, c.NDM * c.GW:2 * c.NDM * c.GW]

    cosK = arD[0:64, 0:c.SEQ]
    sinK = arD[64:128, 0:c.SEQ]
    cosQ = arD[0:64, c.SEQ:c.SEQ + c.QROWS]
    sinQ = arD[64:128, c.SEQ:c.SEQ + c.QROWS]
    tbufA = arD[:, 0:c.D]
    tbufB = arD[:, c.D:2 * c.D]
    xorig = arD[:, 2 * c.D:3 * c.D]

    sacc_f = sacc[:, 0:c.D]
    qtempi = sacc[:, 0:c.D].bitcast(I32)[:, 0:c.D]
    abuf = [tbufA, tbufB]
    foutr = [xorig, sacc_f]
    h1t = tbufA
    xqstage = pring[:, 0:c.D]

    stage = arC[:, c.HQ * c.TOKWIN: c.HQ * c.TOKWIN + 2 * c.D]
    ibc = twin_f[:, 0:c.FSUB].bitcast(F32)[:, 0:c.FSUB] if False else twin_f[:, 0:c.FSUB]
    wpre = arF
    bcslot = [twin_f[:, 0:c.CW], twin_f[:, c.CW:2 * c.CW]]
    s1v = twin_f[0:64, 0:c.TOKWIN]
    s2v = twin_f[0:64, c.TOKWIN:2 * c.TOKWIN]
    iotaF = twin_f[64:128, 0:TW]
    angT = twin_f[64:128, TW:2 * TW]
    scX = twin_f[64:128, 2 * TW:3 * TW]
    scY = twin_f[64:128, 3 * TW:4 * TW]

    with nc.Block() as block:
        @block.gpsimd
        def _(g):
            ins = g.iota(twin_i[:, :], [[1, TW]], base=0, channel_multiplier=0)
            b.inc(ins, 'tab')
            ins = g.iota(fcoli[:, :], [[0, 1]], base=0, channel_multiplier=1)
            b.inc(ins, 'tab')
            ins = g.dma_start(out=poscol[:, :], in_=pos[:, :].broadcast_to([64, 1]))
            b.inc(ins, 'posl', 16)

        @block.vector
        def _(v):
            v.memset(onesc[:, :], 1.0)
            v.memset(onesb[:, :], 1.0)
            for _i, _v in enumerate(cvals):
                v.memset(consts[:, _i:_i + 1], _v)
            ins = v.memset(ones1[:, :], 1.0)
            b.inc(ins, 'tab')
        TAB0 = b.cnt['tab']

        @block.gpsimd
        def _(g):
            def cast_stream(src, dst, ntiles, width, lsem, wsem):
                for t in range(ntiles):
                    slot = stage[:, (t % 2) * c.D:(t % 2) * c.D + width] \
                        if width <= c.D else stage[:, 0:width]
                    if width <= c.D:
                        if t >= 2:
                            b.w(g, wsem, 16 * (t - 1))
                    else:
                        if t >= 1:
                            b.w(g, wsem, 16 * t)
                    ins = g.dma_start(out=slot, in_=src[t * 128:(t + 1) * 128, :])
                    b.inc(ins, lsem, 16)
                    b.w(g, lsem, 16 * (t + 1))
                    ins = g.dma_start(out=dst[t * 128:(t + 1) * 128, :], in_=slot)
                    b.inc(ins, wsem, 16)
            cast_stream(xqT_i, xqbfT_s, c.NDM, c.QROWS, 'xql', 'xqw')
            cast_stream(xbT, xbfT_s, c.NDM, c.SEQ, 'xbl', 'xbw')
        XQW, XBW = b.cnt['xqw'], b.cnt['xbw']

        @block.gpsimd
        def _(g):
            b.w(g, 'xbw', XBW)
            for ch in range(c.NDM):
                if ch >= 2:
                    b.w(g, 'wqp', ch - 1)
                ins = g.dma_start(out=stage[:, (ch % 2) * c.D:(ch % 2) * c.D + c.D],
                                  in_=wq[ch * 128:(ch + 1) * 128, :])
                b.inc(ins, 'wql', 16)

        @block.vector
        def _(v):
            for ch in range(c.NDM):
                b.wr(v, 'wql', 1, ch, 2, c.NDM)
                slot = stage[:, (ch % 2) * c.D:(ch % 2) * c.D + c.D]
                ins = None
                for h in range(c.HQ):
                    ev, od = eo_ap(slot[:, h * c.HD:(h + 1) * c.HD], c.HD)
                    dst = wq_perm[:, ch * c.D + h * c.HD: ch * c.D + (h + 1) * c.HD]
                    ins = v.tensor_single_scalar(out=dst[:, 0:64], in_=ev, scalar=SCALE, op=ALU.mult)
                    ins = v.tensor_single_scalar(out=dst[:, 64:128], in_=od, scalar=SCALE, op=ALU.mult)
                b.inc(ins, 'wqp')
        WQP = b.cnt['wqp']

        @block.vector
        def _(v):
            b.w(v, 'tab', TAB0)
            v.tensor_copy(out=fcol[:, 0:1], in_=fcoli[:, :])
            ins = v.tensor_copy(out=iotaF, in_=twin_i[:, :])
            b.inc(ins, 'tab')
        @block.scalar
        def _(s):
            b.w(s, 'tab', TAB0 + 1)
            ins = s.activation(out=fcol[:, 1:2], in_=fcol[:, 0:1], func=AF.Exp,
                               scale=float(-np.log(c.theta) / 64.0))
            b.inc(ins, 'tab')
        TAB1 = b.cnt['tab']

        def emit_tables(name, nwin, cosbuf, sinbuf, use_pos, prior=None):
            sV, sS = name + 'V', name + 'S'
            @block.vector
            def _(v):
                b.w(v, 'tab', TAB1)
                if prior is not None:
                    b.w(v, prior, b.cnt[prior])
                if use_pos:
                    b.w(v, 'posl', 16)
                for w in range(nwin):
                    ang = angT
                    if w >= 1:
                        b.w(v, sS, 2 * w)
                    if use_pos:
                        v.tensor_scalar(out=ang, in0=iotaF, scalar1=float(w * TW),
                                        scalar2=poscol[:, 0:1], op0=ALU.add, op1=ALU.add)
                    else:
                        v.tensor_single_scalar(out=ang, in_=iotaF, scalar=float(w * TW), op=ALU.add)
                    v.tensor_scalar_mul(out=ang, in0=ang, scalar1=fcol[:, 1:2])
                    v.tensor_scalar(out=scX, in0=ang, scalar1=float(1.0 / TWO_PI),
                                    scalar2=384.0, op0=ALU.mult, op1=ALU.add)
                    ib = scY.bitcast(I32)[:, 0:TW]
                    v.tensor_copy(out=ib, in_=scX)
                    v.tensor_copy(out=scY, in_=ib)
                    ins = v.tensor_tensor(out=scX, in0=scX, in1=scY, op=ALU.subtract)
                    b.f(v, ins, 'vf')
                    v.tensor_single_scalar(out=scX, in_=scX, scalar=0.5, op=ALU.is_ge)
                    v.tensor_tensor(out=scX, in0=scX, in1=scY, op=ALU.add)
                    ins = v.tensor_scalar(out=scX, in0=scX, scalar1=-384.0, scalar2=float(-TWO_PI),
                                    op0=ALU.add, op1=ALU.mult)
                    b.f(v, ins, 'vf')
                    v.tensor_tensor(out=ang, in0=ang, in1=scX, op=ALU.add)
                    ins = v.tensor_single_scalar(out=ang, in_=ang, scalar=PI, op=ALU.min)
                    b.f(v, ins, 'vf')
                    ins = v.tensor_single_scalar(out=ang, in_=ang, scalar=-PI, op=ALU.max)
                    b.inc(ins, sV)
                    b.w(v, sS, 2 * w + 1)
                    ins = v.tensor_single_scalar(out=ang, in_=ang, scalar=float(PI / 2), op=ALU.add)
                    b.f(v, ins, 'vf')
                    ins = v.tensor_single_scalar(out=scX, in_=ang, scalar=PI, op=ALU.is_gt)
                    b.f(v, ins, 'vf')
                    ins = v.scalar_tensor_tensor(out=ang, in0=scX, scalar=float(-TWO_PI),
                                                 in1=ang, op0=ALU.mult, op1=ALU.add)
                    b.inc(ins, sV)
            @block.scalar
            def _(s):
                for w in range(nwin):
                    b.w(s, sV, 2 * w + 1)
                    ins = s.activation(out=sinbuf[:, w * TW:(w + 1) * TW], in_=angT, func=AF.Sin)
                    b.inc(ins, sS)
                    b.w(s, sV, 2 * w + 2)
                    ins = s.activation(out=cosbuf[:, w * TW:(w + 1) * TW], in_=angT, func=AF.Sin)
                    b.inc(ins, sS)
            return b.cnt[sS]

        emit_tables('tK', c.SEQ // TW, cosK, sinK, False)
        emit_tables('tQ', c.QROWS // TW, cosQ, sinQ, True, prior='tKS')
        TKS, TQS = b.cnt['tKS'], b.cnt['tQS']

        @block.sync
        def _(sy):
            b.w(sy, 'xqw', XQW)
            def load_win(sy, w):
                slot = xTw[:, (w % 2) * c.NDM * c.TOKWIN:]
                dst = bass.AP(slot.tensor, slot.offset,
                              [list(slot.ap[0]), [c.TOKWIN, c.NDM], [1, c.TOKWIN]])
                srcap = bass.AP(xqbfT_s.tensor, w * c.TOKWIN,
                                [[c.QROWS, 128], [128 * c.QROWS, c.NDM], [1, c.TOKWIN]])
                ins = sy.dma_start(out=dst, in_=srcap)
                b.inc(ins, 'xTq', 16)
            load_win(sy, 0)
            if NW_Q > 1:
                load_win(sy, 1)
            for w in range(NW_Q):
                b.w(sy, 'ropeQ', (w + 1) * c.HQ * 2)
                dst = bass.AP(qT_s.tensor, w * c.TOKWIN,
                              [[c.QROWS, 128], [128 * c.QROWS, c.HQ], [1, c.TOKWIN]])
                src2 = qTst[:, :].rearrange("p (h t) -> p h t", h=c.HQ)
                ins = sy.dma_start(out=dst, in_=src2)
                b.inc(ins, 'qtsD', 16)
                if w + 2 < NW_Q:
                    b.w(sy, 'qmm', (w + 1) * c.HQ)
                    load_win(sy, w + 2)

        @block.tensor
        def _(t):
            b.w(t, 'wqp', WQP)
            idx = 0
            for w in range(NW_Q):
                b.wr(t, 'xTq', 1, w, 2, NW_Q)
                for h in range(c.HQ):
                    slotp = PSLOT[idx % 8]
                    if idx >= 8:
                        b.w(t, 'ropeQ', (idx - 8 + 1) * 2)
                    ins = None
                    for ch in range(c.NDM):
                        ins = t.matmul(slotp[:, 0:c.TOKWIN],
                                       wq_perm[:, ch * c.D + h * c.HD: ch * c.D + (h + 1) * c.HD],
                                       xTw[:, (w % 2) * c.NDM * c.TOKWIN + ch * c.TOKWIN:
                                           (w % 2) * c.NDM * c.TOKWIN + (ch + 1) * c.TOKWIN],
                                       start=(ch == 0), stop=(ch == c.NDM - 1))
                    b.inc(ins, 'qmm')
                    idx += 1

        @block.vector
        def _(v):
            b.w(v, 'tQS', TQS)
            idx = 0
            for w in range(NW_Q):
                for h in range(c.HQ):
                    slotp = PSLOT[idx % 8]
                    b.w(v, 'qmm', idx + 1)
                    if h == 0:
                        b.w(v, 'qtsD', 16 * w)
                    pe, po = slotp[0:64, 0:c.TOKWIN], slotp[64:128, 0:c.TOKWIN]
                    cw_ = cosQ[:, w * c.TOKWIN:(w + 1) * c.TOKWIN]
                    sw_ = sinQ[:, w * c.TOKWIN:(w + 1) * c.TOKWIN]
                    de = qTst[0:64, h * c.TOKWIN:(h + 1) * c.TOKWIN]
                    do = qTst[64:128, h * c.TOKWIN:(h + 1) * c.TOKWIN]
                    v.tensor_tensor(out=s1v, in0=pe, in1=cw_, op=ALU.mult)
                    v.tensor_tensor(out=s2v, in0=po, in1=sw_, op=ALU.mult)
                    v.tensor_tensor(out=de, in0=s1v, in1=s2v, op=ALU.subtract)
                    v.tensor_tensor(out=s1v, in0=po, in1=cw_, op=ALU.mult)
                    v.tensor_tensor(out=s2v, in0=pe, in1=sw_, op=ALU.mult)
                    ins = v.tensor_tensor(out=do, in0=s1v, in1=s2v, op=ALU.add)
                    b.inc(ins, 'ropeQ', 2)
                    idx += 1

        QTSD = b.cnt['qtsD']

        @block.gpsimd
        def _(g):
            b.w(g, 'wqp', WQP)
            for ch in range(c.NDM):
                if ch >= 2:
                    b.w(g, 'wkp', ch - 1)
                ins = g.dma_start(out=stage[:, (ch % 2) * c.D:(ch % 2) * c.D + c.KVW],
                                  in_=wk[ch * 128:(ch + 1) * 128, :])
                b.inc(ins, 'wkl', 16)
            b.w(g, 'wkp', c.NDM)
            for ch in range(c.NDM):
                ins = g.dma_start(out=wkv[:, (c.NDM + ch) * c.KVW:(c.NDM + ch + 1) * c.KVW],
                                  in_=wv[ch * 128:(ch + 1) * 128, :])
                b.inc(ins, 'wvl', 16)
        WVL = b.cnt['wvl']

        @block.vector
        def _(v):
            b.w(v, 'qtsD', QTSD)
            for ch in range(c.NDM):
                b.wr(v, 'wkl', 1, ch, 2, c.NDM)
                slot = stage[:, (ch % 2) * c.D:(ch % 2) * c.D + c.KVW]
                ins = None
                for h in range(c.HKV):
                    ev, od = eo_ap(slot[:, h * c.HD:(h + 1) * c.HD], c.HD)
                    dst = wkv[:, ch * c.KVW + h * c.HD: ch * c.KVW + (h + 1) * c.HD]
                    ins = v.tensor_copy(out=dst[:, 0:64], in_=ev)
                    ins = v.tensor_copy(out=dst[:, 64:128], in_=od)
                b.inc(ins, 'wkp')
        WKP = b.cnt['wkp']

        @block.sync
        def _(sy):
            b.w(sy, 'xbw', XBW)
            b.w(sy, 'qmm', c.QROWS // c.TOKWIN * c.HQ)
            for w in range(NW_K):
                if w >= 2:
                    b.w(sy, 'vmm', (w - 1) * TT_W)
                slot = xTw[:, (w % 2) * c.NDM * c.TOKWIN:]
                dst = bass.AP(slot.tensor, slot.offset,
                              [list(slot.ap[0]), [c.TOKWIN, c.NDM], [1, c.TOKWIN]])
                srcap = bass.AP(xbfT_s.tensor, w * c.TOKWIN,
                                [[c.SEQ, 128], [128 * c.SEQ, c.NDM], [1, c.TOKWIN]])
                ins = sy.dma_start(out=dst, in_=srcap)
                b.inc(ins, 'xTb', 16)

        @block.tensor
        def _(t):
            b.w(t, 'wkp', WKP)
            b.w(t, 'wvl', WVL)
            b.w(t, 'ropeQ', b.cnt['ropeQ'])
            for w in range(NW_K):
                b.wr(t, 'xTb', 1, w, 2, NW_K)
                if w >= 2:
                    b.w(t, 'ropeK', (w - 1) * c.HKV * 2)
                if w >= 1:
                    b.w(t, 'vev', w * TT_W)
                for kv in range(c.HKV):
                    slotp = PSLOT[(w % 2) * 4 + kv]
                    ins = None
                    for ch in range(c.NDM):
                        ins = t.matmul(slotp[:, 0:c.TOKWIN],
                                       wkv[:, ch * c.KVW + kv * c.HD: ch * c.KVW + (kv + 1) * c.HD],
                                       xTw[:, (w % 2) * c.NDM * c.TOKWIN + ch * c.TOKWIN:
                                           (w % 2) * c.NDM * c.TOKWIN + (ch + 1) * c.TOKWIN],
                                       start=(ch == 0), stop=(ch == c.NDM - 1))
                    b.inc(ins, 'kmm')
                b.w(t, 'ropeK', w * c.HKV * 2)
                if w >= 2:
                    b.w(t, 'vev', (w - 1) * TT_W)
                for tt in range(TT_W):
                    slotp = PSLOT[(1 - (w % 2)) * 4 + tt % 4]
                    ins = None
                    for ch in range(c.NDM):
                        ins = t.matmul(slotp[:, 0:c.KVW],
                                       xTw[:, (w % 2) * c.NDM * c.TOKWIN + ch * c.TOKWIN + tt * 128:
                                           (w % 2) * c.NDM * c.TOKWIN + ch * c.TOKWIN + tt * 128 + 128],
                                       wkv[:, (c.NDM + ch) * c.KVW:(c.NDM + ch + 1) * c.KVW],
                                       start=(ch == 0), stop=(ch == c.NDM - 1))
                    b.inc(ins, 'vmm')

        @block.vector
        def _(v):
            b.w(v, 'tKS', TKS)
            for w in range(NW_K):
                for kv in range(c.HKV):
                    slotp = PSLOT[(w % 2) * 4 + kv]
                    b.w(v, 'kmm', w * c.HKV + kv + 1)
                    pe, po = slotp[0:64, 0:c.TOKWIN], slotp[64:128, 0:c.TOKWIN]
                    cw_ = cosK[:, w * c.TOKWIN:(w + 1) * c.TOKWIN]
                    sw_ = sinK[:, w * c.TOKWIN:(w + 1) * c.TOKWIN]
                    base = kv * c.SEQ + w * c.TOKWIN
                    de, do = kT[0:64, base:base + c.TOKWIN], kT[64:128, base:base + c.TOKWIN]
                    v.tensor_tensor(out=s1v, in0=pe, in1=cw_, op=ALU.mult)
                    v.tensor_tensor(out=s2v, in0=po, in1=sw_, op=ALU.mult)
                    v.tensor_tensor(out=de, in0=s1v, in1=s2v, op=ALU.subtract)
                    v.tensor_tensor(out=s1v, in0=po, in1=cw_, op=ALU.mult)
                    v.tensor_tensor(out=s2v, in0=pe, in1=sw_, op=ALU.mult)
                    ins = v.tensor_tensor(out=do, in0=s1v, in1=s2v, op=ALU.add)
                    b.inc(ins, 'ropeK', 2)
                for tt in range(TT_W):
                    slotp = PSLOT[(1 - (w % 2)) * 4 + tt % 4]
                    b.w(v, 'vmm', w * TT_W + tt + 1)
                    ins = v.tensor_copy(out=vT[:, (w * TT_W + tt) * c.KVW:(w * TT_W + tt + 1) * c.KVW],
                                        in_=slotp[:, 0:c.KVW])
                    b.inc(ins, 'vev')
        KMM_ALL, VEV_ALL, ROPEK_ALL = b.cnt['kmm'], b.cnt['vev'], b.cnt['ropeK']

        pre_jobs = []
        need_g, need_u, need_d = {}, {}, {}
        NB = c.FF // PREW
        assert c.GW == PREW
        WD_RT_PER_BLK = max(1, (c.FF // 128) // NB)
        WD_PIECES = c.D // PREW
        for blk in range(NB):
            for rt in range(c.D // 128):
                pre_jobs.append((wgT, wgT_s, rt, blk * PREW, PREW))
            need_g[blk] = len(pre_jobs)
            for rt in range(c.D // 128):
                pre_jobs.append((wuT, wuT_s, rt, blk * PREW, PREW))
            need_u[blk] = len(pre_jobs)
            for rt in range(blk * WD_RT_PER_BLK, (blk + 1) * WD_RT_PER_BLK):
                for piece in range(WD_PIECES):
                    pre_jobs.append((wdT, wdT_s, rt, piece * PREW, PREW))
            need_d[blk] = len(pre_jobs)
        NPRE = len(pre_jobs)
        pre_cur = {'ld': 0, 'sg': 0}

        def drain_pre_loads(g, n):
            j0 = pre_cur['ld']
            for j in range(j0, min(j0 + n, NPRE)):
                wsrc, wdst, rt, w0, wc = pre_jobs[j]
                slot = wpre[:, (j % 4) * PREW:(j % 4) * PREW + wc]
                if j >= 4:
                    b.w(g, 'prw', 16 * (j - 3))
                ins = g.dma_start(out=slot, in_=wsrc[rt * 128:(rt + 1) * 128, w0:w0 + wc])
                b.inc(ins, 'prl%d' % (j % 4), 16)
            pre_cur['ld'] = min(j0 + n, NPRE)

        def drain_pre_signs(s, n):
            j0 = pre_cur['sg']
            for j in range(j0, min(j0 + n, NPRE)):
                wsrc, wdst, rt, w0, wc = pre_jobs[j]
                slot = wpre[:, (j % 4) * PREW:(j % 4) * PREW + wc]
                b.w(s, 'prl%d' % (j % 4), 16 * (j // 4 + 1))
                ins = s.sign(out=slot, in_=slot)
                b.inc(ins, 'prs')
                b.w(s, 'prs', b.cnt['prs'])
                ins = s.dma_start(out=wdst[rt * 128:(rt + 1) * 128, w0:w0 + wc], in_=slot)
                b.inc(ins, 'prw', 16)
            pre_cur['sg'] = min(j0 + n, NPRE)
        N_LD_SUB = -(-(NPRE // 4) // c.NSUB)
        N_SG_UNIT = -(-(NPRE // 4) // NUNIT)

        @block.sync
        def _(sy):
            b.w(sy, 'kmm', KMM_ALL)
            b.w(sy, 'vmm', NW_K * TT_W)
            b.w(sy, 'qtsD', QTSD)

            def qload(sub):
                slot = qTsub[:, (sub % 2) * c.HQ * c.SUBQ:]
                for h in range(c.HQ):
                    ins = sy.dma_start(
                        out=slot[:, h * c.SUBQ:(h + 1) * c.SUBQ],
                        in_=qT_s[h * 128:(h + 1) * 128, sub * c.SUBQ:(sub + 1) * c.SUBQ])
                    b.inc(ins, 'qsl', 16)
            qload(0)
            if c.NSUB > 1:
                qload(1)
            for sub in range(c.NSUB):
                for uu in range(UPS):
                    u = sub * UPS + uu
                    bc = bcslot[u % 2]
                    b.w(sy, 'rcp', u + 1)
                    ins = sy.dma_start(out=invh_s[u:u + 1, 0:c.CW], in_=bc[0:1, 0:c.CW])
                    b.inc(ins, 'ivo', 16)
                    b.w(sy, 'ivo', 16 * (u + 1))
                    ins = sy.dma_start(out=bc[:, 0:c.CW],
                                       in_=invh_s[u:u + 1, 0:c.CW].broadcast_to([128, c.CW]))
                    b.inc(ins, 'ivb', 16)
                if sub + 2 < c.NSUB:
                    b.w(sy, 'smm', (sub + 1) * UPS * NCH)
                    qload(sub + 2)

        @block.gpsimd
        def _(g):
            for sub in range(c.NSUB):
                drain_pre_loads(g, N_LD_SUB)
                for qp in range(c.QTS // 2):
                    for ch in range(c.HQ):
                        j = (sub * (c.QTS // 2) + qp) * c.HQ + ch
                        if j >= 2:
                            b.w(g, 'woc', j - 1)
                        slot = woring[:, (j % 2) * c.D:(j % 2) * c.D + c.D]
                        ins = g.dma_start(out=slot, in_=wo[ch * 128:(ch + 1) * 128, :])
                        b.inc(ins, 'wol0' if j % 2 == 0 else 'wol1', 16)
                    for qti in (qp * 2, qp * 2 + 1):
                        qt = sub * c.QTS + qti
                        if qt >= 1:
                            b.w(g, 'epi', qt)
                        ins = g.dma_start(out=xorig[:, :], in_=xq[qt * 128:(qt + 1) * 128, :])
                        b.inc(ins, 'xol', 16)

        @block.tensor
        def _(t):
            b.w(t, 'tab', TAB0)
            b.w(t, 'kmm', KMM_ALL)
            b.w(t, 'ropeK', ROPEK_ALL)
            b.w(t, 'vev', VEV_ALL)
            for u in range(NUNIT):
                sub, uu = u // UPS, u % UPS
                kv, hg = uu // (c.GRP // c.GH), uu % (c.GRP // c.GH)
                if uu == 0:
                    b.wr(t, 'qsl', c.HQ, sub, 2, c.NSUB)
                    if sub >= 1:
                        b.w(t, 'ln1a', sub * c.QTS)
                qs = qTsub[:, (sub % 2) * c.HQ * c.SUBQ + (kv * c.GRP + hg * c.GH) * c.SUBQ:
                           (sub % 2) * c.HQ * c.SUBQ + (kv * c.GRP + (hg + 1) * c.GH) * c.SUBQ]
                def av_mm(i2, last):
                    gi2 = u * NCH + i2
                    pi = gi2 % PR
                    ins2 = None
                    for n in range(max(1, c.CW // N5)):
                        wdt = min(N5, c.CW)
                        ins2 = t.matmul(pb[:, (u % 2) * c.CW + n * N5:(u % 2) * c.CW + n * N5 + wdt],
                                        vT[:, i2 * c.KVW + kv * c.HD:
                                           i2 * c.KVW + (kv + 1) * c.HD],
                                        pring[:, pi * c.CW + n * N5: pi * c.CW + n * N5 + wdt],
                                        start=(i2 == 0), stop=last)
                    b.inc(ins2, 'avm')
                for i in range(NCH):
                    gi = u * NCH + i
                    slotp = pa[:, (gi % 2) * c.CW:(gi % 2) * c.CW + c.CW]
                    if gi >= 2:
                        b.w(t, 'exp', gi - 1)
                    if i == 1 and u >= 1:
                        b.w(t, 'rcp', u)
                    ins = None
                    for n in range(max(1, c.CW // N5)):
                        wdt = min(N5, c.CW)
                        ins = t.matmul(slotp[:, n * N5:n * N5 + wdt],
                                       kT[:, kv * c.SEQ + i * 128: kv * c.SEQ + (i + 1) * 128],
                                       qs[:, n * N5:n * N5 + wdt],
                                       start=True, stop=True)
                    b.inc(ins, 'smm')
                    if i >= 2:
                        if i == 2 and u >= 2:
                            b.w(t, 'divd', (u - 1) * c.GH)
                        av_mm(i - 2, False)
                gi = u * NCH + NCH
                b.w(t, 'exp', gi - 1)
                if NCH == 2 and u >= 2:
                    b.w(t, 'divd', (u - 1) * c.GH)
                av_mm(NCH - 2, False)
                b.w(t, 'exp', gi)
                av_mm(NCH - 1, True)
                NPAIRS_S = (NCH - 4) // 2
                srow_base = ((NCH - 1) % 2) * c.CW
                ins = None
                if NPAIRS_S > 0:
                    b.w(t, 'sadd', u * NPAIRS_S + NPAIRS_S)
                    for n in range(max(1, c.CW // N5)):
                        wdt = min(N5, c.CW)
                        ins = t.matmul(pa[0:1, srow_base + n * N5: srow_base + n * N5 + wdt],
                                       onesc[:, :], sacc[:, n * N5:n * N5 + wdt],
                                       start=True, stop=False)
                for i2 in range(NCH - 4, NCH):
                    pi2 = (u * NCH + i2) % PR
                    for n in range(max(1, c.CW // N5)):
                        wdt = min(N5, c.CW)
                        ins = t.matmul(pa[0:1, srow_base + n * N5: srow_base + n * N5 + wdt],
                                       onesb[:, :],
                                       pring[:, pi2 * c.CW + n * N5: pi2 * c.CW + n * N5 + wdt],
                                       start=(NPAIRS_S == 0 and i2 == NCH - 4),
                                       stop=(i2 == NCH - 1))
                b.inc(ins, 'smr')
                if uu == UPS - 1:
                    b.w(t, 'divd', (sub + 1) * c.HKV * c.GRP)
                    for qp in range(c.QTS // 2):
                        if sub * (c.QTS // 2) + qp >= 1:
                            b.w(t, 'ln1a', (sub * (c.QTS // 2) + qp) * 2)
                        ins = None
                        for ch in range(c.HQ):
                            j = (sub * (c.QTS // 2) + qp) * c.HQ + ch
                            b.w(t, 'wol0' if j % 2 == 0 else 'wol1', 16 * (j // 2 + 1))
                            wslot = woring[:, (j % 2) * c.D:(j % 2) * c.D + c.D]
                            for half in range(2):
                                qti = qp * 2 + half
                                pp = pa if half == 0 else pb
                                for n in range(c.D // N5):
                                    ins = t.matmul(pp[:, n * N5:(n + 1) * N5],
                                                   attnT[:, ch * c.SUBQ + qti * 128: ch * c.SUBQ + qti * 128 + 128],
                                                   wslot[:, n * N5:(n + 1) * N5],
                                                   start=(ch == 0), stop=(ch == c.HQ - 1))
                            b.inc(ins, 'woc')

        @block.scalar
        def _(s):
            b.w(s, 'tab', TAB0)
            for u in range(NUNIT):
                sub, uu = u // UPS, u % UPS
                NPAIRS_S = (NCH - 4) // 2
                for i in range(NCH):
                    gi = u * NCH + i
                    b.w(s, 'smm', gi + 1)
                    if gi >= PR:
                        gprev = gi - PR
                        b.w(s, 'avm', gprev + 1)
                        if gprev % NCH >= NCH - 4:
                            b.w(s, 'smr', gprev // NCH + 1)
                        else:
                            b.w(s, 'sadd', (gprev // NCH) * NPAIRS_S + (gprev % NCH) // 2 + 1)
                    pi = gi % PR
                    slotp = pa[:, (gi % 2) * c.CW:(gi % 2) * c.CW + c.CW]
                    ins = s.activation(out=pring[:, pi * c.CW:(pi + 1) * c.CW],
                                       in_=slotp, func=AF.Exp,
                                       bias=float(-c.SHIFT), scale=1.0)
                    b.inc(ins, 'exp')
                drain_pre_signs(s, N_SG_UNIT)
                if uu == UPS - 1:
                    for qti in range(c.QTS):
                        qt = sub * c.QTS + qti
                        b.w(s, 'ln1a', qt + 1)
                        ins = s.activation(out=tbufB[:, :], in_=tbufA[:, :], func=AF.Square,
                                           accum_out=stats[:, 1:2])
                        b.inc(ins, 'ln1b')
                        b.w(s, 'ln1c', qt + 1)
                        ins = s.activation(out=stats[:, 5:6], in_=stats[:, 4:5], func=AF.Ln,
                                           bias=float(c.eps1), scale=1.0)
                        b.f(s, ins, 'sf')
                        ins = s.activation(out=stats[:, 5:6], in_=stats[:, 5:6], func=AF.Exp,
                                           scale=-0.5)
                        b.inc(ins, 'ln1d')
                        b.w(s, 'ln1e', qt + 1)
                        ins = s.activation(out=tbufB[:, :], in_=tbufA[:, :], func=AF.Identity,
                                           bias=stats[:, 6:7], scale=stats[:, 7:8])
                        b.inc(ins, 'epi')
                        b.w(s, 'epi', b.cnt['epi'])
                        ins = s.dma_start(out=h1_s[qt * 128:(qt + 1) * 128, :], in_=tbufB[:, :])
                        b.inc(ins, 'h1w', 16)

        @block.vector
        def _(v):
            for u in range(NUNIT):
                sub, uu = u // UPS, u % UPS
                kv, hg = uu // (c.GRP // c.GH), uu % (c.GRP // c.GH)
                NPAIRS_S = (NCH - 4) // 2
                pairbuf = sacc[:, c.CW:2 * c.CW] if 2 * c.CW <= sacc.shape[1] else None

                def emit_div(u2):
                    sub2, uu2 = u2 // UPS, u2 % UPS
                    kv2, hg2 = uu2 // (c.GRP // c.GH), uu2 % (c.GRP // c.GH)
                    for j in range(c.GH):
                        b.w(v, 'ivb', 16 * (u2 + 1))
                        b.w(v, 'avm', (u2 + 1) * NCH)
                        h = kv2 * c.GRP + hg2 * c.GH + j
                        ins2 = v.tensor_tensor(
                            out=attnT[:, h * c.SUBQ:(h + 1) * c.SUBQ],
                            in0=pb[:, (u2 % 2) * c.CW + j * c.SUBQ:(u2 % 2) * c.CW + (j + 1) * c.SUBQ],
                            in1=bcslot[u2 % 2][:, j * c.SUBQ:(j + 1) * c.SUBQ],
                            op=ALU.mult)
                        b.inc(ins2, 'divd')
                for k in range(NPAIRS_S):
                    g0, g1 = u * NCH + 2 * k, u * NCH + 2 * k + 1
                    b.w(v, 'exp', g1 + 1)
                    p0, p1 = g0 % PR, g1 % PR
                    if k == 0:
                        ins = v.tensor_tensor(out=sacc[:, 0:c.CW],
                                              in0=pring[:, p0 * c.CW:(p0 + 1) * c.CW],
                                              in1=pring[:, p1 * c.CW:(p1 + 1) * c.CW],
                                              op=ALU.add)
                    elif pairbuf is not None:
                        v.tensor_tensor(out=pairbuf, in0=pring[:, p0 * c.CW:(p0 + 1) * c.CW],
                                        in1=pring[:, p1 * c.CW:(p1 + 1) * c.CW], op=ALU.add)
                        ins = v.tensor_tensor(out=sacc[:, 0:c.CW], in0=sacc[:, 0:c.CW],
                                              in1=pairbuf, op=ALU.add)
                    else:
                        v.tensor_tensor(out=sacc[:, 0:c.CW], in0=sacc[:, 0:c.CW],
                                        in1=pring[:, p0 * c.CW:(p0 + 1) * c.CW], op=ALU.add)
                        ins = v.tensor_tensor(out=sacc[:, 0:c.CW], in0=sacc[:, 0:c.CW],
                                              in1=pring[:, p1 * c.CW:(p1 + 1) * c.CW], op=ALU.add)
                    b.inc(ins, 'sadd')
                if u >= 1 and (u - 1) % UPS != UPS - 1:
                    emit_div(u - 1)
                b.w(v, 'smr', u + 1)
                if u >= 2:
                    b.w(v, 'divd', (u - 1) * c.GH)
                srow_base = ((NCH - 1) % 2) * c.CW
                ins = v.tensor_copy(out=bcslot[u % 2][0:1, 0:c.CW],
                                    in_=pa[0:1, srow_base:srow_base + c.CW])
                b.f(v, ins, 'vf')
                ins = v.reciprocal(out=bcslot[u % 2][0:1, 0:c.CW], in_=bcslot[u % 2][0:1, 0:c.CW])
                b.inc(ins, 'rcp')
                if u == NUNIT - 1 or (u % UPS) == UPS - 1:
                    emit_div(u)
                if uu == UPS - 1:
                    for qti in range(c.QTS):
                        qt = sub * c.QTS + qti
                        pp = pa if qti % 2 == 0 else pb
                        b.w(v, 'woc', (sub * (c.QTS // 2) + qti // 2 + 1) * c.HQ)
                        b.w(v, 'xol', 16 * (qt + 1))
                        if qt >= 1:
                            b.w(v, 'epi', qt)
                        v.tensor_tensor(out=tbufA[:, :], in0=pp[:, 0:c.D], in1=xorig[:, :], op=ALU.add)
                        ins = v.reduce_sum(out=stats[:, 0:1], in_=tbufA[:, :], axis=AX.X)
                        b.inc(ins, 'ln1a')
                        b.w(v, 'ln1b', qt + 1)
                        ins = v.tensor_scalar_mul(out=stats[:, 2:3], in0=stats[:, 0:1], scalar1=float(1.0 / c.D))
                        v.tensor_scalar_mul(out=stats[:, 3:4], in0=stats[:, 1:2], scalar1=float(1.0 / c.D))
                        b.f(v, ins, 'vf')
                        ins = v.tensor_tensor(out=stats[:, 4:5], in0=stats[:, 2:3], in1=stats[:, 2:3], op=ALU.mult)
                        b.f(v, ins, 'vf')
                        ins = v.tensor_tensor(out=stats[:, 4:5], in0=stats[:, 3:4], in1=stats[:, 4:5], op=ALU.subtract)
                        b.inc(ins, 'ln1c')
                        b.w(v, 'ln1d', qt + 1)
                        ins = v.tensor_tensor(out=stats[:, 6:7], in0=stats[:, 2:3], in1=stats[:, 5:6], op=ALU.mult)
                        b.f(v, ins, 'vf')
                        ins = v.tensor_scalar_mul(out=stats[:, 6:7], in0=stats[:, 6:7], scalar1=-1.0)
                        b.f(v, ins, 'vf')
                        ins = v.tensor_copy(out=stats[:, 7:8], in_=stats[:, 5:6])
                        b.inc(ins, 'ln1e')

        H1W = b.cnt['h1w']

        if c.dbg:
            dbg_st = nc.dram_tensor("dbg_st", [128, 4 * c.NQT], F32, kind="ExternalOutput").ap()

            @block.sync
            def _(sy):
                b.w(sy, 'ln1e', c.NQT)
                ins = sy.dma_start(out=dbg_st[:, :], in_=statlog[:, :])
                b.inc(ins, 'dbg3', 16)
                b.w(sy, 'dbg3', b.cnt['dbg3'])

            dbg_cos = nc.dram_tensor("dbg_cos", [128, c.SEQ + c.QROWS], F32, kind="ExternalOutput").ap()

            @block.sync
            def _(sy):
                b.w(sy, 'tKS', TKS)
                b.w(sy, 'tQS', TQS)
                ins = sy.dma_start(out=dbg_cos[:, 0:c.SEQ], in_=arD[:, 0:c.SEQ])
                b.inc(ins, 'dbg2', 16)
                ins = sy.dma_start(out=dbg_cos[:, c.SEQ:], in_=arD[:, c.SEQ:c.SEQ + c.QROWS])
                b.inc(ins, 'dbg2', 16)
                b.w(sy, 'dbg2', 32)

            dbg_attnT = nc.dram_tensor("dbg_attnT", [128, c.HQ * c.SUBQ], BF16, kind="ExternalOutput").ap()
            dbg_kT = nc.dram_tensor("dbg_kT", [128, c.HKV * c.SEQ], BF16, kind="ExternalOutput").ap()
            dbg_vT = nc.dram_tensor("dbg_vT", [128, c.NKC * c.KVW], BF16, kind="ExternalOutput").ap()
            dbg_sacc = nc.dram_tensor("dbg_sacc", [128, c.CW], F32, kind="ExternalOutput").ap()
            dbg_inv = nc.dram_tensor("dbg_inv", [1, c.CW], F32, kind="ExternalOutput").ap()
            dbg_p0 = nc.dram_tensor("dbg_p0", [128, c.CW], BF16, kind="ExternalOutput").ap()

            @block.sync
            def _(sy):
                b.w(sy, 'divd', c.GRP)
                b.w(sy, 'sadd', NCH)
                for ap_, dst in ((attnT, dbg_attnT), (kT, dbg_kT), (vT, dbg_vT),
                                 (sacc[:, 0:c.CW], dbg_sacc), (bcslot[0][0:1, 0:c.CW], dbg_inv),
                                 (pring[:, 0:c.CW], dbg_p0)):
                    ins = sy.dma_start(out=dst[:, :], in_=ap_)
                    b.inc(ins, 'dbgw', 16)
                b.w(sy, 'dbgw', b.cnt['dbgw'])

        @block.sync
        def _(sy):
            b.w(sy, 'h1w', H1W)
            for qt in range(c.NQT):
                if qt >= 1:
                    b.w(sy, 'qzd', 16 * qt)
                ins = sy.dma_start(out=tbufA[:, :], in_=h1_s[qt * 128:(qt + 1) * 128, :])
                b.inc(ins, 'h1r', 16)

        @block.vector
        def _(v):
            b.w(v, 'epi', c.NQT)
            b.w(v, 'sadd', b.cnt.get('sadd', 0))
            b.w(v, 'exp', b.cnt['exp'])
            for qt in range(c.NQT):
                b.w(v, 'h1r', 16 * (qt + 1))
                ins = v.reduce_sum(out=stats[:, 0:1], in_=tbufA[:, :], axis=AX.X)
                b.inc(ins, 'ln2a')
                b.w(v, 'ln2b', qt + 1)
                ins = v.tensor_scalar_mul(out=stats[:, 2:3], in0=stats[:, 0:1], scalar1=float(1.0 / c.D))
                v.tensor_scalar_mul(out=stats[:, 3:4], in0=stats[:, 1:2], scalar1=float(1.0 / c.D))
                b.f(v, ins, 'vf')
                ins = v.tensor_tensor(out=stats[:, 4:5], in0=stats[:, 2:3], in1=stats[:, 2:3], op=ALU.mult)
                b.f(v, ins, 'vf')
                ins = v.tensor_tensor(out=stats[:, 4:5], in0=stats[:, 3:4], in1=stats[:, 4:5], op=ALU.subtract)
                b.inc(ins, 'ln2c')
                b.w(v, 'ln2d', qt + 1)
                ins = v.tensor_tensor(out=stats[:, 6:7], in0=stats[:, 2:3], in1=stats[:, 5:6], op=ALU.mult)
                b.f(v, ins, 'vf')
                ins = v.tensor_scalar_mul(out=stats[:, 6:7], in0=stats[:, 6:7], scalar1=-1.0)
                b.f(v, ins, 'vf')
                ins = v.tensor_copy(out=stats[:, 7:8], in_=stats[:, 5:6])
                b.inc(ins, 'ln2e')
                b.w(v, 'ln2f', qt + 1)
                ins = v.reduce_max(out=stats[:, 0:1], in_=tbufB[:, :], axis=AX.X,
                             apply_absolute_value=True)
                b.f(v, ins, 'vf')
                ins = v.reciprocal(out=stats[:, 1:2], in_=stats[:, 0:1])
                b.f(v, ins, 'vf')
                ins = v.tensor_scalar_mul(out=stats[:, 1:2], in0=stats[:, 1:2], scalar1=127.0)
                b.f(v, ins, 'vf')
                v.reciprocal(out=invs_sb[:, qt:qt + 1], in_=stats[:, 1:2])
                if qt >= 1:
                    b.w(v, 'qzd', 16 * qt)
                v.tensor_scalar(out=xorig[:, :], in0=tbufB[:, :], scalar1=stats[:, 1:2],
                                scalar2=384.0, op0=ALU.mult, op1=ALU.add)
                v.tensor_copy(out=qtempi[:, :], in_=xorig[:, :])
                v.tensor_copy(out=tbufB[:, :], in_=qtempi[:, :])
                ins = v.tensor_tensor(out=xorig[:, :], in0=xorig[:, :], in1=tbufB[:, :], op=ALU.subtract)
                b.f(v, ins, 'vf')
                v.tensor_single_scalar(out=xorig[:, :], in_=xorig[:, :], scalar=0.5, op=ALU.is_ge)
                ins = v.scalar_tensor_tensor(out=xqstage, in0=tbufB[:, :], scalar=-384.0,
                                             in1=xorig[:, :], op0=ALU.add, op1=ALU.add)
                b.inc(ins, 'qz')

        @block.scalar
        def _(s):
            for qt in range(c.NQT):
                b.w(s, 'ln2a', qt + 1)
                ins = s.activation(out=tbufB[:, :], in_=tbufA[:, :], func=AF.Square,
                                   accum_out=stats[:, 1:2])
                b.inc(ins, 'ln2b')
                b.w(s, 'ln2c', qt + 1)
                ins = s.activation(out=stats[:, 5:6], in_=stats[:, 4:5], func=AF.Ln,
                             bias=float(c.eps2), scale=1.0)
                b.f(s, ins, 'sf')
                ins = s.activation(out=stats[:, 5:6], in_=stats[:, 5:6], func=AF.Exp,
                                   scale=-0.5)
                b.inc(ins, 'ln2d')
                b.w(s, 'ln2e', qt + 1)
                ins = s.activation(out=tbufB[:, :], in_=tbufA[:, :], func=AF.Identity,
                                   bias=stats[:, 6:7], scale=stats[:, 7:8])
                b.inc(ins, 'ln2f')
                b.w(s, 'qz', qt + 1)
                ins = s.dma_start(out=xqn_s[qt * 128:(qt + 1) * 128, :], in_=xqstage)
                b.inc(ins, 'qzd', 16)
        QZD = b.cnt['qzd']

        GPP = c.D_PAIR // c.GU_GROUP

        @block.sync
        def _(sy):
            b.w(sy, 'qzd', QZD)
            dstT = bass.AP(invs_h.tensor, 0, [[1, 128], [128, c.NQT]])
            ins = sy.dma_start(out=dstT, in_=invs_sb[:, 0:c.NQT], allow_slow_non_contiguous=True)
            b.inc(ins, 'ivw', 16)
            b.w(sy, 'ivw', 16)
            for fs in range(c.NFSUB):
                if fs >= 1:
                    b.w(sy, 'pmul', fs * c.NGRP * c.GU_GROUP)
                ins = sy.dma_start(out=ibc, in_=invs_h[0:1, fs * c.FSUB:(fs + 1) * c.FSUB]
                                   .broadcast_to([128, c.FSUB]))
                b.inc(ins, 'ibl', 16)
                if fs >= 1:
                    b.w(sy, 'gmm', fs * c.NGRP * c.GU_GROUP)
                for ch in range(c.NDM):
                    ins = sy.dma_start(out=xqT[:, ch * c.FSUB:(ch + 1) * c.FSUB],
                                       in_=xqn_s[fs * c.FSUB:(fs + 1) * c.FSUB,
                                                 ch * 128:(ch + 1) * 128], transpose=True)
                    b.inc(ins, 'xqTl', 16)
                for pr_ in range(c.NPAIR):
                    P = fs * c.NPAIR + pr_
                    for gidx in range(pr_ * (c.D_PAIR // c.GU_GROUP),
                                      (pr_ + 1) * (c.D_PAIR // c.GU_GROUP)):
                        GG = fs * c.NGRP + gidx
                        b.w(sy, 'gmm', GG * c.GU_GROUP)
                        b.w(sy, 'prw', 16 * need_g[gidx])
                        dstG = bass.AP(gubufG.tensor, gubufG.offset,
                                       [list(gubufG.ap[0]), [c.GW, c.NDM], [1, c.GW]])
                        srcG = bass.AP(wgT_s.tensor, gidx * c.GW,
                                       [[c.FF, 128], [128 * c.FF, c.NDM], [1, c.GW]])
                        ins = sy.dma_start(out=dstG, in_=srcG)
                        b.inc(ins, 'wgl', 16)
                        b.w(sy, 'umm', GG * c.GU_GROUP)
                        b.w(sy, 'prw', 16 * need_u[gidx])
                        dstU = bass.AP(gubufU.tensor, gubufU.offset,
                                       [list(gubufU.ap[0]), [c.GW, c.NDM], [1, c.GW]])
                        srcU = bass.AP(wuT_s.tensor, gidx * c.GW,
                                       [[c.FF, 128], [128 * c.FF, c.NDM], [1, c.GW]])
                        ins = sy.dma_start(out=dstU, in_=srcU)
                        b.inc(ins, 'wul', 16)
                    b.w(sy, 'dmm', P * c.QT_F)
                    blk_d = ((pr_ + 1) * c.D_PAIR - 1) // WD_RT_PER_BLK
                    b.w(sy, 'prw', 16 * need_d[min(blk_d, NB - 1)])
                    for hc in range(c.D_PAIR):
                        row = (pr_ * c.D_PAIR + hc) * 128
                        ins = sy.dma_start(out=wdbuf[:, hc * c.D:(hc + 1) * c.D],
                                           in_=wdT_s[row:row + 128, :])
                        b.inc(ins, 'wdl', 16)

        @block.tensor
        def _(t):
            b.w(t, 'woc', b.cnt['woc'])
            for fs in range(c.NFSUB):
                b.w(t, 'xqTl', 16 * c.NDM * (fs + 1))
                for gidx in range(c.NGRP):
                    GG = fs * c.NGRP + gidx
                    P = GG // GPP
                    b.w(t, 'wgl', 16 * (GG + 1))
                    b.w(t, 'wul', 16 * (GG + 1))
                    if gidx % GPP == 0:
                        b.w(t, 'flv', P * c.QT_F)
                    for ht in range(c.GU_GROUP):
                        k = GG * c.GU_GROUP + ht
                        if k >= 1:
                            b.w(t, 'aev', k)
                        ins = None
                        for ch in range(c.NDM):
                            for n in range(max(1, c.FSUB // N5)):
                                wdt = min(N5, c.FSUB)
                                ins = t.matmul(pa[:, n * N5:n * N5 + wdt],
                                               gubufG[:, ch * c.GW + ht * 128: ch * c.GW + ht * 128 + 128],
                                               xqT[:, ch * c.FSUB + n * N5: ch * c.FSUB + n * N5 + wdt],
                                               start=(ch == 0), stop=(ch == c.NDM - 1))
                        b.inc(ins, 'gmm')
                        if k >= 1:
                            b.w(t, 'pmul', k)
                        ins = None
                        UOFF = 1024
                        for n in range(max(1, c.FSUB // N5)):
                            wdt = min(N5, c.FSUB)
                            ups = pa[:, UOFF + n * N5: UOFF + n * N5 + wdt]
                            for ch in range(c.NDM):
                                ins = t.matmul(ups,
                                               gubufU[:, ch * c.GW + ht * 128: ch * c.GW + ht * 128 + 128],
                                               xqT[:, ch * c.FSUB + n * N5: ch * c.FSUB + n * N5 + wdt],
                                               start=(ch == 0), stop=(ch == c.NDM - 1))
                        b.inc(ins, 'umm')
                    if (gidx + 1) % GPP == 0:
                        b.w(t, 'pmul', (GG + 1) * c.GU_GROUP)
                        b.w(t, 'wdl', 16 * c.D_PAIR * (P + 1))
                        for qti in range(c.QT_F):
                            fi = P * c.QT_F + qti
                            pp = pb if qti % 2 == 0 else pa
                            if qti >= 2:
                                b.w(t, 'flv', fi - 1)
                            elif qti == 1 and P >= 1:
                                pass
                            ins = None
                            ps_ = (P % 2) * c.D_PAIR * c.FSUB
                            for hc in range(c.D_PAIR):
                                for n in range(c.D // N5):
                                    ins = t.matmul(pp[:, n * N5:(n + 1) * N5],
                                                   pTb[:, ps_ + hc * c.FSUB + qti * 128:
                                                       ps_ + hc * c.FSUB + qti * 128 + 128],
                                                   wdbuf[:, hc * c.D + n * N5: hc * c.D + (n + 1) * N5],
                                                   start=(hc == 0), stop=(hc == c.D_PAIR - 1))
                            b.inc(ins, 'dmm')

        @block.vector
        def _(v):
            for fs in range(c.NFSUB):
                for gidx in range(c.NGRP):
                    GG = fs * c.NGRP + gidx
                    P = GG // GPP
                    ps_ = (P % 2) * c.D_PAIR * c.FSUB
                    within = (gidx % GPP) * c.GU_GROUP
                    for ht in range(c.GU_GROUP):
                        k = GG * c.GU_GROUP + ht
                        b.w(v, 'gmm', k + 1)
                        if k >= 2:
                            b.w(v, 'sil', k - 2 + 1)
                        if gidx == 0 and ht == 0:
                            b.w(v, 'ibl', 16 * (fs + 1))
                        ins = v.tensor_tensor(out=abuf[ht % 2][:, 0:c.FSUB],
                                              in0=pa[:, 0:c.FSUB], in1=ibc, op=ALU.mult)
                        b.inc(ins, 'aev')
                        b.w(v, 'umm', k + 1)
                        b.w(v, 'sil', k + 1)
                        ups = pa[:, 1024:1024 + c.FSUB]
                        ins = v.tensor_tensor(out=pTb[:, ps_ + (within + ht) * c.FSUB:
                                                     ps_ + (within + ht + 1) * c.FSUB],
                                              in0=abuf[ht % 2][:, 0:c.FSUB], in1=ups, op=ALU.mult)
                        b.inc(ins, 'pmul')
                    if (gidx + 1) % GPP == 0:
                        for qti in range(c.QT_F):
                            fi = P * c.QT_F + qti
                            pp = pb if qti % 2 == 0 else pa
                            b.w(v, 'dmm', fi + 1)
                            b.w(v, 'flw', 16 * fi)
                            ins = v.tensor_copy(out=foutr[qti % 2][:, :], in_=pp[:, 0:c.D])
                            b.inc(ins, 'flv')

        @block.scalar
        def _(s):
            drain_pre_signs(s, 3 * (c.D // 128) * 3)
            for fs in range(c.NFSUB):
                for gidx in range(c.NGRP):
                    if fs == 0:
                        drain_pre_signs(s, (c.D // 128) * 3)
                    for ht in range(c.GU_GROUP):
                        k = (fs * c.NGRP + gidx) * c.GU_GROUP + ht
                        b.w(s, 'aev', k + 1)
                        ins = s.activation(out=abuf[ht % 2][:, 0:c.FSUB],
                                           in_=abuf[ht % 2][:, 0:c.FSUB], func=AF.Silu)
                        b.inc(ins, 'sil')
            drain_pre_signs(s, NPRE)

        @block.gpsimd
        def _(g):
            drain_pre_loads(g, 3 * (c.D // 128) * 3)
            for fs in range(c.NFSUB):
                for pr_ in range(c.NPAIR):
                    P = fs * c.NPAIR + pr_
                    if fs == 0:
                        drain_pre_loads(g, 2 * (c.D // 128) * 3)
                    for qti in range(c.QT_F):
                        fi = P * c.QT_F + qti
                        b.w(g, 'flv', fi + 1)
                        qt = fs * c.QT_F + qti
                        if pr_ >= 1:
                            b.w(g, 'flw', 16 * ((P - 1) * c.QT_F + qti + 1))
                        ins = g.dma_start(out=dout_s[qt * 128:(qt + 1) * 128, :],
                                          in_=foutr[qti % 2][:, :],
                                          accum_op=(ALU.add if pr_ >= 1 else ALU.bypass))
                        b.inc(ins, 'flw', 16)
            drain_pre_loads(g, NPRE)
        FLW = b.cnt['flw']

        @block.sync
        def _(sy):
            b.w(sy, 'flw', FLW)
            b.w(sy, 'sil', b.cnt['sil'])
            for qt in range(c.NQT):
                if qt >= 1:
                    b.w(sy, 'ep3', qt)
                ins = sy.dma_start(out=foutr[0][:, :], in_=dout_s[qt * 128:(qt + 1) * 128, :])
                b.inc(ins, 'dor', 16)
                ins = sy.dma_start(out=h1t[:, :], in_=h1_s[qt * 128:(qt + 1) * 128, :])
                b.inc(ins, 'dor', 16)

        @block.vector
        def _(v):
            for qt in range(c.NQT):
                b.w(v, 'dor', 32 * (qt + 1))
                v.scalar_tensor_tensor(out=foutr[1][:, :], in0=foutr[0][:, :],
                                       scalar=invs_sb[:, qt:qt + 1], in1=h1t[:, :],
                                       op0=ALU.mult, op1=ALU.add)
                ins = v.reduce_sum(out=stats[:, 0:1], in_=foutr[1][:, :], axis=AX.X)
                b.inc(ins, 'ln3a')
                b.w(v, 'ln3b', qt + 1)
                ins = v.tensor_scalar_mul(out=stats[:, 2:3], in0=stats[:, 0:1], scalar1=float(1.0 / c.D))
                v.tensor_scalar_mul(out=stats[:, 3:4], in0=stats[:, 1:2], scalar1=float(1.0 / c.D))
                b.f(v, ins, 'vf')
                ins = v.tensor_tensor(out=stats[:, 4:5], in0=stats[:, 2:3], in1=stats[:, 2:3], op=ALU.mult)
                b.f(v, ins, 'vf')
                ins = v.tensor_tensor(out=stats[:, 4:5], in0=stats[:, 3:4], in1=stats[:, 4:5], op=ALU.subtract)
                b.inc(ins, 'ln3c')
                b.w(v, 'ln3d', qt + 1)
                ins = v.tensor_tensor(out=stats[:, 6:7], in0=stats[:, 2:3], in1=stats[:, 5:6], op=ALU.mult)
                b.f(v, ins, 'vf')
                ins = v.tensor_scalar_mul(out=stats[:, 6:7], in0=stats[:, 6:7], scalar1=-1.0)
                b.f(v, ins, 'vf')
                ins = v.tensor_copy(out=stats[:, 7:8], in_=stats[:, 5:6])
                b.inc(ins, 'ln3e')

        @block.scalar
        def _(s):
            for qt in range(c.NQT):
                b.w(s, 'ln3a', qt + 1)
                ins = s.activation(out=foutr[0][:, :], in_=foutr[1][:, :], func=AF.Square,
                                   accum_out=stats[:, 1:2])
                b.inc(ins, 'ln3b')
                b.w(s, 'ln3c', qt + 1)
                ins = s.activation(out=stats[:, 5:6], in_=stats[:, 4:5], func=AF.Ln,
                             bias=float(c.eps1), scale=1.0)
                b.f(s, ins, 'sf')
                ins = s.activation(out=stats[:, 5:6], in_=stats[:, 5:6], func=AF.Exp, scale=-0.5)
                b.inc(ins, 'ln3d')
                b.w(s, 'ln3e', qt + 1)
                ins = s.activation(out=foutr[1][:, :], in_=foutr[1][:, :], func=AF.Identity,
                                   bias=stats[:, 6:7], scale=stats[:, 7:8])
                b.inc(ins, 'ln3f')

        @block.gpsimd
        def _(g):
            for qt in range(c.NQT):
                b.w(g, 'ln3f', qt + 1)
                ins = g.dma_start(out=out[qt * 128:(qt + 1) * 128, :], in_=foutr[1][:, :])
                b.inc(ins, 'outw', 16)
                g.wait_ge(b.s('outw'), 16 * (qt + 1))
                ins = g.memset(stats[0:1, 1:2], 0.0)
                b.inc(ins, 'ep3')

    es.close()
    return nc


def kernel(L0_attn_hidden_states, L0_attn_Wq, L0_attn_Wk, L0_attn_Wv,
           L0_attn_Wo, L0_ffn_W_g, L0_ffn_W_u, L0_ffn_W_d):
    cfg = Cfg()
    x = np.ascontiguousarray(np.asarray(L0_attn_hidden_states, dtype=np.float32))
    B_, S_, D_ = x.shape
    nc = build_nc(cfg)
    ws = dict(wq=np.ascontiguousarray(L0_attn_Wq, dtype=np.float32),
              wk=np.ascontiguousarray(L0_attn_Wk, dtype=np.float32),
              wv=np.ascontiguousarray(L0_attn_Wv, dtype=np.float32),
              wo=np.ascontiguousarray(L0_attn_Wo, dtype=np.float32),
              wgT=np.ascontiguousarray(np.asarray(L0_ffn_W_g, dtype=np.float32).T),
              wuT=np.ascontiguousarray(np.asarray(L0_ffn_W_u, dtype=np.float32).T),
              wdT=np.ascontiguousarray(np.asarray(L0_ffn_W_d, dtype=np.float32).T))
    in_maps = []
    for core in range(8):
        bidx, r = core // 2, core % 2
        im = dict(ws)
        im["xbT"] = np.ascontiguousarray(x[bidx].T)
        im["xq"] = np.ascontiguousarray(x[bidx, r * cfg.QROWS:(r + 1) * cfg.QROWS])
        im["xqT_i"] = np.ascontiguousarray(x[bidx, r * cfg.QROWS:(r + 1) * cfg.QROWS].T)
        im["pos"] = np.array([[float(r * cfg.QROWS)]], dtype=np.float32)
        in_maps.append(im)
    res = run_bass_kernel_spmd(nc, in_maps, core_ids=list(range(8)))
    outp = np.empty((B_, S_, D_), dtype=np.float32)
    for core in range(8):
        bidx, r = core // 2, core % 2
        outp[bidx, r * cfg.QROWS:(r + 1) * cfg.QROWS] = res.results[core]["out"]
    return outp
